# version 23
# baseline (speedup 1.0000x reference)
import contextlib
import numpy as np
import concourse.bass as bass
import concourse.mybir as mybir
from concourse.bass_utils import run_bass_kernel_spmd

F32 = mybir.dt.float32
BF16 = mybir.dt.bfloat16
FP8 = mybir.dt.float8e4
ALU = mybir.AluOpType
ACT = mybir.ActivationFunctionType
AX = mybir.AxisListType

NT, D, HC, NH, DH, TW = 1024, 2048, 1024, 16, 64, 512
NW = NT // TW
II = 5632
NIC, NFC, NKC, NTC = II // 128, D // 128, HC // 128, NT // 128
EPS = 1e-5

PIECES = [(k, m, 0) for m in range(NIC) for k in ("g", "u")]
NP = len(PIECES)
NSQ = 11


class Sem:
    def __init__(self, ctx, nc, name):
        self.h = ctx.enter_context(nc.semaphore(name))
        self.n = 0
    def inc(self, ins, k=1):
        ins.then_inc(self.h, k)
        self.n += k
        return self.n
    def wait(self, eng, val=None):
        eng.wait_ge(self.h, self.n if val is None else val)


def build_nc(dbg=False):
    nc = bass.Bass("TRN2", target_bir_lowering=False)
    io = {}
    def di(name, shape, dt=F32, kind="ExternalInput"):
        io[name] = nc.dram_tensor(name, shape, dt, kind=kind).ap()
    di("hs", [NT, D]); di("wc", [D, HC]); di("wq", [HC, HC]); di("wk", [HC, HC])
    di("wv", [HC, HC]); di("wd", [HC, D]); di("bc", [HC]); di("bq", [HC])
    di("bk", [HC]); di("bv", [HC]); di("bd", [D]); di("bmat", [TW, TW])
    di("idn", [128, 128], BF16); di("idn8", [128, 128], FP8)
    di("wg", [II, D]); di("wu", [II, D]); di("wdn", [D, II])
    di("out", [NT, D], F32, "ExternalOutput")
    if dbg:
        for nm, shp, dt in [("xc", [HC, NT], BF16), ("qf", [HC, NT], BF16),
                            ("kf", [HC, NT], BF16), ("vtm", [NT, HC], BF16),
                            ("ctx", [HC, NT], BF16), ("res1", [NT, D], F32),
                            ("x1", [NT, D], BF16), ("qxf", [D, NT], FP8),
                            ("tf", [II, NT], BF16), ("qtf", [II, NT], FP8),
                            ("qwd", [NIC * 128, D], FP8)]:
            di("dbg_" + nm, shp, dt, "ExternalOutput")
    with contextlib.ExitStack() as ctx:
        _build(nc, ctx, io, dbg)
    return nc


def _build(nc, ctx, io, dbg):
    E = ctx.enter_context
    hs, wc, wqd, wkd, wvd, wdd = io["hs"], io["wc"], io["wq"], io["wk"], io["wv"], io["wd"]
    bcd, bqd, bkd, bvd, bdd = io["bc"], io["bq"], io["bk"], io["bv"], io["bd"]
    bmat, idn, wg, wu, wdn, out = io["bmat"], io["idn"], io["wg"], io["wu"], io["wdn"], io["out"]
    idn8 = io["idn8"]

    ident = E(nc.sbuf_tensor("ident", [128, 128], BF16))
    ident8 = E(nc.sbuf_tensor("ident8", [128, 128], FP8))
    ones16 = E(nc.sbuf_tensor("ones16", [16, 128], F32))
    bc_sb = E(nc.sbuf_tensor("bc_sb", [128, NKC], F32))
    bq8_sb = E(nc.sbuf_tensor("bq8_sb", [128, NKC], F32))
    bk_sb = E(nc.sbuf_tensor("bk_sb", [128, NKC], F32))
    bv_sb = E(nc.sbuf_tensor("bv_sb", [128, NKC], F32))
    stats = E(nc.sbuf_tensor("stats", [128, 24], F32))
    aggr = E(nc.sbuf_tensor("aggr", [128, 2], F32))
    rstd = E(nc.sbuf_tensor("rstd", [128, 1], F32))
    nmr = E(nc.sbuf_tensor("nmr", [128, 1], F32))
    rmx = E(nc.sbuf_tensor("rmx", [128, NTC], F32))
    rmx224 = E(nc.sbuf_tensor("rmx224", [128, NTC], F32))
    rmt = E(nc.sbuf_tensor("rmt", [128, NTC], F32))
    rmt4 = E(nc.sbuf_tensor("rmt4", [128, NTC], F32))
    rwd_keep = E(nc.sbuf_tensor("rwd_keep", [128, NFC], F32))
    rwdq = E(nc.sbuf_tensor("rwdq", [128, NFC], F32))
    rwdrec = E(nc.sbuf_tensor("rwdrec", [128, NFC], F32))

    s_ld = Sem(ctx, nc, "s_ld")
    s_aw = {k: Sem(ctx, nc, f"s_aw_{k}") for k in ("c", "q", "k", "v", "d", "B")}
    s_tr = [Sem(ctx, nc, "s_tr0"), Sem(ctx, nc, "s_tr1")]
    s_pe = Sem(ctx, nc, "s_pe")
    s_dv = Sem(ctx, nc, "s_dv")
    s_ac = Sem(ctx, nc, "s_ac")
    s_ev = Sem(ctx, nc, "s_ev")
    s_ev2 = Sem(ctx, nc, "s_ev2")
    s_out = Sem(ctx, nc, "s_out")
    NR = 2
    NQ = 3
    s_wr = [Sem(ctx, nc, f"swr{j}") for j in range(NR)]
    s_wru = [Sem(ctx, nc, f"swru{j}") for j in range(NR)]
    s_ws = [Sem(ctx, nc, f"sws{j}") for j in range(NR)]
    s_wt = [Sem(ctx, nc, f"swt{j}") for j in range(3)]
    s_w8 = [Sem(ctx, nc, f"sw8{j}") for j in range(NQ)]
    s_cons = [Sem(ctx, nc, f"scons{j}") for j in range(NQ)]
    s_qd = [Sem(ctx, nc, f"sqd{j}") for j in range(2)]
    s_qw = [Sem(ctx, nc, "s_qwA"), Sem(ctx, nc, "s_qwB")]
    s_gp = Sem(ctx, nc, "s_gp")
    s_ost = [Sem(ctx, nc, "s_ostA"), Sem(ctx, nc, "s_ostB")]
    s_hs = [Sem(ctx, nc, "s_hsA"), Sem(ctx, nc, "s_hsB")]
    s_dnl = Sem(ctx, nc, "s_dnl")
    s_dnv = Sem(ctx, nc, "s_dnv")
    s_dnsc = Sem(ctx, nc, "s_dnsc")
    s_dntr = [Sem(ctx, nc, "s_dntrA"), Sem(ctx, nc, "s_dntrB")]
    s_dnq = Sem(ctx, nc, "s_dnq")
    s_dnimg = Sem(ctx, nc, "s_dnimg")
    s_wdn2 = [Sem(ctx, nc, "s_wdnA"), Sem(ctx, nc, "s_wdnB")]
    s_ln = [Sem(ctx, nc, "s_lnA"), Sem(ctx, nc, "s_lnB")]
    s_lnr = [Sem(ctx, nc, "s_lnrA"), Sem(ctx, nc, "s_lnrB")]
    s_rb5 = Sem(ctx, nc, "s_rb5")
    s_rb7 = Sem(ctx, nc, "s_rb7")

    ln1_d = nc.dram_tensor("ln1_d", [NT, D], BF16).ap()
    rowd5 = nc.dram_tensor("rowd5", [NT], F32).ap()
    rowd7 = nc.dram_tensor("rowd7", [NT + D], F32).ap()
    qwd_img = nc.dram_tensor("qwd_img", [II, D], FP8).ap()

    blk = ctx.enter_context(nc.Block())

    sc_hs = contextlib.ExitStack()
    hs_tm = sc_hs.enter_context(nc.sbuf_tensor("hs_tm", [128, NTC, D], BF16, side="left"))
    sc_dn = contextlib.ExitStack()
    dn_raw = sc_dn.enter_context(nc.sbuf_tensor("dn_raw", [128, II], F32, side="left"))
    dn_sc = sc_dn.enter_context(nc.sbuf_tensor("dn_sc", [128, II // 2], BF16, side="left"))
    dn_tr = sc_dn.enter_context(nc.sbuf_tensor("dn_tr", [128, 2, II // 2], BF16, side="left",
                                               is_transpose_dest=True))
    dn_st = sc_dn.enter_context(nc.sbuf_tensor("dn_st", [128, II], FP8, side="left"))
    sc_xc = contextlib.ExitStack()
    xc_fm = sc_xc.enter_context(nc.sbuf_tensor("xc_fm", [128, NKC, NT], BF16, side="right"))
    sc01 = contextlib.ExitStack()
    hs_fm = sc01.enter_context(nc.sbuf_tensor("hs_fm", [128, NFC, NT], BF16, side="right"))
    wc_sb = sc01.enter_context(nc.sbuf_tensor("wc_sb", [128, NFC, HC], BF16, side="right"))
    sc00 = contextlib.ExitStack()
    hs_st = sc00.enter_context(nc.sbuf_tensor("hs_st", [128, 2, D], F32, side="right"))
    hs_bf = sc00.enter_context(nc.sbuf_tensor("hs_bf", [128, 2, D], BF16, side="right"))
    bd_bc = sc00.enter_context(nc.sbuf_tensor("bd_bc", [128, D], F32, side="right"))
    scps0 = contextlib.ExitStack()
    psA = scps0.enter_context(nc.psum_tensor("psA", [128, 512], F32))
    psB = scps0.enter_context(nc.psum_tensor("psB", [128, 512], F32))
    ps = [psA, psB]

    HF = II // 2
    dncur = {"load": 0, "vec": 0, "sc": 0, "q8": 0, "img": 0}
    dnrec = {"tr": {}}

    def dn_load(g, upto):
        while dncur["load"] < min(upto, NFC):
            c = dncur["load"]
            if c >= 1 and dncur["sc"] < 2 * c:
                return
            dncur["load"] += 1
            if c >= 1:
                g.wait_ge(s_dnsc.h, 2 * c)
            s_dnl.inc(g.dma_start(out=dn_raw[:, :], in_=wdn[c * 128:(c + 1) * 128, :]), 16)

    def dn_vec(v, upto):
        while dncur["vec"] < min(upto, NFC, dncur["load"]):
            c = dncur["vec"]; dncur["vec"] += 1
            v.wait_ge(s_dnl.h, 16 * (c + 1))
            v.reduce_max(rwd_keep[:, c:c + 1], dn_raw[:, :], axis=AX.X,
                         apply_absolute_value=True)
            v.drain()
            v.reciprocal(rwdrec[:, c:c + 1], rwd_keep[:, c:c + 1])
            s_dnv.inc(v.drain())

    def dn_sct(s, upto):
        while dncur["sc"] < min(upto, 2 * NFC, 2 * dncur["vec"]):
            ih = dncur["sc"]
            c, h = ih // 2, ih % 2
            if ih >= 2 and dncur["q8"] < ih // 2:
                return
            dncur["sc"] += 1
            s.wait_ge(s_dnv.h, c + 1)
            if ih >= 1:
                s.wait_ge(s_dntr[(ih - 1) % 2].h, dnrec["tr"][ih - 1])
            if ih >= 2:
                s.wait_ge(s_dnq.h, ih // 2)
            s.activation(dn_sc[:, :], dn_raw[:, h * HF:(h + 1) * HF], ACT.Copy,
                         scale=rwdrec[:, c:c + 1])
            s_dnsc.inc(s.drain())
            dnrec["tr"][ih] = s_dntr[ih % 2].inc(s.dma_start_transpose(
                dn_tr[:, ih % 2, :].rearrange("p (b j) -> p b j", j=128),
                dn_sc[:, :].rearrange("p (b j) -> p b j", j=128)), 16)

    def dn_q8(v, upto):
        while dncur["q8"] < min(upto, NFC) and dncur["sc"] >= 2 * (dncur["q8"] + 1):
            c = dncur["q8"]; dncur["q8"] += 1
            v.wait_ge(s_dntr[0].h, dnrec["tr"][2 * c])
            v.wait_ge(s_dntr[1].h, dnrec["tr"][2 * c + 1])
            if c >= 1:
                v.wait_ge(s_dnimg.h, 16 * c)
            s_dnq.inc(v.tensor_scalar_mul(dn_st[:, :], dn_tr[:, :, :], 224.0))

    def dn_img(g, upto):
        while dncur["img"] < min(upto, NFC, dncur["q8"]):
            c = dncur["img"]; dncur["img"] += 1
            g.wait_ge(s_dnq.h, c + 1)
            s_dnimg.inc(g.dma_start(
                out=bass.AP(qwd_img.tensor, c * 128, [[D, 128], [128 * D, NIC], [1, 128]]),
                in_=dn_st[:, :].rearrange("p (b j) -> p b j", j=128)), 16)

    def dn_round(b, n):
        for _ in range(n):
            @b.gpsimd
            def _(g):
                dn_load(g, NFC)
                dn_img(g, NFC)
            @b.vector
            def _(v):
                dn_vec(v, NFC)
                dn_q8(v, NFC)
            @b.scalar
            def _(s):
                dn_sct(s, 2 * NFC)

    if True:
        b = blk
        ac0 = s_ac.n
        dv0 = s_dv.n
        @b.sync
        def _(sync):
            s_ld.inc(sync.dma_start(out=ident[:, :], in_=idn[:, :]), 16)
            s_ld.inc(sync.dma_start(out=ident8[:, :], in_=idn8[:, :]), 16)
            for dst, src in ((bc_sb, bcd), (bq8_sb, bqd), (bk_sb, bkd), (bv_sb, bvd)):
                s_ld.inc(sync.dma_start(out=dst[:, :], in_=src.rearrange("(c p) -> p c", p=128),
                                        allow_slow_non_contiguous=True), 16)
            s_ld.inc(sync.dma_start(out=bd_bc[:, :],
                                    in_=bass.AP(bdd.tensor, 0, [[0, 128], [1, D]])), 16)
            for tc in range(NTC):
                if tc >= 2:
                    sync.wait_ge(s_ac.h, ac0 + 2 * (tc - 2) + 2)
                    sync.wait_ge(s_dv.h, dv0 + 1 + (tc - 2) + 1)
                s_hs[tc % 2].inc(sync.dma_start(out=hs_st[:, tc % 2, :],
                                                in_=hs[tc * 128:(tc + 1) * 128, :]), 16)

        @b.gpsimd
        def _(g):
            s_aw["c"].inc(g.dma_start(out=wc_sb[:, :, :],
                          in_=wc.rearrange("(c p) j -> p c j", p=128)), 16)
            s_gp.inc(g.memset(ones16[:, :], 1.0))

        @b.scalar
        def _(s):
            for tc in range(NTC):
                s.wait_ge(s_hs[tc % 2].h, 16 * (tc // 2 + 1))
                if tc >= 2:
                    s.wait_ge(s_tr[tc % 2].h, 16 * ((tc - 2) // 2 + 1))
                s_ac.inc(s.activation(hs_bf[:, tc % 2, :], hs_st[:, tc % 2, :], ACT.Copy))
                s_ac.inc(s.drain())
                s_tr[tc % 2].inc(s.dma_start_transpose(
                    hs_fm[:, :, tc * 128:(tc + 1) * 128],
                    hs_bf[:, tc % 2, :].rearrange("p (c j) -> p c j", j=128)), 16)

        @b.vector
        def _(v):
            s_ld.wait(v)
            v.tensor_scalar_mul(bq8_sb[:, :], bq8_sb[:, :], 0.125)
            s_dv.inc(v.drain())
            for tc in range(NTC):
                v.wait_ge(s_hs[tc % 2].h, 16 * (tc // 2 + 1))
                s_dv.inc(v.scalar_tensor_tensor(hs_tm[:, tc, :], hs_st[:, tc % 2, :], 1.0,
                                                bd_bc[:, :], ALU.mult, ALU.add))

    dn_round(blk, 4)
    sc00.close()

    if True:
        b = blk
        pe1 = s_pe.n
        e21 = s_ev2.n
        @b.tensor
        def _(t):
            s_aw["c"].wait(t)
            s_tr[0].wait(t)
            s_tr[1].wait(t)
            t.wait_ge(s_ev.h, s_ev.n)
            i = 0
            for m in range(NKC):
                for n in range(2):
                    if i >= 2:
                        t.wait_ge(s_ev2.h, e21 + i - 1)
                    ins = None
                    for k in range(NFC):
                        ins = t.matmul(ps[i % 2][:, :], wc_sb[:, k, m * 128:(m + 1) * 128],
                                       hs_fm[:, k, n * 512:(n + 1) * 512],
                                       start=(k == 0), stop=(k == NFC - 1))
                    s_pe.inc(ins)
                    i += 1

        @b.scalar
        def _(s):
            i = 0
            for m in range(NKC):
                for n in range(2):
                    s_pe.wait(s, pe1 + i + 1)
                    s_ev2.inc(s.activation(xc_fm[:, m, n * 512:(n + 1) * 512], ps[i % 2][:, :],
                                           ACT.Identity, bias=bc_sb[:, m:m + 1]))
                    i += 1
        if dbg:
            @b.sync
            def _(sync):
                sync.wait_ge(s_ev2.h, s_ev2.n)
                for m in range(NKC):
                    s_out.inc(sync.dma_start(out=io["dbg_xc"][m * 128:(m + 1) * 128, :],
                                             in_=xc_fm[:, m, :]), 16)
                sync.wait_ge(s_out.h, s_out.n)

    dn_round(blk, 4)
    sc01.close()

    sc_ctx = contextlib.ExitStack()
    ctx_fm = sc_ctx.enter_context(nc.sbuf_tensor("ctx_fm", [128, NKC, NT], BF16, side="left"))
    sc2 = contextlib.ExitStack()
    q_fm = sc2.enter_context(nc.sbuf_tensor("q_fm", [128, NKC, NT], BF16, side="right"))
    k_fm = sc2.enter_context(nc.sbuf_tensor("k_fm", [128, NKC, NT], BF16, side="right"))
    v_tm = sc2.enter_context(nc.sbuf_tensor("v_tm", [128, NTC, HC], BF16, side="right"))
    sc2b = contextlib.ExitStack()
    v_fm = sc2b.enter_context(nc.sbuf_tensor("v_fm", [128, NKC, NT], BF16, side="right"))
    wx_sb = sc2b.enter_context(nc.sbuf_tensor("wx_sb", [128, 2, NKC, HC], BF16, side="right"))
    psT16 = sc2b.enter_context(nc.psum_tensor("psT16", [128, 1024], BF16))

    if True:
        b = blk
        pe2 = s_pe.n
        e22 = s_ev2.n
        @b.gpsimd
        def _(g):
            g.wait_ge(s_tr[0].h, s_tr[0].n)
            g.wait_ge(s_tr[1].h, s_tr[1].n)
            g.wait_ge(s_dv.h, s_dv.n)
            g.wait_ge(s_ac.h, s_ac.n)
            s_aw["q"].inc(g.dma_start(out=wx_sb[:, 0, :, :],
                          in_=wqd.rearrange("(c p) j -> p c j", p=128)), 16)
            s_aw["k"].inc(g.dma_start(out=wx_sb[:, 1, :, :],
                          in_=wkd.rearrange("(c p) j -> p c j", p=128)), 16)
            g.wait_ge(s_pe.h, pe2 + 16)
            s_aw["v"].inc(g.dma_start(out=wx_sb[:, 0, :, :],
                          in_=wvd.rearrange("(c p) j -> p c j", p=128)), 16)

        dsts = None
        @b.tensor
        def _(t):
            t.wait_ge(s_ev2.h, e22)
            projs = [(0, "q"), (1, "k"), (0, "v")]
            i = 0
            for pi, (wslot, awk) in enumerate(projs):
                t.wait_ge(s_aw[awk].h, 16)
                for m in range(NKC):
                    for n in range(2):
                        if i >= 2:
                            t.wait_ge(s_ev2.h, e22 + i - 1)
                        ins = None
                        for k in range(NKC):
                            ins = t.matmul(ps[i % 2][:, :],
                                           wx_sb[:, wslot, k, m * 128:(m + 1) * 128],
                                           xc_fm[:, k, n * 512:(n + 1) * 512],
                                           start=(k == 0), stop=(k == NKC - 1))
                        s_pe.inc(ins)
                        i += 1

        @b.scalar
        def _(s):
            projs2 = [(q_fm, bq8_sb, 0.125), (k_fm, bk_sb, 1.0), (v_fm, bv_sb, 1.0)]
            i = 0
            for dst, bias, scale in projs2:
                for m in range(NKC):
                    for n in range(2):
                        s_pe.wait(s, pe2 + i + 1)
                        s_ev2.inc(s.activation(dst[:, m, n * 512:(n + 1) * 512],
                                               ps[i % 2][:, :], ACT.Identity,
                                               bias=bias[:, m:m + 1], scale=scale))
                        i += 1

        pe_vt = s_pe.n
        ev_vt = s_ev.n
        @b.tensor
        def _(t):
            t.wait_ge(s_ev2.h, e22 + 48)
            for tc in range(NTC):
                if tc >= 1:
                    t.wait_ge(s_ev.h, ev_vt + tc)
                ins = None
                for m in range(NKC):
                    ins = t.transpose(psT16[:, m * 128:(m + 1) * 128],
                                      v_fm[:, m, tc * 128:(tc + 1) * 128], ident[:, :])
                s_pe.inc(ins)

        @b.vector
        def _(v):
            for tc in range(NTC):
                s_pe.wait(v, pe_vt + tc + 1)
                s_ev.inc(v.tensor_copy(v_tm[:, tc, :], psT16[:, :]))

        if dbg:
            @b.sync
            def _(sync):
                sync.wait_ge(s_ev2.h, s_ev2.n)
                sync.wait_ge(s_ev.h, s_ev.n)
                for m in range(NKC):
                    s_out.inc(sync.dma_start(out=io["dbg_qf"][m * 128:(m + 1) * 128, :],
                                             in_=q_fm[:, m, :]), 16)
                    s_out.inc(sync.dma_start(out=io["dbg_kf"][m * 128:(m + 1) * 128, :],
                                             in_=k_fm[:, m, :]), 16)
                for tc in range(NTC):
                    s_out.inc(sync.dma_start(out=io["dbg_vtm"][tc * 128:(tc + 1) * 128, :],
                                             in_=v_tm[:, tc, :]), 16)
                sync.wait_ge(s_out.h, s_out.n)

    dn_round(blk, 5)
    sc2b.close()
    scps0.close()

    scps34 = contextlib.ExitStack()
    psC = scps34.enter_context(nc.psum_tensor("psC", [128, 512], F32))
    psD = scps34.enter_context(nc.psum_tensor("psD", [128, 512], F32))
    sc3 = contextlib.ExitStack()
    Bsb = sc3.enter_context(nc.sbuf_tensor("Bsb", [128, 4, TW], BF16, side="right"))
    sneg = sc3.enter_context(nc.sbuf_tensor("sneg", [128, 4, TW], F32, side="right"))
    pexp = sc3.enter_context(nc.sbuf_tensor("pexp", [128, 4, TW], BF16, side="right"))
    probsT = sc3.enter_context(nc.sbuf_tensor("probsT", [128, 4, TW], BF16, side="right"))
    m4 = sc3.enter_context(nc.sbuf_tensor("m4", [128, 4], F32, side="right"))
    den4 = sc3.enter_context(nc.sbuf_tensor("den4", [128, 4], F32, side="right"))
    rc4 = sc3.enter_context(nc.sbuf_tensor("rc4", [128, 4], F32, side="right"))
    psC2 = sc3.enter_context(nc.psum_tensor("psC2", [128, 512], F32))
    psD2 = sc3.enter_context(nc.psum_tensor("psD2", [128, 512], F32))
    psPT = sc3.enter_context(nc.psum_tensor("psPT", [128, 1024], BF16))
    psPT2 = sc3.enter_context(nc.psum_tensor("psPT2", [128, 1024], BF16))
    psX = sc3.enter_context(nc.psum_tensor("psX", [128, 512], F32))
    psc = [psC, psD, psC2, psD2]
    pn = pexp

    WH = [(w, h) for w in range(NW) for h in range(NH)]

    def q_ap(w, h, qc):
        base = (h % 2) * 64
        return q_fm[base:base + 64, h // 2, w * TW + qc * 128: w * TW + (qc + 1) * 128]

    def k_ap(w, h):
        base = (h % 2) * 64
        return k_fm[base:base + 64, h // 2, w * TW:(w + 1) * TW]

    def ptbank(kc):
        return (psPT if kc < 2 else psPT2), (kc % 2) * 512

    NWH = len(WH)

    if True:
        b = blk
        pe_s = s_pe.n
        dv_s = s_dv.n
        ac_s = s_ac.n

        def pe_sc_val(i, qc):
            return pe_s + (qc + 1 if i == 0 else 4 + 6 * (i - 1) + qc + 1)

        def pe_tr_val(i):
            return pe_s + 4 + 6 * i + (4 if i + 1 < NWH else 0) + 1

        def pe_cx_val(i):
            return pe_tr_val(i) + 1

        @b.gpsimd
        def _(g):
            g.wait_ge(s_pe.h, pe_s)
            s_aw["B"].inc(g.dma_start(out=Bsb[:, :, :],
                          in_=bmat.rearrange("(c p) k -> p c k", p=128)), 16)

        @b.tensor
        def _(t):
            s_aw["B"].wait(t)
            t.wait_ge(s_ev.h, ev_vt + 8)
            w0, h0 = WH[0]
            for qc in range(4):
                s_pe.inc(t.matmul(psc[qc][:, :], q_ap(w0, h0, qc), k_ap(w0, h0),
                                  start=True, stop=True))
            for i in range(len(WH)):
                if i + 1 < len(WH):
                    w2, h2 = WH[i + 1]
                    t.wait_ge(s_dv.h, dv_s + 2 * i + 1)
                    for qc in range(4):
                        s_pe.inc(t.matmul(psc[qc][:, :], q_ap(w2, h2, qc), k_ap(w2, h2),
                                          start=True, stop=True))
                w, h = WH[i]
                t.wait_ge(s_dv.h, dv_s + 2 * i + 2)
                if i >= 1:
                    t.wait_ge(s_ac.h, ac_s + 3 * i - 1)
                ins = None
                for kc in range(4):
                    bank, off = ptbank(kc)
                    for qc in range(4):
                        ins = t.transpose(bank[:, off + qc * 128:off + (qc + 1) * 128],
                                          pn[:, qc, kc * 128:(kc + 1) * 128], ident[:, :])
                s_pe.inc(ins)
                t.wait_ge(s_ac.h, ac_s + 3 * i + 2)
                for kc in range(4):
                    ins = t.matmul(psX[0:64, :], v_tm[:, w * 4 + kc, h * 64:(h + 1) * 64],
                                   probsT[:, kc, :], start=(kc == 0), stop=(kc == 3))
                s_pe.inc(ins)

        @b.vector
        def _(v):
            for i in range(len(WH)):
                for qc in range(4):
                    v.wait_ge(s_pe.h, pe_sc_val(i, qc))
                    v.tensor_add(sneg[:, qc, :], psc[qc][:, :], Bsb[:, qc, :])
                v.drain()
                for qc in range(4):
                    v.reduce_max(m4[:, qc:qc + 1], sneg[:, qc, :], axis=AX.X, negate=True)
                s_dv.inc(v.drain())
                v.wait_ge(s_ac.h, ac_s + 3 * i + 1)
                if i >= 1:
                    v.wait_ge(s_pe.h, pe_tr_val(i - 1))
                v.reciprocal(rc4[:, :], den4[:, :])
                v.drain()
                for qc in range(4):
                    v.tensor_scalar_mul(pn[:, qc, :], pexp[:, qc, :], rc4[:, qc:qc + 1])
                s_dv.inc(v.drain())

        @b.scalar
        def _(s):
            for i in range(len(WH)):
                s.wait_ge(s_dv.h, dv_s + 2 * i + 1)
                if i >= 1:
                    s.wait_ge(s_pe.h, pe_tr_val(i - 1))
                for qc in range(4):
                    s.activation(pexp[:, qc, :], sneg[:, qc, :], ACT.Exp,
                                 bias=m4[:, qc:qc + 1], scale=1.0,
                                 accum_out=den4[:, qc:qc + 1])
                s_ac.inc(s.drain())
                s.wait_ge(s_pe.h, pe_tr_val(i))
                for kc in range(4):
                    bank, off = ptbank(kc)
                    s.activation(probsT[:, kc, :], bank[:, off:off + 512], ACT.Copy)
                s_ac.inc(s.drain())
                s.wait_ge(s_pe.h, pe_cx_val(i))
                w, h = WH[i]
                base = (h % 2) * 64
                s_ac.inc(s.activation(ctx_fm[base:base + 64, h // 2, w * TW:(w + 1) * TW],
                                      psX[0:64, :], ACT.Copy))

        if dbg:
            @b.sync
            def _(sync):
                sync.wait_ge(s_ac.h, s_ac.n)
                for m in range(NKC):
                    s_out.inc(sync.dma_start(out=io["dbg_ctx"][m * 128:(m + 1) * 128, :],
                                             in_=ctx_fm[:, m, :]), 16)
                sync.wait_ge(s_out.h, s_out.n)

    dn_round(blk, 3)
    sc3.close()
    sc2.close()
    sc_xc.close()

    sc4 = contextlib.ExitStack()
    res1 = sc4.enter_context(nc.sbuf_tensor("res1_tm", [128, NTC, D], F32, side="right"))
    sc4w = contextlib.ExitStack()
    wd_n = sc4w.enter_context(nc.sbuf_tensor("wd_n", [128, 2, NKC, 512], BF16, side="right"))
    ps2 = [psC, psD]

    if True:
        b = blk
        pe_w = s_pe.n
        ev_w = s_ev.n
        @b.gpsimd
        def _(g):
            for n in range(4):
                if n == 0:
                    g.wait_ge(s_dv.h, s_dv.n)
                    g.wait_ge(s_ac.h, s_ac.n)
                    g.wait_ge(s_pe.h, pe_w)
                if n >= 2:
                    g.wait_ge(s_pe.h, pe_w + 8 * (n - 1))
                s_wdn2[n % 2].inc(g.dma_start(
                    out=wd_n[:, n % 2, :, :],
                    in_=wdd.rearrange("(c p) j -> p c j", p=128)[:, :, n * 512:(n + 1) * 512]), 16)

        @b.tensor
        def _(t):
            t.wait_ge(s_ac.h, s_ac.n)
            i = 0
            for n in range(4):
                t.wait_ge(s_wdn2[n % 2].h, 16 * (n // 2 + 1))
                for tc in range(NTC):
                    if i >= 2:
                        t.wait_ge(s_ev.h, ev_w + i - 1)
                    ins = None
                    for k in range(NKC):
                        ins = t.matmul(ps2[i % 2][:, :], ctx_fm[:, k, tc * 128:(tc + 1) * 128],
                                       wd_n[:, n % 2, k, :],
                                       start=(k == 0), stop=(k == NKC - 1))
                    s_pe.inc(ins)
                    i += 1

        @b.vector
        def _(v):
            i = 0
            for n in range(4):
                for tc in range(NTC):
                    v.wait_ge(s_pe.h, pe_w + i + 1)
                    s_ev.inc(v.scalar_tensor_tensor(
                        res1[:, tc, n * 512:(n + 1) * 512], ps2[i % 2][:, :], 1.0,
                        hs_tm[:, tc, n * 512:(n + 1) * 512], ALU.mult, ALU.add))
                    i += 1
        if dbg:
            @b.sync
            def _(sync):
                sync.wait_ge(s_ev.h, s_ev.n)
                for tc in range(NTC):
                    s_out.inc(sync.dma_start(out=io["dbg_res1"][tc * 128:(tc + 1) * 128, :],
                                             in_=res1[:, tc, :]), 16)
                sync.wait_ge(s_out.h, s_out.n)

    dn_round(blk, 6)
    assert dncur["img"] == NFC, f"dn prepass incomplete: {dncur}"
    sc4w.close()
    sc_ctx.close()
    sc_dn.close()
    sc_hs.close()
    scps34.close()

    sc5q = contextlib.ExitStack()
    qx_fm = sc5q.enter_context(nc.sbuf_tensor("qx_fm", [128, NFC, NT], FP8, side="left"))
    bc_rm = sc5q.enter_context(nc.sbuf_tensor("bc_rm", [128, NT], F32, side="left"))
    NRT = 3
    wraw = [sc5q.enter_context(nc.sbuf_tensor(f"wp_raw{j}", [128, 2048], F32, side="left"))
            for j in range(NR)]
    wsc = [sc5q.enter_context(nc.sbuf_tensor(f"wp_sc{j}", [128, 2048], BF16, side="left"))
           for j in range(NR)]
    wtrn = [sc5q.enter_context(nc.sbuf_tensor(f"wp_tr{j}", [128, 2048], BF16, side="left",
                                              is_transpose_dest=True)) for j in range(NRT)]
    wq8 = [sc5q.enter_context(nc.sbuf_tensor(f"wp_q8{j}", [128, 2048], FP8, side="left"))
           for j in range(NQ)]
    wp_rw = sc5q.enter_context(nc.sbuf_tensor("wp_rw", [128, NR], F32, side="left"))
    wp_rwp = sc5q.enter_context(nc.sbuf_tensor("wp_rwp", [128, 4], F32, side="left"))
    wp_rwrec = sc5q.enter_context(nc.sbuf_tensor("wp_rwrec", [128, 2 * NIC], F32, side="left"))
    wp_rwq = sc5q.enter_context(nc.sbuf_tensor("wp_rwq", [128, 2 * NIC], F32, side="left"))
    GORD = {}
    _gi = 0
    for _i, (_k, _c, _q) in enumerate(PIECES):
        if _k in ("g", "u"):
            GORD[_i] = _gi
            _gi += 1
    wp_guard = {"pe": None, "ev": None}
    sc5 = contextlib.ExitStack()
    ln1f = sc5.enter_context(nc.sbuf_tensor("ln1f", [128, 1, D], F32, side="right"))
    ln1_bf2 = sc5.enter_context(nc.sbuf_tensor("ln1_bf2", [128, 2, D], BF16, side="right"))
    qx_tm = sc5.enter_context(nc.sbuf_tensor("qx_tm", [128, 2, D], FP8, side="right"))
    identf = sc5.enter_context(nc.sbuf_tensor("identf", [128, 128], F32, side="right"))
    rtmp = sc5.enter_context(nc.sbuf_tensor("rtmp", [128, NTC], F32, side="right"))
    row5 = sc5.enter_context(nc.sbuf_tensor("row5", [1, NT], F32, side="right"))
    ps8 = sc5.enter_context(nc.psum_tensor("ps8", [128, 4096], FP8))
    psY = sc5.enter_context(nc.psum_tensor("psY", [128, 512], F32))
    psZ = sc5.enter_context(nc.psum_tensor("psZ", [128, 512], F32))

    if True:
        b = blk
        dv5 = s_dv.n
        ac5 = s_ac.n
        pe5 = s_pe.n
        ev5 = s_ev.n
        wp_guard["pe"] = pe5
        wp_guard["ev"] = ev5

        @b.vector
        def _(v):
            for tc in range(NTC):
                if tc >= 1:
                    v.wait_ge(s_pe.h, pe5 + tc)
                    s_ev.inc(v.tensor_copy(
                        qx_fm[:, :, (tc - 1) * 128:tc * 128],
                        bass.AP(ps8.tensor if hasattr(ps8, "tensor") else ps8, 0,
                                [[4096, 128], [256, 16], [2, 128]])))
                v.wait_ge(s_ev.h, ev_w + 24 + tc + 1)
                for c in range(4):
                    v.bn_stats(stats[:, 6 * c:6 * c + 6], res1[:, tc, 512 * c:512 * (c + 1)])
                v.drain()
                v.bn_aggr(aggr[:, :], stats[:, :])
                v.drain()
                v.tensor_scalar_add(rstd[:, :], aggr[:, 1:2], EPS)
                v.drain()
                v.reciprocal(rstd[:, :], rstd[:, :])
                s_dv.inc(v.drain())
                v.wait_ge(s_ac.h, ac5 + 4 * tc + 1)
                v.tensor_scalar_mul(nmr[:, :], aggr[:, 0:1], -1.0)
                v.drain()
                v.tensor_mul(nmr[:, :], nmr[:, :], rstd[:, :])
                s_dv.inc(v.drain())
                v.wait_ge(s_ac.h, ac5 + 4 * tc + 2)
                v.reduce_max(rmx[:, tc:tc + 1], ln1f[:, 0, :], axis=AX.X,
                             apply_absolute_value=True)
                v.drain()
                v.reciprocal(rmx224[:, tc:tc + 1], rmx[:, tc:tc + 1])
                v.drain()
                v.tensor_scalar_mul(rmx224[:, tc:tc + 1], rmx224[:, tc:tc + 1], 224.0)
                s_dv.inc(v.drain())

        @b.scalar
        def _(s):
            for tc in range(NTC):
                s.wait_ge(s_dv.h, dv5 + 3 * tc + 1)
                s.activation(rstd[:, :], rstd[:, :], ACT.Sqrt)
                s_ac.inc(s.drain())
                s.wait_ge(s_dv.h, dv5 + 3 * tc + 2)
                if tc >= 2:
                    s.wait_ge(s_pe.h, pe5 + tc - 1)
                s.activation(ln1f[:, 0, :], res1[:, tc, :], ACT.Identity,
                             bias=nmr[:, 0:1], scale=rstd[:, 0:1])
                s_ac.inc(s.drain())
                if tc >= 2:
                    s.wait_ge(s_ln[tc % 2].h, 16 * ((tc - 2) // 2 + 1))
                s.activation(ln1_bf2[:, tc % 2, :], ln1f[:, 0, :], ACT.Copy)
                s_ac.inc(s.drain())
                s.wait_ge(s_dv.h, dv5 + 3 * tc + 3)
                s.activation(qx_tm[:, tc % 2, :], ln1f[:, 0, :], ACT.Copy,
                             scale=rmx224[:, tc:tc + 1])
                s_ac.inc(s.drain())

        @b.sync
        def _(sync):
            for tc in range(NTC):
                sync.wait_ge(s_ac.h, ac5 + 4 * tc + 3)
                s_ln[tc % 2].inc(sync.dma_start(out=ln1_d[tc * 128:(tc + 1) * 128, :],
                                                in_=ln1_bf2[:, tc % 2, :]), 16)

        @b.tensor
        def _(t):
            t.wait_ge(s_ev.h, ev5)
            for tc in range(NTC):
                t.wait_ge(s_ac.h, ac5 + 4 * tc + 4)
                if tc >= 1:
                    t.wait_ge(s_ev.h, ev5 + tc)
                ins = None
                for fc in range(NFC):
                    dst = bass.AP(ps8.tensor if hasattr(ps8, "tensor") else ps8,
                                  fc * 256, [[4096, 128], [2, 128]])
                    ins = t.transpose(dst, qx_tm[:, tc % 2, fc * 128:(fc + 1) * 128],
                                      ident8[:, :])
                s_pe.inc(ins)

        @b.vector
        def _(v):
            v.wait_ge(s_pe.h, pe5 + NTC)
            s_ev.inc(v.tensor_copy(
                qx_fm[:, :, (NTC - 1) * 128:NTC * 128],
                bass.AP(ps8.tensor if hasattr(ps8, "tensor") else ps8, 0,
                        [[4096, 128], [256, 16], [2, 128]])))

        @b.scalar
        def _(s):
            s.activation(identf[:, :], ident[:, :], ACT.Copy)
            s_ac.inc(s.drain())

        @b.vector
        def _(v):
            v.wait_ge(s_dv.h, dv5 + 3 * NTC)
            v.tensor_scalar_mul(rtmp[:, 0:NTC], rmx[:, 0:NTC], 4.0 / (448.0 * 448.0))
            s_dv.inc(v.drain())

        pe5b = s_pe.n
        dv5b = s_dv.n
        @b.tensor
        def _(t):
            t.wait_ge(s_dv.h, dv5b)
            t.wait_ge(s_ac.h, s_ac.n)
            for rnd in range(2):
                if rnd >= 1:
                    t.wait_ge(s_dv.h, dv5b + rnd)
                ins = None
                for bb in range(4):
                    ins = t.transpose(psY[0:1, bb * 128:(bb + 1) * 128],
                                      rtmp[:, rnd * 4 + bb:rnd * 4 + bb + 1], identf[:, :])
                s_pe.inc(ins)

        @b.vector
        def _(v):
            for rnd in range(2):
                v.wait_ge(s_pe.h, pe5b + rnd + 1)
                s_dv.inc(v.tensor_copy(row5[0:1, rnd * 512:(rnd + 1) * 512], psY[0:1, 0:512]))

        @b.sync
        def _(sync):
            sync.wait_ge(s_dv.h, dv5b + 2)
            s_rb5.inc(sync.dma_start(out=rowd5.rearrange("(o d) -> o d", o=1),
                                     in_=row5[0:1, 0:NT]), 16)
            sync.wait_ge(s_rb5.h, 16)
            s_rb5.inc(sync.dma_start(
                out=bc_rm[:, :], in_=bass.AP(rowd5.tensor, 0, [[0, 128], [1, NT]])), 16)

        if dbg:
            @b.sync
            def _(sync):
                sync.wait_ge(s_ac.h, s_ac.n)
                sync.wait_ge(s_ev.h, s_ev.n)
                for fc in range(NFC):
                    s_out.inc(sync.dma_start(out=io["dbg_qxf"][fc * 128:(fc + 1) * 128, :],
                                             in_=qx_fm[:, fc, :]), 16)
                sync.wait_ge(s_out.h, s_out.n)

    sc5.close()
    sc4.close()

    sc6t = contextlib.ExitStack()
    t_fm = sc6t.enter_context(nc.sbuf_tensor("t_fm", [128, NIC, NT], BF16, side="right"))
    Mabs = sc6t.enter_context(nc.sbuf_tensor("Mabs", [128, NT], BF16, side="right"))
    Mlo = sc6t.enter_context(nc.sbuf_tensor("Mlo", [128, NT], BF16, side="right"))
    sc6 = contextlib.ExitStack()
    gbuf = sc6.enter_context(nc.sbuf_tensor("gbuf", [128, NT], F32, side="right"))
    ubuf = sc6.enter_context(nc.sbuf_tensor("ubuf", [128, NT], F32, side="right"))
    psG = [sc6.enter_context(nc.psum_tensor(f"psG{j}", [128, 512], F32)) for j in range(8)]


    wrec = {"load": {}, "vec": {}, "sc": {}, "tr": {}, "act": {}, "cons": {}}
    wcur = {"load": 0, "loadu": 0, "loads": 0, "vec": 0, "sc": 0, "act": 0}
    wsrc = {"g": wg, "u": wu}
    IGORD = {d: i for i, d in GORD.items()}
    PM_DR = mybir.MatmulPerfMode.DoubleRow

    def pfree(i):
        k, c, q = PIECES[i]
        return PF[q] if k == "dr" else 2048

    def wp_load(sync, upto):
        upto = min(upto, NP)
        while wcur["load"] < upto:
            i = wcur["load"]
            if i % 2 == 1:
                wcur["load"] += 1
                continue
            if i >= NR and (i - NR) not in wrec["sc"]:
                return
            if i == 0 and wp_guard["pe"] is not None:
                sync.wait_ge(s_pe.h, wp_guard["pe"])
                sync.wait_ge(s_ev.h, wp_guard["ev"])
                sync.wait_ge(s_dnimg.h, 16 * NFC)
                sync.wait_ge(s_dnl.h, 16 * NFC)
                sync.wait_ge(s_dnq.h, NFC)
                sync.wait_ge(s_dntr[0].h, s_dntr[0].n)
                sync.wait_ge(s_dntr[1].h, s_dntr[1].n)
            wcur["load"] += 1
            j = i % NR
            k, c, q = PIECES[i]
            if i >= NR:
                sync.wait_ge(s_cons[j].h, wrec["sc"][i - NR])
            wrec["load"][i] = s_wr[j].inc(sync.dma_start(
                out=wraw[j][:, :], in_=wsrc[k][c * 128:(c + 1) * 128, :]), 16)

    def wp_load_u(g, upto):
        upto = min(upto, NP, wcur["load"] + 1)
        while wcur["loadu"] < upto:
            i = wcur["loadu"]
            if i % 4 != 1:
                wcur["loadu"] += 1
                continue
            if i >= NR and (i - NR) not in wrec["sc"]:
                return
            if i == 1:
                g.wait_ge(s_dnimg.h, 16 * NFC)
                g.wait_ge(s_dnq.h, NFC)
                g.wait_ge(s_dnv.h, NFC)
            wcur["loadu"] += 1
            j = i % NR
            k, c, q = PIECES[i]
            if i >= NR:
                g.wait_ge(s_cons[j].h, wrec["sc"][i - NR])
            wrec["load"][i] = s_wr[j].inc(g.dma_start(
                out=wraw[j][:, :], in_=wsrc[k][c * 128:(c + 1) * 128, :]), 16)

    def wp_load_s(s, upto):
        upto = min(upto, NP, wcur["load"] + 1)
        while wcur["loads"] < upto:
            i = wcur["loads"]
            if i % 4 != 3:
                wcur["loads"] += 1
                continue
            if i >= NR and (i - NR) not in wrec["sc"]:
                return
            wcur["loads"] += 1
            j = i % NR
            k, c, q = PIECES[i]
            if i >= NR:
                s.wait_ge(s_cons[j].h, wrec["sc"][i - NR])
            wrec["load"][i] = s_wru[j].inc(s.dma_start(
                out=wraw[j][:, :], in_=wsrc[k][c * 128:(c + 1) * 128, :]), 16)

    def wp_vec(v, upto):
        upto = min(upto, NP)
        while wcur["vec"] < upto:
            i = wcur["vec"]
            if i not in wrec["load"]:
                return
            wcur["vec"] += 1
            j = i % NR
            k, c, q = PIECES[i]
            f = pfree(i)
            if i % 4 == 3:
                v.wait_ge(s_wru[j].h, wrec["load"][i])
            else:
                v.wait_ge(s_wr[j].h, wrec["load"][i])
            if k == "dr":
                v.reduce_max(wp_rwp[:, q:q + 1], wraw[j][:, 0:f], axis=AX.X,
                             apply_absolute_value=True)
                if q == 2:
                    v.drain()
                    v.tensor_max(wp_rwp[:, 3:4], wp_rwp[:, 0:1], wp_rwp[:, 1:2])
                    v.drain()
                    v.tensor_max(rwd_keep[:, c:c + 1], wp_rwp[:, 3:4], wp_rwp[:, 2:3])
                wrec["vec"][i] = s_ws[j].inc(v.drain())
                continue
            d = GORD[i]
            v.reduce_max(wp_rwq[:, d:d + 1], wraw[j][:, 0:f], axis=AX.X,
                         apply_absolute_value=True)
            v.drain()
            v.reciprocal(wp_rwrec[:, d:d + 1], wp_rwq[:, d:d + 1])
            wrec["vec"][i] = s_ws[j].inc(v.drain())

    def wp_sct(s, upto):
        upto = min(upto, NP)
        while wcur["sc"] < upto:
            i = wcur["sc"]
            if PIECES[i][0] == "dr":
                wrec["tr"][i] = None
                wcur["sc"] += 1
                continue
            if i not in wrec["vec"]:
                return
            d = GORD[i]
            p3 = IGORD.get(d - NRT)
            if p3 is not None and p3 not in wrec["act"]:
                return
            wcur["sc"] += 1
            if i == 0:
                s.wait_ge(s_dnimg.h, 16 * NFC)
            j = i % NR
            js = d % NR
            jt = d % NRT
            p2 = IGORD.get(d - NR)
            if p2 is not None:
                s.wait_ge(s_wt[GORD[p2] % NRT].h, wrec["tr"][p2])
            s.wait_ge(s_ws[j].h, wrec["vec"][i])
            if p3 is not None:
                sem, val = wrec["act"][p3]
                s.wait_ge(sem.h, val)
            s.activation(wsc[js][:, :], wraw[j][:, :], ACT.Copy,
                         scale=wp_rwrec[:, GORD[i]:GORD[i] + 1])
            wrec["sc"][i] = s_cons[j].inc(s.drain())
            wrec["tr"][i] = s_wt[jt].inc(s.dma_start_transpose(
                wtrn[jt][:, :].rearrange("p (c j2) -> p c j2", j2=128),
                wsc[js][:, :].rearrange("p (c j2) -> p c j2", j2=128)), 16)

    def wp_act(v, upto):
        upto = min(upto, NP)
        while wcur["act"] < upto:
            i = wcur["act"]
            if PIECES[i][0] == "dr":
                wcur["act"] += 1
                continue
            if wrec["tr"].get(i) is None:
                return
            d = GORD[i]
            pq = IGORD.get(d - NQ)
            if pq is not None and pq not in wrec["cons"]:
                return
            wcur["act"] += 1
            if i == 0:
                v.wait_ge(s_dnimg.h, 16 * NFC)
            jt = d % NRT
            jq = d % NQ
            v.wait_ge(s_wt[jt].h, wrec["tr"][i])
            if pq is not None:
                sem, val = wrec["cons"][pq]
                v.wait_ge(sem.h, val)
            ins = v.tensor_scalar_mul(wq8[jq][:, :], wtrn[jt][:, :], 224.0)
            wrec["act"][i] = (s_w8[jq], s_w8[jq].inc(ins))

    GIDX = {}
    for i, (k, c, q) in enumerate(PIECES):
        if k in ("g", "u"):
            GIDX[(k, c)] = i

    if True:
        b = blk
        dv6 = s_dv.n
        ac6 = s_ac.n
        pe6 = s_pe.n

        @b.vector
        def _(v):
            v.wait_ge(s_ln[0].h, s_ln[0].n)
            v.wait_ge(s_ln[1].h, s_ln[1].n)
            v.wait_ge(s_rb5.h, 16)
            v.memset(Mabs[:, :], -1e30)
            v.memset(Mlo[:, :], 1e30)
            s_dv.inc(v.drain())
        dv6 += 1

        for jj in range(8):
            @b.sync
            def _(sync, jj=jj):
                wp_load(sync, 2 + 2 * jj)
            @b.gpsimd
            def _(g, jj=jj):
                wp_load_u(g, 2 + 2 * jj)
            @b.scalar
            def _(s, jj=jj):
                wp_load_s(s, 2 + 2 * jj)
            @b.vector
            def _(v, jj=jj):
                for _ in range(3):
                    wp_vec(v, 1 + 2 * jj)
                    wp_act(v, 2 * jj)
            @b.scalar
            def _(s, jj=jj):
                wp_sct(s, 2 * jj)

        for m in range(NIC):
            ig = GIDX[("g", m)]
            iu = GIDX[("u", m)]

            @b.tensor
            def _(t, m=m, ig=ig, iu=iu):
                if m == 0:
                    t.wait_ge(s_ev.h, s_ev.n)
                    t.wait_ge(s_rb5.h, 32)
                for pidx, pi in ((0, ig), (1, iu)):
                    jq = GORD[pi] % NQ
                    sem, val = wrec["act"][pi]
                    t.wait_ge(sem.h, val)
                    if m >= 2 and pidx == 0:
                        t.wait_ge(s_dv.h, dv6 + 2 * (m - 2) + 1)
                    ins = None
                    for n in range(2):
                        for k in range(8):
                            ins = t.matmul(psG[4 * (m % 2) + 2 * pidx + n][:, :],
                                           wq8[jq][:, k * 256:(k + 1) * 256].rearrange(
                                               "p (r mm) -> p r mm", r=2),
                                           qx_fm[:, 2 * k:2 * k + 2, n * 512:(n + 1) * 512],
                                           start=(k == 0), stop=(k == 7),
                                           perf_mode=PM_DR)
                    s_pe.inc(ins)
                    wrec["cons"][pi] = (s_pe, s_pe.n)

            for rr in range(3):
                @b.sync
                def _(sync, m=m):
                    tgt = (GIDX[("u", m + 1)] + 1 + 4) if m + 1 < NIC else NP
                    wp_load(sync, tgt)

                @b.gpsimd
                def _(g, m=m):
                    tgt = (GIDX[("u", m + 1)] + 1 + 4) if m + 1 < NIC else NP
                    wp_load_u(g, tgt)

                @b.scalar
                def _(s, m=m):
                    tgt = (GIDX[("u", m + 1)] + 1 + 4) if m + 1 < NIC else NP
                    wp_load_s(s, tgt)

                @b.vector
                def _(v, m=m):
                    tgt = (GIDX[("u", m + 1)] + 1 + 2) if m + 1 < NIC else NP
                    wp_vec(v, tgt)
                    wp_act(v, tgt - 2)

                @b.scalar
                def _(s, m=m):
                    tgt = (GIDX[("u", m + 1)] + 1 + 1) if m + 1 < NIC else NP
                    wp_sct(s, tgt)

            @b.vector
            def _(v, m=m, ig=ig, iu=iu):
                v.wait_ge(s_pe.h, pe6 + 2 * m + 2)
                for pidx, pi, buf in ((0, ig, gbuf), (1, iu, ubuf)):
                    for n in range(2):
                        v.scalar_tensor_tensor(
                            buf[:, n * 512:(n + 1) * 512],
                            psG[4 * (m % 2) + 2 * pidx + n][:, :],
                            wp_rwq[:, GORD[pi]:GORD[pi] + 1], bc_rm[:, n * 512:(n + 1) * 512],
                            ALU.mult, ALU.mult)
                s_dv.inc(v.drain())
                v.wait_ge(s_ac.h, ac6 + m + 1)
                v.tensor_mul(t_fm[:, m, :], gbuf[:, :], ubuf[:, :])
                v.drain()
                v.tensor_max(Mabs[:, :], Mabs[:, :], t_fm[:, m, :])
                v.tensor_tensor(Mlo[:, :], Mlo[:, :], t_fm[:, m, :], ALU.min)
                s_dv.inc(v.drain())

            @b.scalar
            def _(s, m=m, ig=ig, iu=iu):
                s.wait_ge(s_dv.h, dv6 + 2 * m + 1)
                s_ac.inc(s.activation(gbuf[:, :], gbuf[:, :], ACT.Silu))

        for jj in range(6):
            @b.sync
            def _(sync, jj=jj):
                wp_load(sync, NP)
            @b.gpsimd
            def _(g, jj=jj):
                wp_load_u(g, NP)
            @b.scalar
            def _(s, jj=jj):
                wp_load_s(s, NP)
            @b.vector
            def _(v, jj=jj):
                for _ in range(3):
                    wp_vec(v, NP)
                    wp_act(v, NP)
            @b.scalar
            def _(s, jj=jj):
                wp_sct(s, NP)

        if dbg:
            @b.sync
            def _(sync):
                sync.wait_ge(s_dv.h, s_dv.n)
                for mm in range(NIC):
                    s_out.inc(sync.dma_start(out=io["dbg_tf"][mm * 128:(mm + 1) * 128, :],
                                             in_=t_fm[:, mm, :]), 16)
                sync.wait_ge(s_out.h, s_out.n)

    sc6.close()
    sc5q.close()

    sc7 = contextlib.ExitStack()
    qt_fm = sc7.enter_context(nc.sbuf_tensor("qt_fm", [128, NIC, NT], FP8, side="left"))
    bc_rwd = sc7.enter_context(nc.sbuf_tensor("bc_rwd", [128, D], F32, side="left"))
    identf2 = sc7.enter_context(nc.sbuf_tensor("identf2", [128, 128], F32, side="left"))
    sc7b = contextlib.ExitStack()
    bc_qt = sc7b.enter_context(nc.sbuf_tensor("bc_qt", [128, NT], F32, side="right"))
    row7 = sc7b.enter_context(nc.sbuf_tensor("row7", [1, NT + D], F32, side="right"))
    psT7 = sc7b.enter_context(nc.psum_tensor("psT7", [128, 1024], BF16))
    psU7 = sc7b.enter_context(nc.psum_tensor("psU7", [128, 512], F32))

    if True:
        b = blk
        dv7 = s_dv.n
        pe7 = s_pe.n
        ev7 = s_ev.n

        @b.scalar
        def _(s):
            for j in range(3):
                s.wait_ge(s_wt[j].h, s_wt[j].n)
            s.activation(identf2[:, :], ident[:, :], ACT.Copy)
            s_ac.inc(s.drain())
        ac7 = s_ac.n

        @b.tensor
        def _(t):
            t.wait_ge(s_dv.h, dv7)
            ins = None
            for bb in range(NTC):
                ins = t.transpose(psT7[:, bb * 128:(bb + 1) * 128],
                                  Mabs[:, bb * 128:(bb + 1) * 128], ident[:, :])
            s_pe.inc(ins)
            t.wait_ge(s_dv.h, dv7 + 1)
            ins = None
            for bb in range(NTC):
                ins = t.transpose(psT7[:, bb * 128:(bb + 1) * 128],
                                  Mlo[:, bb * 128:(bb + 1) * 128], ident[:, :])
            s_pe.inc(ins)

        @b.vector
        def _(v):
            v.wait_ge(s_pe.h, pe7 + 1)
            for bb in range(NTC):
                v.reduce_max(rmt[:, bb:bb + 1], psT7[:, bb * 128:(bb + 1) * 128], axis=AX.X)
            s_dv.inc(v.drain())
            v.wait_ge(s_pe.h, pe7 + 2)
            for bb in range(NTC):
                v.tensor_reduce(rmx224[:, bb:bb + 1], psT7[:, bb * 128:(bb + 1) * 128],
                                axis=AX.X, op=ALU.min, negate=True)
            v.drain()
            v.tensor_max(rmt[:, :], rmt[:, :], rmx224[:, :])
            v.drain()
            v.tensor_scalar_mul(rmt4[:, :], rmt[:, :], 4.0 / 448.0)
            v.reciprocal(rmx224[:, :], rmt[:, :])
            v.drain()
            v.tensor_scalar_mul(rmx224[:, :], rmx224[:, :], 224.0)
            v.wait_ge(s_dnv.h, NFC)
            v.tensor_scalar_mul(rwdq[:, :], rwd_keep[:, :], 1.0 / 448.0)
            s_dv.inc(v.drain())

        pe7b = s_pe.n
        dv7b = s_dv.n
        @b.tensor
        def _(t):
            t.wait_ge(s_dv.h, dv7b)
            t.wait_ge(s_ac.h, ac7)
            rr = 0
            for rnd in range(2):
                if rr >= 1:
                    t.wait_ge(s_dv.h, dv7b + rr)
                ins = None
                for bb in range(4):
                    ins = t.transpose(psU7[0:1, bb * 128:(bb + 1) * 128],
                                      rmx224[:, rnd * 4 + bb:rnd * 4 + bb + 1], identf2[:, :])
                s_pe.inc(ins)
                rr += 1
            for rnd in range(4):
                t.wait_ge(s_dv.h, dv7b + rr)
                ins = None
                for bb in range(4):
                    ins = t.transpose(psU7[0:1, bb * 128:(bb + 1) * 128],
                                      rwdq[:, rnd * 4 + bb:rnd * 4 + bb + 1], identf2[:, :])
                s_pe.inc(ins)
                rr += 1

        @b.vector
        def _(v):
            for rr in range(6):
                v.wait_ge(s_pe.h, pe7b + rr + 1)
                s_dv.inc(v.tensor_copy(row7[0:1, rr * 512:(rr + 1) * 512], psU7[0:1, 0:512]))

        @b.sync
        def _(sync):
            sync.wait_ge(s_dv.h, dv7b + 6)
            s_rb7.inc(sync.dma_start(out=rowd7.rearrange("(o d) -> o d", o=1),
                                     in_=row7[0:1, 0:NT + D]), 16)
            sync.wait_ge(s_rb7.h, 16)
            s_rb7.inc(sync.dma_start(
                out=bc_qt[:, :], in_=bass.AP(rowd7.tensor, 0, [[0, 128], [1, NT]])), 16)
            s_rb7.inc(sync.dma_start(
                out=bc_rwd[:, :], in_=bass.AP(rowd7.tensor, NT, [[0, 128], [1, D]])), 16)

        dvq = s_dv.n
        qt_val = {}
        bch = bc_qt.tensor if hasattr(bc_qt, "tensor") else bc_qt
        bc4 = bass.AP(bch, 0, [[NT, 128], [0, 4], [1, NT]])
        for sq in range(NSQ):
            @b.vector
            def _(v, sq=sq):
                if sq == 0:
                    v.wait_ge(s_rb7.h, 48)
                s_dv.inc(v.tensor_tensor(qt_fm[:, 4 * sq:4 * sq + 4, :],
                                         t_fm[:, 4 * sq:4 * sq + 4, :], bc4, ALU.mult))
            qt_val[sq] = s_dv.n

        if dbg:
            @b.sync
            def _(sync):
                sync.wait_ge(s_dv.h, qt_val[NSQ - 1])
                for mm in range(NIC):
                    s_out.inc(sync.dma_start(out=io["dbg_qtf"][mm * 128:(mm + 1) * 128, :],
                                             in_=qt_fm[:, mm, :]), 16)
                sync.wait_ge(s_out.h, s_out.n)

    sc7b.close()
    sc6t.close()

    sc8 = contextlib.ExitStack()
    qwdT = sc8.enter_context(nc.sbuf_tensor("qwdT", [128, NIC, D], FP8, side="right"))
    res2 = sc8.enter_context(nc.sbuf_tensor("res2", [128, 2, D], F32, side="right"))
    lnst = sc8.enter_context(nc.sbuf_tensor("lnst", [128, 2, D], BF16, side="right"))
    psE = [sc8.enter_context(nc.psum_tensor(f"psE{j}", [128, 512], F32)) for j in range(8)]

    if True:
        b = blk
        pe8 = s_pe.n
        dv8 = s_dv.n
        ac8 = s_ac.n
        ev8 = s_ev.n
        qw_marks = {}

        @b.gpsimd
        def _(g):
            g.wait_ge(s_pe.h, pe8)
            g.wait_ge(s_rb7.h, 16)
            for tc in range(NTC):
                if tc >= 2:
                    g.wait_ge(s_dv.h, dv8 + 2 * (tc - 2) + 1)
                s_lnr[tc % 2].inc(g.dma_start(out=lnst[:, tc % 2, :],
                                              in_=ln1_d[tc * 128:(tc + 1) * 128, :]), 16)

        @b.sync
        def _(sync):
            sync.wait_ge(s_dnimg.h, 16 * NFC)
            for sq in range(NSQ):
                jj = sq % 2
                sync.wait_ge(s_dv.h, qt_val[sq])
                if sq >= 2:
                    sync.wait_ge(s_qw[jj].h, qw_marks[sq - 2][1])
                s_qw[jj].inc(sync.dma_start(
                    out=qwdT[:, 4 * sq:4 * sq + 4, :],
                    in_=qwd_img[sq * 512:(sq + 1) * 512, :].rearrange(
                        "(bk p) d -> p bk d", p=128)), 16)
                qw_marks[sq] = (jj, s_qw[jj].n)

        @b.tensor
        def _(t):
            t.wait_ge(s_dv.h, qt_val[0])
            for tg in range(4):
                if tg >= 1:
                    t.wait_ge(s_ev.h, ev8 + 8 * tg)
                ins = None
                for mi in range(0, NIC, 2):
                    if tg == 0 and (mi % 4) == 0:
                        jj, val = qw_marks[min(mi // 4, NSQ - 1)]
                        t.wait_ge(s_qw[jj].h, val)
                    for tci in range(2):
                        tc = 2 * tg + tci
                        for n in range(4):
                            ins = t.matmul(psE[4 * tci + n][:, :],
                                           qt_fm[:, mi:mi + 2, tc * 128:(tc + 1) * 128],
                                           qwdT[:, mi:mi + 2, n * 512:(n + 1) * 512],
                                           start=(mi == 0), stop=(mi == NIC - 2),
                                           perf_mode=PM_DR)
                s_pe.inc(ins)

        out_base = s_out.n

        @b.vector
        def _(v):
            for tg in range(4):
                v.wait_ge(s_pe.h, pe8 + tg + 1)
                for tci in range(2):
                    tc = 2 * tg + tci
                    if tg >= 1:
                        v.wait_ge(s_ost[tci].h, 16 * tg)
                    for n in range(4):
                        v.scalar_tensor_tensor(
                            res2[:, tci, n * 512:(n + 1) * 512], psE[4 * tci + n][:, :],
                            rmt4[:, tc:tc + 1], bc_rwd[:, n * 512:(n + 1) * 512],
                            ALU.mult, ALU.mult)
                    v.drain()
                    v.wait_ge(s_lnr[tc % 2].h, 16 * (tc // 2 + 1))
                    v.tensor_add(res2[:, tci, :], res2[:, tci, :], lnst[:, tc % 2, :])
                    v.drain()
                    for cc in range(4):
                        v.bn_stats(stats[:, 6 * cc:6 * cc + 6],
                                   res2[:, tci, 512 * cc:512 * (cc + 1)])
                    v.drain()
                    v.bn_aggr(aggr[:, :], stats[:, :])
                    v.drain()
                    if tc >= 1:
                        v.wait_ge(s_ac.h, ac8 + 2 * tc)
                    v.tensor_scalar_add(rstd[:, :], aggr[:, 1:2], EPS)
                    v.drain()
                    v.reciprocal(rstd[:, :], rstd[:, :])
                    s_dv.inc(v.drain())
                    v.wait_ge(s_ac.h, ac8 + 2 * tc + 1)
                    v.tensor_scalar_mul(nmr[:, :], aggr[:, 0:1], -1.0)
                    v.drain()
                    v.tensor_mul(nmr[:, :], nmr[:, :], rstd[:, :])
                    s_dv.inc(v.drain())
                for _ in range(8):
                    s_ev.inc(v.engine_nop())

        @b.scalar
        def _(s):
            for tg in range(4):
                for tci in range(2):
                    tc = 2 * tg + tci
                    s.wait_ge(s_dv.h, dv8 + 2 * tc + 1)
                    s.activation(rstd[:, :], rstd[:, :], ACT.Sqrt)
                    s_ac.inc(s.drain())
                    s.wait_ge(s_dv.h, dv8 + 2 * tc + 2)
                    s.activation(res2[:, tci, :], res2[:, tci, :], ACT.Identity,
                                 bias=nmr[:, 0:1], scale=rstd[:, 0:1])
                    s_ac.inc(s.drain())

        @b.sync
        def _(sync):
            for tg in range(4):
                for tci in range(2):
                    tc = 2 * tg + tci
                    sync.wait_ge(s_ac.h, ac8 + 2 * tc + 2)
                    s_ost[tci].inc(sync.dma_start(out=out[tc * 128:(tc + 1) * 128, :],
                                                  in_=res2[:, tci, :]), 16)
            if dbg:
                sync.wait_ge(s_qw[0].h, s_qw[0].n)
                sync.wait_ge(s_qw[1].h, s_qw[1].n)
                for mi in range(NIC):
                    s_out.inc(sync.dma_start(out=io["dbg_qwd"][mi * 128:(mi + 1) * 128, :],
                                             in_=qwdT[:, mi, :]), 16)
            sync.wait_ge(s_ost[0].h, s_ost[0].n)
            sync.wait_ge(s_ost[1].h, s_ost[1].n)
            sync.wait_ge(s_out.h, s_out.n)

    sc8.close()
    sc7.close()


def _host_prep(inputs):
    hs_full = np.ascontiguousarray(np.asarray(inputs["L0_attn_hidden_states"], np.float32)[0])
    rb = np.asarray(inputs["L0_attn_rel_bias"], np.float32)
    pos = np.arange(TW)
    rel = np.clip(pos[None, :] - pos[:, None], -128, 128) + 128
    bmat = np.ascontiguousarray(rb[rel])
    di_ = np.asarray(inputs["L0_ffn_down_reorder_index"])
    wdn = np.ascontiguousarray(np.asarray(inputs["L0_ffn_w_down"], np.float32)[:, np.argsort(di_)])
    import ml_dtypes
    com = dict(
        wc=np.ascontiguousarray(np.asarray(inputs["L0_attn_Wc"], np.float32)),
        wq=np.ascontiguousarray(np.asarray(inputs["L0_attn_Wq"], np.float32)),
        wk=np.ascontiguousarray(np.asarray(inputs["L0_attn_Wk"], np.float32)),
        wv=np.ascontiguousarray(np.asarray(inputs["L0_attn_Wv"], np.float32)),
        wd=np.ascontiguousarray(np.asarray(inputs["L0_attn_Wd"], np.float32)),
        bc=np.ascontiguousarray(np.asarray(inputs["L0_attn_bc"], np.float32)),
        bq=np.ascontiguousarray(np.asarray(inputs["L0_attn_bq"], np.float32)),
        bk=np.ascontiguousarray(np.asarray(inputs["L0_attn_bk"], np.float32)),
        bv=np.ascontiguousarray(np.asarray(inputs["L0_attn_bv"], np.float32)),
        bd=np.ascontiguousarray(np.asarray(inputs["L0_attn_bd"], np.float32)),
        bmat=bmat,
        idn=np.eye(128, dtype=ml_dtypes.bfloat16),
        idn8=np.eye(128, dtype=ml_dtypes.float8_e4m3),
        wg=np.ascontiguousarray(np.asarray(inputs["L0_ffn_w_gate"], np.float32)),
        wu=np.ascontiguousarray(np.asarray(inputs["L0_ffn_w_up"], np.float32)),
        wdn=wdn,
    )
    in_maps = []
    for c in range(8):
        m = dict(com)
        m["hs"] = np.ascontiguousarray(hs_full[c * NT:(c + 1) * NT])
        in_maps.append(m)
    return in_maps


def kernel(**inputs):
    nc = build_nc(dbg=False)
    in_maps = _host_prep(inputs)
    res = run_bass_kernel_spmd(nc, in_maps, core_ids=list(range(8)))
    outp = np.concatenate([np.asarray(r["out"]) for r in res.results], axis=0)
    return outp[None].astype(np.float32)


# version 24
# speedup vs baseline: 1.0274x; 1.0274x over previous
import contextlib
import numpy as np
import concourse.bass as bass
import concourse.mybir as mybir
from concourse.bass_utils import run_bass_kernel_spmd

F32 = mybir.dt.float32
BF16 = mybir.dt.bfloat16
FP8 = mybir.dt.float8e4
ALU = mybir.AluOpType
ACT = mybir.ActivationFunctionType
AX = mybir.AxisListType

NT, D, HC, NH, DH, TW = 1024, 2048, 1024, 16, 64, 512
NW = NT // TW
II = 5632
NIC, NFC, NKC, NTC = II // 128, D // 128, HC // 128, NT // 128
EPS = 1e-5

PIECES = [(k, m, 0) for m in range(NIC) for k in ("g", "u")]
NP = len(PIECES)
NSQ = 11


class Sem:
    def __init__(self, ctx, nc, name):
        self.h = ctx.enter_context(nc.semaphore(name))
        self.n = 0
    def inc(self, ins, k=1):
        ins.then_inc(self.h, k)
        self.n += k
        return self.n
    def wait(self, eng, val=None):
        eng.wait_ge(self.h, self.n if val is None else val)


def build_nc(dbg=False):
    nc = bass.Bass("TRN2", target_bir_lowering=False)
    io = {}
    def di(name, shape, dt=F32, kind="ExternalInput"):
        io[name] = nc.dram_tensor(name, shape, dt, kind=kind).ap()
    di("hs", [NT, D]); di("wc", [D, HC]); di("wq", [HC, HC]); di("wk", [HC, HC])
    di("wv", [HC, HC]); di("wd", [HC, D]); di("bc", [HC]); di("bq", [HC])
    di("bk", [HC]); di("bv", [HC]); di("bd", [D]); di("bmat", [TW, TW])
    di("idn", [128, 128], BF16); di("idn8", [128, 128], FP8)
    di("wg", [II, D]); di("wu", [II, D]); di("wdn", [D, II])
    di("out", [NT, D], F32, "ExternalOutput")
    if dbg:
        for nm, shp, dt in [("xc", [HC, NT], BF16), ("qf", [HC, NT], BF16),
                            ("kf", [HC, NT], BF16), ("vtm", [NT, HC], BF16),
                            ("ctx", [HC, NT], BF16), ("res1", [NT, D], F32),
                            ("x1", [NT, D], BF16), ("qxf", [D, NT], FP8),
                            ("tf", [II, NT], BF16), ("qtf", [II, NT], FP8),
                            ("qwd", [NIC * 128, D], FP8)]:
            di("dbg_" + nm, shp, dt, "ExternalOutput")
    with contextlib.ExitStack() as ctx:
        _build(nc, ctx, io, dbg)
    return nc


def _build(nc, ctx, io, dbg):
    E = ctx.enter_context
    hs, wc, wqd, wkd, wvd, wdd = io["hs"], io["wc"], io["wq"], io["wk"], io["wv"], io["wd"]
    bcd, bqd, bkd, bvd, bdd = io["bc"], io["bq"], io["bk"], io["bv"], io["bd"]
    bmat, idn, wg, wu, wdn, out = io["bmat"], io["idn"], io["wg"], io["wu"], io["wdn"], io["out"]
    idn8 = io["idn8"]

    ident = E(nc.sbuf_tensor("ident", [128, 128], BF16))
    ident8 = E(nc.sbuf_tensor("ident8", [128, 128], FP8))
    ones16 = E(nc.sbuf_tensor("ones16", [16, 128], F32))
    bc_sb = E(nc.sbuf_tensor("bc_sb", [128, NKC], F32))
    bq8_sb = E(nc.sbuf_tensor("bq8_sb", [128, NKC], F32))
    bk_sb = E(nc.sbuf_tensor("bk_sb", [128, NKC], F32))
    bv_sb = E(nc.sbuf_tensor("bv_sb", [128, NKC], F32))
    stats = E(nc.sbuf_tensor("stats", [128, 24], F32))
    aggr = E(nc.sbuf_tensor("aggr", [128, 2], F32))
    rstd = E(nc.sbuf_tensor("rstd", [128, 1], F32))
    nmr = E(nc.sbuf_tensor("nmr", [128, 1], F32))
    rmx = E(nc.sbuf_tensor("rmx", [128, NTC], F32))
    rmx224 = E(nc.sbuf_tensor("rmx224", [128, NTC], F32))
    rmt = E(nc.sbuf_tensor("rmt", [128, NTC], F32))
    rmt4 = E(nc.sbuf_tensor("rmt4", [128, NTC], F32))
    rwd_keep = E(nc.sbuf_tensor("rwd_keep", [128, NFC], F32))
    rwdq = E(nc.sbuf_tensor("rwdq", [128, NFC], F32))
    rwdrec = E(nc.sbuf_tensor("rwdrec", [128, NFC], F32))

    s_ld = Sem(ctx, nc, "s_ld")
    s_aw = {k: Sem(ctx, nc, f"s_aw_{k}") for k in ("c", "q", "k", "v", "d", "B")}
    s_tr = [Sem(ctx, nc, "s_tr0"), Sem(ctx, nc, "s_tr1")]
    s_pe = Sem(ctx, nc, "s_pe")
    s_dv = Sem(ctx, nc, "s_dv")
    s_ac = Sem(ctx, nc, "s_ac")
    s_ev = Sem(ctx, nc, "s_ev")
    s_ev2 = Sem(ctx, nc, "s_ev2")
    s_out = Sem(ctx, nc, "s_out")
    NR = 2
    NQ = 3
    s_wr = [Sem(ctx, nc, f"swr{j}") for j in range(NR)]
    s_ws = [Sem(ctx, nc, f"sws{j}") for j in range(NR)]
    s_wt = [Sem(ctx, nc, f"swt{j}") for j in range(3)]
    s_w8 = [Sem(ctx, nc, f"sw8{j}") for j in range(NQ)]
    s_cons = [Sem(ctx, nc, f"scons{j}") for j in range(NQ)]
    s_qd = [Sem(ctx, nc, f"sqd{j}") for j in range(2)]
    s_qw = [Sem(ctx, nc, "s_qwA"), Sem(ctx, nc, "s_qwB")]
    s_gp = Sem(ctx, nc, "s_gp")
    s_ost = [Sem(ctx, nc, "s_ostA"), Sem(ctx, nc, "s_ostB")]
    s_hs = [Sem(ctx, nc, "s_hsA"), Sem(ctx, nc, "s_hsB")]
    s_dnl = Sem(ctx, nc, "s_dnl")
    s_dnv = Sem(ctx, nc, "s_dnv")
    s_dnsc = Sem(ctx, nc, "s_dnsc")
    s_dntr = [Sem(ctx, nc, "s_dntrA"), Sem(ctx, nc, "s_dntrB")]
    s_dnq = Sem(ctx, nc, "s_dnq")
    s_dnimg = Sem(ctx, nc, "s_dnimg")
    s_wdn2 = [Sem(ctx, nc, "s_wdnA"), Sem(ctx, nc, "s_wdnB")]
    s_ln = [Sem(ctx, nc, "s_lnA"), Sem(ctx, nc, "s_lnB")]
    s_lnr = [Sem(ctx, nc, "s_lnrA"), Sem(ctx, nc, "s_lnrB")]
    s_rb5 = Sem(ctx, nc, "s_rb5")
    s_rb7 = Sem(ctx, nc, "s_rb7")

    ln1_d = nc.dram_tensor("ln1_d", [NT, D], BF16).ap()
    rowd5 = nc.dram_tensor("rowd5", [NT], F32).ap()
    rowd7 = nc.dram_tensor("rowd7", [NT + D], F32).ap()
    qwd_img = nc.dram_tensor("qwd_img", [II, D], FP8).ap()

    blk = ctx.enter_context(nc.Block())

    sc_hs = contextlib.ExitStack()
    hs_tm = sc_hs.enter_context(nc.sbuf_tensor("hs_tm", [128, NTC, D], BF16, side="left"))
    sc_dn = contextlib.ExitStack()
    dn_raw = sc_dn.enter_context(nc.sbuf_tensor("dn_raw", [128, II], F32, side="left"))
    dn_sc = sc_dn.enter_context(nc.sbuf_tensor("dn_sc", [128, II // 2], BF16, side="left"))
    dn_tr = sc_dn.enter_context(nc.sbuf_tensor("dn_tr", [128, 2, II // 2], BF16, side="left",
                                               is_transpose_dest=True))
    dn_st = sc_dn.enter_context(nc.sbuf_tensor("dn_st", [128, II], FP8, side="left"))
    sc_xc = contextlib.ExitStack()
    xc_fm = sc_xc.enter_context(nc.sbuf_tensor("xc_fm", [128, NKC, NT], BF16, side="right"))
    sc01 = contextlib.ExitStack()
    hs_fm = sc01.enter_context(nc.sbuf_tensor("hs_fm", [128, NFC, NT], BF16, side="right"))
    wc_sb = sc01.enter_context(nc.sbuf_tensor("wc_sb", [128, NFC, HC], BF16, side="right"))
    sc00 = contextlib.ExitStack()
    hs_st = sc00.enter_context(nc.sbuf_tensor("hs_st", [128, 2, D], F32, side="right"))
    hs_bf = sc00.enter_context(nc.sbuf_tensor("hs_bf", [128, 2, D], BF16, side="right"))
    bd_bc = sc00.enter_context(nc.sbuf_tensor("bd_bc", [128, D], F32, side="right"))
    scps0 = contextlib.ExitStack()
    psA = scps0.enter_context(nc.psum_tensor("psA", [128, 512], F32))
    psB = scps0.enter_context(nc.psum_tensor("psB", [128, 512], F32))
    ps = [psA, psB]

    HF = II // 2
    dncur = {"load": 0, "vec": 0, "sc": 0, "q8": 0, "img": 0}
    dnrec = {"tr": {}}

    def dn_load(g, upto):
        while dncur["load"] < min(upto, NFC):
            c = dncur["load"]
            if c >= 1 and dncur["sc"] < 2 * c:
                return
            dncur["load"] += 1
            if c >= 1:
                g.wait_ge(s_dnsc.h, 2 * c)
            s_dnl.inc(g.dma_start(out=dn_raw[:, :], in_=wdn[c * 128:(c + 1) * 128, :]), 16)

    def dn_vec(v, upto):
        while dncur["vec"] < min(upto, NFC, dncur["load"]):
            c = dncur["vec"]; dncur["vec"] += 1
            v.wait_ge(s_dnl.h, 16 * (c + 1))
            v.reduce_max(rwd_keep[:, c:c + 1], dn_raw[:, :], axis=AX.X,
                         apply_absolute_value=True)
            v.drain()
            v.reciprocal(rwdrec[:, c:c + 1], rwd_keep[:, c:c + 1])
            s_dnv.inc(v.drain())

    def dn_sct(s, upto):
        while dncur["sc"] < min(upto, 2 * NFC, 2 * dncur["vec"]):
            ih = dncur["sc"]
            c, h = ih // 2, ih % 2
            if ih >= 2 and dncur["q8"] < ih // 2:
                return
            dncur["sc"] += 1
            s.wait_ge(s_dnv.h, c + 1)
            if ih >= 1:
                s.wait_ge(s_dntr[(ih - 1) % 2].h, dnrec["tr"][ih - 1])
            if ih >= 2:
                s.wait_ge(s_dnq.h, ih // 2)
            s.activation(dn_sc[:, :], dn_raw[:, h * HF:(h + 1) * HF], ACT.Copy,
                         scale=rwdrec[:, c:c + 1])
            s_dnsc.inc(s.drain())
            dnrec["tr"][ih] = s_dntr[ih % 2].inc(s.dma_start_transpose(
                dn_tr[:, ih % 2, :].rearrange("p (b j) -> p b j", j=128),
                dn_sc[:, :].rearrange("p (b j) -> p b j", j=128)), 16)

    def dn_q8(v, upto):
        while dncur["q8"] < min(upto, NFC) and dncur["sc"] >= 2 * (dncur["q8"] + 1):
            c = dncur["q8"]; dncur["q8"] += 1
            v.wait_ge(s_dntr[0].h, dnrec["tr"][2 * c])
            v.wait_ge(s_dntr[1].h, dnrec["tr"][2 * c + 1])
            if c >= 1:
                v.wait_ge(s_dnimg.h, 16 * c)
            s_dnq.inc(v.tensor_scalar_mul(dn_st[:, :], dn_tr[:, :, :], 224.0))

    def dn_img(g, upto):
        while dncur["img"] < min(upto, NFC, dncur["q8"]):
            c = dncur["img"]; dncur["img"] += 1
            g.wait_ge(s_dnq.h, c + 1)
            s_dnimg.inc(g.dma_start(
                out=bass.AP(qwd_img.tensor, c * 128, [[D, 128], [128 * D, NIC], [1, 128]]),
                in_=dn_st[:, :].rearrange("p (b j) -> p b j", j=128)), 16)

    def dn_round(b, n):
        for _ in range(n):
            @b.gpsimd
            def _(g):
                dn_load(g, NFC)
                dn_img(g, NFC)
            @b.vector
            def _(v):
                dn_vec(v, NFC)
                dn_q8(v, NFC)
            @b.scalar
            def _(s):
                dn_sct(s, 2 * NFC)

    if True:
        b = blk
        ac0 = s_ac.n
        dv0 = s_dv.n
        @b.sync
        def _(sync):
            s_ld.inc(sync.dma_start(out=ident[:, :], in_=idn[:, :]), 16)
            s_ld.inc(sync.dma_start(out=ident8[:, :], in_=idn8[:, :]), 16)
            for dst, src in ((bc_sb, bcd), (bq8_sb, bqd), (bk_sb, bkd), (bv_sb, bvd)):
                s_ld.inc(sync.dma_start(out=dst[:, :], in_=src.rearrange("(c p) -> p c", p=128),
                                        allow_slow_non_contiguous=True), 16)
            s_ld.inc(sync.dma_start(out=bd_bc[:, :],
                                    in_=bass.AP(bdd.tensor, 0, [[0, 128], [1, D]])), 16)
            for tc in range(NTC):
                if tc >= 2:
                    sync.wait_ge(s_ac.h, ac0 + 2 * (tc - 2) + 2)
                    sync.wait_ge(s_dv.h, dv0 + 1 + (tc - 2) + 1)
                s_hs[tc % 2].inc(sync.dma_start(out=hs_st[:, tc % 2, :],
                                                in_=hs[tc * 128:(tc + 1) * 128, :]), 16)

        @b.gpsimd
        def _(g):
            s_aw["c"].inc(g.dma_start(out=wc_sb[:, :, :],
                          in_=wc.rearrange("(c p) j -> p c j", p=128)), 16)
            s_gp.inc(g.memset(ones16[:, :], 1.0))

        @b.scalar
        def _(s):
            for tc in range(NTC):
                s.wait_ge(s_hs[tc % 2].h, 16 * (tc // 2 + 1))
                if tc >= 2:
                    s.wait_ge(s_tr[tc % 2].h, 16 * ((tc - 2) // 2 + 1))
                s_ac.inc(s.activation(hs_bf[:, tc % 2, :], hs_st[:, tc % 2, :], ACT.Copy))
                s_ac.inc(s.drain())
                s_tr[tc % 2].inc(s.dma_start_transpose(
                    hs_fm[:, :, tc * 128:(tc + 1) * 128],
                    hs_bf[:, tc % 2, :].rearrange("p (c j) -> p c j", j=128)), 16)

        @b.vector
        def _(v):
            s_ld.wait(v)
            v.tensor_scalar_mul(bq8_sb[:, :], bq8_sb[:, :], 0.125)
            s_dv.inc(v.drain())
            for tc in range(NTC):
                v.wait_ge(s_hs[tc % 2].h, 16 * (tc // 2 + 1))
                s_dv.inc(v.scalar_tensor_tensor(hs_tm[:, tc, :], hs_st[:, tc % 2, :], 1.0,
                                                bd_bc[:, :], ALU.mult, ALU.add))

    dn_round(blk, 4)
    sc00.close()

    if True:
        b = blk
        pe1 = s_pe.n
        e21 = s_ev2.n
        @b.tensor
        def _(t):
            s_aw["c"].wait(t)
            s_tr[0].wait(t)
            s_tr[1].wait(t)
            t.wait_ge(s_ev.h, s_ev.n)
            i = 0
            for m in range(NKC):
                for n in range(2):
                    if i >= 2:
                        t.wait_ge(s_ev2.h, e21 + i - 1)
                    ins = None
                    for k in range(NFC):
                        ins = t.matmul(ps[i % 2][:, :], wc_sb[:, k, m * 128:(m + 1) * 128],
                                       hs_fm[:, k, n * 512:(n + 1) * 512],
                                       start=(k == 0), stop=(k == NFC - 1))
                    s_pe.inc(ins)
                    i += 1

        @b.scalar
        def _(s):
            i = 0
            for m in range(NKC):
                for n in range(2):
                    s_pe.wait(s, pe1 + i + 1)
                    s_ev2.inc(s.activation(xc_fm[:, m, n * 512:(n + 1) * 512], ps[i % 2][:, :],
                                           ACT.Identity, bias=bc_sb[:, m:m + 1]))
                    i += 1
        if dbg:
            @b.sync
            def _(sync):
                sync.wait_ge(s_ev2.h, s_ev2.n)
                for m in range(NKC):
                    s_out.inc(sync.dma_start(out=io["dbg_xc"][m * 128:(m + 1) * 128, :],
                                             in_=xc_fm[:, m, :]), 16)
                sync.wait_ge(s_out.h, s_out.n)

    dn_round(blk, 4)
    sc01.close()

    sc_ctx = contextlib.ExitStack()
    ctx_fm = sc_ctx.enter_context(nc.sbuf_tensor("ctx_fm", [128, NKC, NT], BF16, side="left"))
    sc2 = contextlib.ExitStack()
    q_fm = sc2.enter_context(nc.sbuf_tensor("q_fm", [128, NKC, NT], BF16, side="right"))
    k_fm = sc2.enter_context(nc.sbuf_tensor("k_fm", [128, NKC, NT], BF16, side="right"))
    v_tm = sc2.enter_context(nc.sbuf_tensor("v_tm", [128, NTC, HC], BF16, side="right"))
    sc2b = contextlib.ExitStack()
    v_fm = sc2b.enter_context(nc.sbuf_tensor("v_fm", [128, NKC, NT], BF16, side="right"))
    wx_sb = sc2b.enter_context(nc.sbuf_tensor("wx_sb", [128, 2, NKC, HC], BF16, side="right"))
    psT16 = sc2b.enter_context(nc.psum_tensor("psT16", [128, 1024], BF16))

    if True:
        b = blk
        pe2 = s_pe.n
        e22 = s_ev2.n
        @b.gpsimd
        def _(g):
            g.wait_ge(s_tr[0].h, s_tr[0].n)
            g.wait_ge(s_tr[1].h, s_tr[1].n)
            g.wait_ge(s_dv.h, s_dv.n)
            g.wait_ge(s_ac.h, s_ac.n)
            s_aw["q"].inc(g.dma_start(out=wx_sb[:, 0, :, :],
                          in_=wqd.rearrange("(c p) j -> p c j", p=128)), 16)
            s_aw["k"].inc(g.dma_start(out=wx_sb[:, 1, :, :],
                          in_=wkd.rearrange("(c p) j -> p c j", p=128)), 16)
            g.wait_ge(s_pe.h, pe2 + 16)
            s_aw["v"].inc(g.dma_start(out=wx_sb[:, 0, :, :],
                          in_=wvd.rearrange("(c p) j -> p c j", p=128)), 16)

        dsts = None
        @b.tensor
        def _(t):
            t.wait_ge(s_ev2.h, e22)
            projs = [(0, "q"), (1, "k"), (0, "v")]
            i = 0
            for pi, (wslot, awk) in enumerate(projs):
                t.wait_ge(s_aw[awk].h, 16)
                for m in range(NKC):
                    for n in range(2):
                        if i >= 2:
                            t.wait_ge(s_ev2.h, e22 + i - 1)
                        ins = None
                        for k in range(NKC):
                            ins = t.matmul(ps[i % 2][:, :],
                                           wx_sb[:, wslot, k, m * 128:(m + 1) * 128],
                                           xc_fm[:, k, n * 512:(n + 1) * 512],
                                           start=(k == 0), stop=(k == NKC - 1))
                        s_pe.inc(ins)
                        i += 1

        @b.scalar
        def _(s):
            projs2 = [(q_fm, bq8_sb, 0.125), (k_fm, bk_sb, 1.0), (v_fm, bv_sb, 1.0)]
            i = 0
            for dst, bias, scale in projs2:
                for m in range(NKC):
                    for n in range(2):
                        s_pe.wait(s, pe2 + i + 1)
                        s_ev2.inc(s.activation(dst[:, m, n * 512:(n + 1) * 512],
                                               ps[i % 2][:, :], ACT.Identity,
                                               bias=bias[:, m:m + 1], scale=scale))
                        i += 1

        pe_vt = s_pe.n
        ev_vt = s_ev.n
        @b.tensor
        def _(t):
            t.wait_ge(s_ev2.h, e22 + 48)
            for tc in range(NTC):
                if tc >= 1:
                    t.wait_ge(s_ev.h, ev_vt + tc)
                ins = None
                for m in range(NKC):
                    ins = t.transpose(psT16[:, m * 128:(m + 1) * 128],
                                      v_fm[:, m, tc * 128:(tc + 1) * 128], ident[:, :])
                s_pe.inc(ins)

        @b.vector
        def _(v):
            for tc in range(NTC):
                s_pe.wait(v, pe_vt + tc + 1)
                s_ev.inc(v.tensor_copy(v_tm[:, tc, :], psT16[:, :]))

        if dbg:
            @b.sync
            def _(sync):
                sync.wait_ge(s_ev2.h, s_ev2.n)
                sync.wait_ge(s_ev.h, s_ev.n)
                for m in range(NKC):
                    s_out.inc(sync.dma_start(out=io["dbg_qf"][m * 128:(m + 1) * 128, :],
                                             in_=q_fm[:, m, :]), 16)
                    s_out.inc(sync.dma_start(out=io["dbg_kf"][m * 128:(m + 1) * 128, :],
                                             in_=k_fm[:, m, :]), 16)
                for tc in range(NTC):
                    s_out.inc(sync.dma_start(out=io["dbg_vtm"][tc * 128:(tc + 1) * 128, :],
                                             in_=v_tm[:, tc, :]), 16)
                sync.wait_ge(s_out.h, s_out.n)

    dn_round(blk, 5)
    sc2b.close()
    scps0.close()

    scps34 = contextlib.ExitStack()
    sc3 = contextlib.ExitStack()
    Bsb = sc3.enter_context(nc.sbuf_tensor("Bsb", [128, 4, TW], BF16, side="right"))
    sneg = sc3.enter_context(nc.sbuf_tensor("sneg", [128, 4, TW], F32, side="right"))
    pexp = sc3.enter_context(nc.sbuf_tensor("pexp", [128, 4, TW], BF16, side="right"))
    probsT = sc3.enter_context(nc.sbuf_tensor("probsT", [128, 4, TW], BF16, side="right"))
    m4 = sc3.enter_context(nc.sbuf_tensor("m4", [128, 4], F32, side="right"))
    den4 = sc3.enter_context(nc.sbuf_tensor("den4", [128, 4], F32, side="right"))
    rc4 = sc3.enter_context(nc.sbuf_tensor("rc4", [128, 4], F32, side="right"))
    psc4 = sc3.enter_context(nc.psum_tensor("psc4", [128, 2048], F32))
    psPT = sc3.enter_context(nc.psum_tensor("psPT", [128, 1024], BF16))
    psPT2 = sc3.enter_context(nc.psum_tensor("psPT2", [128, 1024], BF16))
    psX = sc3.enter_context(nc.psum_tensor("psX", [128, 512], F32))
    psc = [psc4[:, qc * 512:(qc + 1) * 512] for qc in range(4)]
    pn = pexp

    WH = [(w, h) for w in range(NW) for h in range(NH)]

    def q_ap(w, h, qc):
        base = (h % 2) * 64
        return q_fm[base:base + 64, h // 2, w * TW + qc * 128: w * TW + (qc + 1) * 128]

    def k_ap(w, h):
        base = (h % 2) * 64
        return k_fm[base:base + 64, h // 2, w * TW:(w + 1) * TW]

    def ptbank(kc):
        return (psPT if kc < 2 else psPT2), (kc % 2) * 512

    NWH = len(WH)

    if True:
        b = blk
        pe_s = s_pe.n
        dv_s = s_dv.n
        ac_s = s_ac.n

        def pe_sc_val(i, qc):
            return pe_s + (qc + 1 if i == 0 else 4 + 6 * (i - 1) + qc + 1)

        def pe_tr_val(i):
            return pe_s + 4 + 6 * i + (4 if i + 1 < NWH else 0) + 1

        def pe_cx_val(i):
            return pe_tr_val(i) + 1

        @b.gpsimd
        def _(g):
            g.wait_ge(s_pe.h, pe_s)
            s_aw["B"].inc(g.dma_start(out=Bsb[:, :, :],
                          in_=bmat.rearrange("(c p) k -> p c k", p=128)), 16)

        @b.tensor
        def _(t):
            s_aw["B"].wait(t)
            t.wait_ge(s_ev.h, ev_vt + 8)
            w0, h0 = WH[0]
            for qc in range(4):
                s_pe.inc(t.matmul(psc[qc], q_ap(w0, h0, qc), k_ap(w0, h0),
                                  start=True, stop=True))
            for i in range(len(WH)):
                if i + 1 < len(WH):
                    w2, h2 = WH[i + 1]
                    t.wait_ge(s_dv.h, dv_s + 2 * i + 1)
                    for qc in range(4):
                        s_pe.inc(t.matmul(psc[qc], q_ap(w2, h2, qc), k_ap(w2, h2),
                                          start=True, stop=True))
                w, h = WH[i]
                t.wait_ge(s_dv.h, dv_s + 2 * i + 2)
                if i >= 1:
                    t.wait_ge(s_ac.h, ac_s + 3 * i - 1)
                ins = None
                for kc in range(4):
                    bank, off = ptbank(kc)
                    for qc in range(4):
                        ins = t.transpose(bank[:, off + qc * 128:off + (qc + 1) * 128],
                                          pn[:, qc, kc * 128:(kc + 1) * 128], ident[:, :])
                s_pe.inc(ins)
                t.wait_ge(s_ac.h, ac_s + 3 * i + 2)
                for kc in range(4):
                    ins = t.matmul(psX[0:64, :], v_tm[:, w * 4 + kc, h * 64:(h + 1) * 64],
                                   probsT[:, kc, :], start=(kc == 0), stop=(kc == 3))
                s_pe.inc(ins)

        rc4h = rc4.tensor if hasattr(rc4, "tensor") else rc4
        rc4b = bass.AP(rc4h, 0, [[4, 128], [1, 4], [0, TW]])

        @b.vector
        def _(v):
            for i in range(len(WH)):
                v.wait_ge(s_pe.h, pe_sc_val(i, 3))
                v.tensor_add(sneg[:, :, :], psc4[:, :], Bsb[:, :, :])
                v.drain()
                v.reduce_max(m4[:, :], sneg[:, :, :], axis=AX.X, negate=True)
                s_dv.inc(v.drain())
                v.wait_ge(s_ac.h, ac_s + 3 * i + 1)
                if i >= 1:
                    v.wait_ge(s_pe.h, pe_tr_val(i - 1))
                v.reciprocal(rc4[:, :], den4[:, :])
                v.drain()
                v.tensor_tensor(pn[:, :, :], pexp[:, :, :], rc4b, ALU.mult)
                s_dv.inc(v.drain())

        @b.scalar
        def _(s):
            for i in range(len(WH)):
                s.wait_ge(s_dv.h, dv_s + 2 * i + 1)
                if i >= 1:
                    s.wait_ge(s_pe.h, pe_tr_val(i - 1))
                for qc in range(4):
                    s.activation(pexp[:, qc, :], sneg[:, qc, :], ACT.Exp,
                                 bias=m4[:, qc:qc + 1], scale=1.0,
                                 accum_out=den4[:, qc:qc + 1])
                s_ac.inc(s.drain())
                s.wait_ge(s_pe.h, pe_tr_val(i))
                for kc in range(4):
                    bank, off = ptbank(kc)
                    s.activation(probsT[:, kc, :], bank[:, off:off + 512], ACT.Copy)
                s_ac.inc(s.drain())
                s.wait_ge(s_pe.h, pe_cx_val(i))
                w, h = WH[i]
                base = (h % 2) * 64
                s_ac.inc(s.activation(ctx_fm[base:base + 64, h // 2, w * TW:(w + 1) * TW],
                                      psX[0:64, :], ACT.Copy))

        if dbg:
            @b.sync
            def _(sync):
                sync.wait_ge(s_ac.h, s_ac.n)
                for m in range(NKC):
                    s_out.inc(sync.dma_start(out=io["dbg_ctx"][m * 128:(m + 1) * 128, :],
                                             in_=ctx_fm[:, m, :]), 16)
                sync.wait_ge(s_out.h, s_out.n)

    dn_round(blk, 3)
    sc3.close()
    sc2.close()
    sc_xc.close()

    sc4 = contextlib.ExitStack()
    psC = scps34.enter_context(nc.psum_tensor("psC", [128, 512], F32))
    psD = scps34.enter_context(nc.psum_tensor("psD", [128, 512], F32))
    res1 = sc4.enter_context(nc.sbuf_tensor("res1_tm", [128, NTC, D], F32, side="right"))
    sc4w = contextlib.ExitStack()
    wd_n = sc4w.enter_context(nc.sbuf_tensor("wd_n", [128, 2, NKC, 512], BF16, side="right"))
    ps2 = [psC, psD]

    if True:
        b = blk
        pe_w = s_pe.n
        ev_w = s_ev.n
        @b.gpsimd
        def _(g):
            for n in range(4):
                if n == 0:
                    g.wait_ge(s_dv.h, s_dv.n)
                    g.wait_ge(s_ac.h, s_ac.n)
                    g.wait_ge(s_pe.h, pe_w)
                if n >= 2:
                    g.wait_ge(s_pe.h, pe_w + 8 * (n - 1))
                s_wdn2[n % 2].inc(g.dma_start(
                    out=wd_n[:, n % 2, :, :],
                    in_=wdd.rearrange("(c p) j -> p c j", p=128)[:, :, n * 512:(n + 1) * 512]), 16)

        @b.tensor
        def _(t):
            t.wait_ge(s_ac.h, s_ac.n)
            i = 0
            for n in range(4):
                t.wait_ge(s_wdn2[n % 2].h, 16 * (n // 2 + 1))
                for tc in range(NTC):
                    if i >= 2:
                        t.wait_ge(s_ev.h, ev_w + i - 1)
                    ins = None
                    for k in range(NKC):
                        ins = t.matmul(ps2[i % 2][:, :], ctx_fm[:, k, tc * 128:(tc + 1) * 128],
                                       wd_n[:, n % 2, k, :],
                                       start=(k == 0), stop=(k == NKC - 1))
                    s_pe.inc(ins)
                    i += 1

        @b.vector
        def _(v):
            i = 0
            for n in range(4):
                for tc in range(NTC):
                    v.wait_ge(s_pe.h, pe_w + i + 1)
                    s_ev.inc(v.scalar_tensor_tensor(
                        res1[:, tc, n * 512:(n + 1) * 512], ps2[i % 2][:, :], 1.0,
                        hs_tm[:, tc, n * 512:(n + 1) * 512], ALU.mult, ALU.add))
                    i += 1
        if dbg:
            @b.sync
            def _(sync):
                sync.wait_ge(s_ev.h, s_ev.n)
                for tc in range(NTC):
                    s_out.inc(sync.dma_start(out=io["dbg_res1"][tc * 128:(tc + 1) * 128, :],
                                             in_=res1[:, tc, :]), 16)
                sync.wait_ge(s_out.h, s_out.n)

    dn_round(blk, 6)
    assert dncur["img"] == NFC, f"dn prepass incomplete: {dncur}"
    sc4w.close()
    sc_ctx.close()
    sc_dn.close()
    sc_hs.close()
    scps34.close()

    sc5q = contextlib.ExitStack()
    qx_fm = sc5q.enter_context(nc.sbuf_tensor("qx_fm", [128, NFC, NT], FP8, side="left"))
    bc_rm = sc5q.enter_context(nc.sbuf_tensor("bc_rm", [128, NT], F32, side="left"))
    NRT = 3
    wraw = [sc5q.enter_context(nc.sbuf_tensor(f"wp_raw{j}", [128, 2048], F32, side="left"))
            for j in range(NR)]
    wsc = [sc5q.enter_context(nc.sbuf_tensor(f"wp_sc{j}", [128, 2048], BF16, side="left"))
           for j in range(NR)]
    wtrn = [sc5q.enter_context(nc.sbuf_tensor(f"wp_tr{j}", [128, 2048], BF16, side="left",
                                              is_transpose_dest=True)) for j in range(NRT)]
    wq8 = [sc5q.enter_context(nc.sbuf_tensor(f"wp_q8{j}", [128, 2048], FP8, side="left"))
           for j in range(NQ)]
    wp_rw = sc5q.enter_context(nc.sbuf_tensor("wp_rw", [128, NR], F32, side="left"))
    wp_rwp = sc5q.enter_context(nc.sbuf_tensor("wp_rwp", [128, 4], F32, side="left"))
    wp_rwrec = sc5q.enter_context(nc.sbuf_tensor("wp_rwrec", [128, 2 * NIC], F32, side="left"))
    wp_rwq = sc5q.enter_context(nc.sbuf_tensor("wp_rwq", [128, 2 * NIC], F32, side="left"))
    GORD = {}
    _gi = 0
    for _i, (_k, _c, _q) in enumerate(PIECES):
        if _k in ("g", "u"):
            GORD[_i] = _gi
            _gi += 1
    wp_guard = {"pe": None, "ev": None}
    sc5 = contextlib.ExitStack()
    ln1f = sc5.enter_context(nc.sbuf_tensor("ln1f", [128, 1, D], F32, side="right"))
    ln1_bf2 = sc5.enter_context(nc.sbuf_tensor("ln1_bf2", [128, 2, D], BF16, side="right"))
    qx_tm = sc5.enter_context(nc.sbuf_tensor("qx_tm", [128, 2, D], FP8, side="right"))
    identf = sc5.enter_context(nc.sbuf_tensor("identf", [128, 128], F32, side="right"))
    rtmp = sc5.enter_context(nc.sbuf_tensor("rtmp", [128, NTC], F32, side="right"))
    row5 = sc5.enter_context(nc.sbuf_tensor("row5", [1, NT], F32, side="right"))
    ps8 = sc5.enter_context(nc.psum_tensor("ps8", [128, 4096], FP8))
    psY = sc5.enter_context(nc.psum_tensor("psY", [128, 512], F32))
    psZ = sc5.enter_context(nc.psum_tensor("psZ", [128, 512], F32))

    if True:
        b = blk
        dv5 = s_dv.n
        ac5 = s_ac.n
        pe5 = s_pe.n
        ev5 = s_ev.n
        wp_guard["pe"] = pe5
        wp_guard["ev"] = ev5

        @b.vector
        def _(v):
            for tc in range(NTC):
                if tc >= 1:
                    v.wait_ge(s_pe.h, pe5 + tc)
                    s_ev.inc(v.tensor_copy(
                        qx_fm[:, :, (tc - 1) * 128:tc * 128],
                        bass.AP(ps8.tensor if hasattr(ps8, "tensor") else ps8, 0,
                                [[4096, 128], [256, 16], [2, 128]])))
                v.wait_ge(s_ev.h, ev_w + 24 + tc + 1)
                for c in range(4):
                    v.bn_stats(stats[:, 6 * c:6 * c + 6], res1[:, tc, 512 * c:512 * (c + 1)])
                v.drain()
                v.bn_aggr(aggr[:, :], stats[:, :])
                v.drain()
                v.tensor_scalar_add(rstd[:, :], aggr[:, 1:2], EPS)
                v.drain()
                v.reciprocal(rstd[:, :], rstd[:, :])
                s_dv.inc(v.drain())
                v.wait_ge(s_ac.h, ac5 + 4 * tc + 1)
                v.tensor_scalar_mul(nmr[:, :], aggr[:, 0:1], -1.0)
                v.drain()
                v.tensor_mul(nmr[:, :], nmr[:, :], rstd[:, :])
                s_dv.inc(v.drain())
                v.wait_ge(s_ac.h, ac5 + 4 * tc + 2)
                v.reduce_max(rmx[:, tc:tc + 1], ln1f[:, 0, :], axis=AX.X,
                             apply_absolute_value=True)
                v.drain()
                v.reciprocal(rmx224[:, tc:tc + 1], rmx[:, tc:tc + 1])
                v.drain()
                v.tensor_scalar_mul(rmx224[:, tc:tc + 1], rmx224[:, tc:tc + 1], 224.0)
                s_dv.inc(v.drain())

        @b.scalar
        def _(s):
            for tc in range(NTC):
                s.wait_ge(s_dv.h, dv5 + 3 * tc + 1)
                s.activation(rstd[:, :], rstd[:, :], ACT.Sqrt)
                s_ac.inc(s.drain())
                s.wait_ge(s_dv.h, dv5 + 3 * tc + 2)
                if tc >= 2:
                    s.wait_ge(s_pe.h, pe5 + tc - 1)
                s.activation(ln1f[:, 0, :], res1[:, tc, :], ACT.Identity,
                             bias=nmr[:, 0:1], scale=rstd[:, 0:1])
                s_ac.inc(s.drain())
                if tc >= 2:
                    s.wait_ge(s_ln[tc % 2].h, 16 * ((tc - 2) // 2 + 1))
                s.activation(ln1_bf2[:, tc % 2, :], ln1f[:, 0, :], ACT.Copy)
                s_ac.inc(s.drain())
                s.wait_ge(s_dv.h, dv5 + 3 * tc + 3)
                s.activation(qx_tm[:, tc % 2, :], ln1f[:, 0, :], ACT.Copy,
                             scale=rmx224[:, tc:tc + 1])
                s_ac.inc(s.drain())

        @b.sync
        def _(sync):
            for tc in range(NTC):
                sync.wait_ge(s_ac.h, ac5 + 4 * tc + 3)
                s_ln[tc % 2].inc(sync.dma_start(out=ln1_d[tc * 128:(tc + 1) * 128, :],
                                                in_=ln1_bf2[:, tc % 2, :]), 16)

        @b.tensor
        def _(t):
            t.wait_ge(s_ev.h, ev5)
            for tc in range(NTC):
                t.wait_ge(s_ac.h, ac5 + 4 * tc + 4)
                if tc >= 1:
                    t.wait_ge(s_ev.h, ev5 + tc)
                ins = None
                for fc in range(NFC):
                    dst = bass.AP(ps8.tensor if hasattr(ps8, "tensor") else ps8,
                                  fc * 256, [[4096, 128], [2, 128]])
                    ins = t.transpose(dst, qx_tm[:, tc % 2, fc * 128:(fc + 1) * 128],
                                      ident8[:, :])
                s_pe.inc(ins)

        @b.vector
        def _(v):
            v.wait_ge(s_pe.h, pe5 + NTC)
            s_ev.inc(v.tensor_copy(
                qx_fm[:, :, (NTC - 1) * 128:NTC * 128],
                bass.AP(ps8.tensor if hasattr(ps8, "tensor") else ps8, 0,
                        [[4096, 128], [256, 16], [2, 128]])))

        @b.scalar
        def _(s):
            s.activation(identf[:, :], ident[:, :], ACT.Copy)
            s_ac.inc(s.drain())

        @b.vector
        def _(v):
            v.wait_ge(s_dv.h, dv5 + 3 * NTC)
            v.tensor_scalar_mul(rtmp[:, 0:NTC], rmx[:, 0:NTC], 4.0 / (448.0 * 448.0))
            s_dv.inc(v.drain())

        pe5b = s_pe.n
        dv5b = s_dv.n
        @b.tensor
        def _(t):
            t.wait_ge(s_dv.h, dv5b)
            t.wait_ge(s_ac.h, s_ac.n)
            for rnd in range(2):
                if rnd >= 1:
                    t.wait_ge(s_dv.h, dv5b + rnd)
                ins = None
                for bb in range(4):
                    ins = t.transpose(psY[0:1, bb * 128:(bb + 1) * 128],
                                      rtmp[:, rnd * 4 + bb:rnd * 4 + bb + 1], identf[:, :])
                s_pe.inc(ins)

        @b.vector
        def _(v):
            for rnd in range(2):
                v.wait_ge(s_pe.h, pe5b + rnd + 1)
                s_dv.inc(v.tensor_copy(row5[0:1, rnd * 512:(rnd + 1) * 512], psY[0:1, 0:512]))

        @b.sync
        def _(sync):
            sync.wait_ge(s_dv.h, dv5b + 2)
            s_rb5.inc(sync.dma_start(out=rowd5.rearrange("(o d) -> o d", o=1),
                                     in_=row5[0:1, 0:NT]), 16)
            sync.wait_ge(s_rb5.h, 16)
            s_rb5.inc(sync.dma_start(
                out=bc_rm[:, :], in_=bass.AP(rowd5.tensor, 0, [[0, 128], [1, NT]])), 16)

        if dbg:
            @b.sync
            def _(sync):
                sync.wait_ge(s_ac.h, s_ac.n)
                sync.wait_ge(s_ev.h, s_ev.n)
                for fc in range(NFC):
                    s_out.inc(sync.dma_start(out=io["dbg_qxf"][fc * 128:(fc + 1) * 128, :],
                                             in_=qx_fm[:, fc, :]), 16)
                sync.wait_ge(s_out.h, s_out.n)

    sc5.close()
    sc4.close()

    sc6t = contextlib.ExitStack()
    t_fm = sc6t.enter_context(nc.sbuf_tensor("t_fm", [128, NIC, NT], BF16, side="right"))
    Mabs = sc6t.enter_context(nc.sbuf_tensor("Mabs", [128, NT], BF16, side="right"))
    Mlo = sc6t.enter_context(nc.sbuf_tensor("Mlo", [128, NT], BF16, side="right"))
    sc6 = contextlib.ExitStack()
    gbuf = sc6.enter_context(nc.sbuf_tensor("gbuf", [128, NT], F32, side="right"))
    ubuf = sc6.enter_context(nc.sbuf_tensor("ubuf", [128, NT], F32, side="right"))
    psG = [sc6.enter_context(nc.psum_tensor(f"psG{j}", [128, 512], F32)) for j in range(8)]


    wrec = {"load": {}, "vec": {}, "sc": {}, "tr": {}, "act": {}, "cons": {}}
    wcur = {"load": 0, "loadu": 0, "vec": 0, "sc": 0, "act": 0}
    wsrc = {"g": wg, "u": wu}
    IGORD = {d: i for i, d in GORD.items()}
    PM_DR = mybir.MatmulPerfMode.DoubleRow

    def pfree(i):
        k, c, q = PIECES[i]
        return PF[q] if k == "dr" else 2048

    def wp_load(sync, upto):
        upto = min(upto, NP)
        while wcur["load"] < upto:
            i = wcur["load"]
            if i % 2 == 1:
                wcur["load"] += 1
                continue
            if i >= NR and (i - NR) not in wrec["sc"]:
                return
            if i == 0 and wp_guard["pe"] is not None:
                sync.wait_ge(s_pe.h, wp_guard["pe"])
                sync.wait_ge(s_ev.h, wp_guard["ev"])
                sync.wait_ge(s_dnimg.h, 16 * NFC)
                sync.wait_ge(s_dnl.h, 16 * NFC)
                sync.wait_ge(s_dnq.h, NFC)
                sync.wait_ge(s_dntr[0].h, s_dntr[0].n)
                sync.wait_ge(s_dntr[1].h, s_dntr[1].n)
            wcur["load"] += 1
            j = i % NR
            k, c, q = PIECES[i]
            if i >= NR:
                sync.wait_ge(s_cons[j].h, wrec["sc"][i - NR])
            wrec["load"][i] = s_wr[j].inc(sync.dma_start(
                out=wraw[j][:, :], in_=wsrc[k][c * 128:(c + 1) * 128, :]), 16)

    def wp_load_u(g, upto):
        upto = min(upto, NP, wcur["load"] + 1)
        while wcur["loadu"] < upto:
            i = wcur["loadu"]
            if i % 2 == 0:
                wcur["loadu"] += 1
                continue
            if i >= NR and (i - NR) not in wrec["sc"]:
                return
            if i == 1:
                g.wait_ge(s_dnimg.h, 16 * NFC)
                g.wait_ge(s_dnq.h, NFC)
                g.wait_ge(s_dnv.h, NFC)
            wcur["loadu"] += 1
            j = i % NR
            k, c, q = PIECES[i]
            if i >= NR:
                g.wait_ge(s_cons[j].h, wrec["sc"][i - NR])
            wrec["load"][i] = s_wr[j].inc(g.dma_start(
                out=wraw[j][:, :], in_=wsrc[k][c * 128:(c + 1) * 128, :]), 16)

    def wp_vec(v, upto):
        upto = min(upto, NP)
        while wcur["vec"] < upto:
            i = wcur["vec"]
            if i not in wrec["load"]:
                return
            wcur["vec"] += 1
            j = i % NR
            k, c, q = PIECES[i]
            f = pfree(i)
            v.wait_ge(s_wr[j].h, wrec["load"][i])
            if k == "dr":
                v.reduce_max(wp_rwp[:, q:q + 1], wraw[j][:, 0:f], axis=AX.X,
                             apply_absolute_value=True)
                if q == 2:
                    v.drain()
                    v.tensor_max(wp_rwp[:, 3:4], wp_rwp[:, 0:1], wp_rwp[:, 1:2])
                    v.drain()
                    v.tensor_max(rwd_keep[:, c:c + 1], wp_rwp[:, 3:4], wp_rwp[:, 2:3])
                wrec["vec"][i] = s_ws[j].inc(v.drain())
                continue
            d = GORD[i]
            v.reduce_max(wp_rwq[:, d:d + 1], wraw[j][:, 0:f], axis=AX.X,
                         apply_absolute_value=True)
            v.drain()
            v.reciprocal(wp_rwrec[:, d:d + 1], wp_rwq[:, d:d + 1])
            wrec["vec"][i] = s_ws[j].inc(v.drain())

    def wp_sct(s, upto):
        upto = min(upto, NP)
        while wcur["sc"] < upto:
            i = wcur["sc"]
            if PIECES[i][0] == "dr":
                wrec["tr"][i] = None
                wcur["sc"] += 1
                continue
            if i not in wrec["vec"]:
                return
            d = GORD[i]
            p3 = IGORD.get(d - NRT)
            if p3 is not None and p3 not in wrec["act"]:
                return
            wcur["sc"] += 1
            if i == 0:
                s.wait_ge(s_dnimg.h, 16 * NFC)
            j = i % NR
            js = d % NR
            jt = d % NRT
            p2 = IGORD.get(d - NR)
            if p2 is not None:
                s.wait_ge(s_wt[GORD[p2] % NRT].h, wrec["tr"][p2])
            s.wait_ge(s_ws[j].h, wrec["vec"][i])
            if p3 is not None:
                sem, val = wrec["act"][p3]
                s.wait_ge(sem.h, val)
            s.activation(wsc[js][:, :], wraw[j][:, :], ACT.Copy,
                         scale=wp_rwrec[:, GORD[i]:GORD[i] + 1])
            wrec["sc"][i] = s_cons[j].inc(s.drain())
            wrec["tr"][i] = s_wt[jt].inc(s.dma_start_transpose(
                wtrn[jt][:, :].rearrange("p (c j2) -> p c j2", j2=128),
                wsc[js][:, :].rearrange("p (c j2) -> p c j2", j2=128)), 16)

    def wp_act(v, upto):
        upto = min(upto, NP)
        while wcur["act"] < upto:
            i = wcur["act"]
            if PIECES[i][0] == "dr":
                wcur["act"] += 1
                continue
            if wrec["tr"].get(i) is None:
                return
            d = GORD[i]
            pq = IGORD.get(d - NQ)
            if pq is not None and pq not in wrec["cons"]:
                return
            wcur["act"] += 1
            if i == 0:
                v.wait_ge(s_dnimg.h, 16 * NFC)
            jt = d % NRT
            jq = d % NQ
            v.wait_ge(s_wt[jt].h, wrec["tr"][i])
            if pq is not None:
                sem, val = wrec["cons"][pq]
                v.wait_ge(sem.h, val)
            ins = v.tensor_scalar_mul(wq8[jq][:, :], wtrn[jt][:, :], 224.0)
            wrec["act"][i] = (s_w8[jq], s_w8[jq].inc(ins))

    GIDX = {}
    for i, (k, c, q) in enumerate(PIECES):
        if k in ("g", "u"):
            GIDX[(k, c)] = i

    if True:
        b = blk
        dv6 = s_dv.n
        ac6 = s_ac.n
        pe6 = s_pe.n

        @b.vector
        def _(v):
            v.wait_ge(s_ln[0].h, s_ln[0].n)
            v.wait_ge(s_ln[1].h, s_ln[1].n)
            v.wait_ge(s_rb5.h, 16)
            v.memset(Mabs[:, :], -1e30)
            v.memset(Mlo[:, :], 1e30)
            s_dv.inc(v.drain())
        dv6 += 1

        for jj in range(8):
            @b.sync
            def _(sync, jj=jj):
                wp_load(sync, 2 + 2 * jj)
            @b.gpsimd
            def _(g, jj=jj):
                wp_load_u(g, 2 + 2 * jj)
            @b.vector
            def _(v, jj=jj):
                for _ in range(3):
                    wp_vec(v, 1 + 2 * jj)
                    wp_act(v, 2 * jj)
            @b.scalar
            def _(s, jj=jj):
                wp_sct(s, 2 * jj)

        for m in range(NIC):
            ig = GIDX[("g", m)]
            iu = GIDX[("u", m)]

            @b.tensor
            def _(t, m=m, ig=ig, iu=iu):
                if m == 0:
                    t.wait_ge(s_ev.h, s_ev.n)
                    t.wait_ge(s_rb5.h, 32)
                for pidx, pi in ((0, ig), (1, iu)):
                    jq = GORD[pi] % NQ
                    sem, val = wrec["act"][pi]
                    t.wait_ge(sem.h, val)
                    if m >= 2 and pidx == 0:
                        t.wait_ge(s_dv.h, dv6 + 2 * (m - 2) + 1)
                    ins = None
                    for n in range(2):
                        for k in range(8):
                            ins = t.matmul(psG[4 * (m % 2) + 2 * pidx + n][:, :],
                                           wq8[jq][:, k * 256:(k + 1) * 256].rearrange(
                                               "p (r mm) -> p r mm", r=2),
                                           qx_fm[:, 2 * k:2 * k + 2, n * 512:(n + 1) * 512],
                                           start=(k == 0), stop=(k == 7),
                                           perf_mode=PM_DR)
                    s_pe.inc(ins)
                    wrec["cons"][pi] = (s_pe, s_pe.n)

            for rr in range(3):
                @b.sync
                def _(sync, m=m):
                    tgt = (GIDX[("u", m + 1)] + 1 + 4) if m + 1 < NIC else NP
                    wp_load(sync, tgt)

                @b.gpsimd
                def _(g, m=m):
                    tgt = (GIDX[("u", m + 1)] + 1 + 4) if m + 1 < NIC else NP
                    wp_load_u(g, tgt)

                @b.vector
                def _(v, m=m):
                    tgt = (GIDX[("u", m + 1)] + 1 + 2) if m + 1 < NIC else NP
                    wp_vec(v, tgt)
                    wp_act(v, tgt - 2)

                @b.scalar
                def _(s, m=m):
                    tgt = (GIDX[("u", m + 1)] + 1 + 1) if m + 1 < NIC else NP
                    wp_sct(s, tgt)

            @b.vector
            def _(v, m=m, ig=ig, iu=iu):
                v.wait_ge(s_pe.h, pe6 + 2 * m + 2)
                for pidx, pi, buf in ((0, ig, gbuf), (1, iu, ubuf)):
                    for n in range(2):
                        v.scalar_tensor_tensor(
                            buf[:, n * 512:(n + 1) * 512],
                            psG[4 * (m % 2) + 2 * pidx + n][:, :],
                            wp_rwq[:, GORD[pi]:GORD[pi] + 1], bc_rm[:, n * 512:(n + 1) * 512],
                            ALU.mult, ALU.mult)
                s_dv.inc(v.drain())
                v.wait_ge(s_ac.h, ac6 + m + 1)
                v.tensor_mul(t_fm[:, m, :], gbuf[:, :], ubuf[:, :])
                v.drain()
                v.tensor_max(Mabs[:, :], Mabs[:, :], t_fm[:, m, :])
                v.tensor_tensor(Mlo[:, :], Mlo[:, :], t_fm[:, m, :], ALU.min)
                s_dv.inc(v.drain())

            @b.scalar
            def _(s, m=m, ig=ig, iu=iu):
                s.wait_ge(s_dv.h, dv6 + 2 * m + 1)
                s_ac.inc(s.activation(gbuf[:, :], gbuf[:, :], ACT.Silu))

        for jj in range(6):
            @b.sync
            def _(sync, jj=jj):
                wp_load(sync, NP)
            @b.gpsimd
            def _(g, jj=jj):
                wp_load_u(g, NP)
            @b.vector
            def _(v, jj=jj):
                for _ in range(3):
                    wp_vec(v, NP)
                    wp_act(v, NP)
            @b.scalar
            def _(s, jj=jj):
                wp_sct(s, NP)

        if dbg:
            @b.sync
            def _(sync):
                sync.wait_ge(s_dv.h, s_dv.n)
                for mm in range(NIC):
                    s_out.inc(sync.dma_start(out=io["dbg_tf"][mm * 128:(mm + 1) * 128, :],
                                             in_=t_fm[:, mm, :]), 16)
                sync.wait_ge(s_out.h, s_out.n)

    sc6.close()
    sc5q.close()

    sc7 = contextlib.ExitStack()
    qt_fm = sc7.enter_context(nc.sbuf_tensor("qt_fm", [128, NIC, NT], FP8, side="left"))
    bc_rwd = sc7.enter_context(nc.sbuf_tensor("bc_rwd", [128, D], F32, side="left"))
    identf2 = sc7.enter_context(nc.sbuf_tensor("identf2", [128, 128], F32, side="left"))
    sc7b = contextlib.ExitStack()
    bc_qt = sc7b.enter_context(nc.sbuf_tensor("bc_qt", [128, NT], F32, side="right"))
    row7 = sc7b.enter_context(nc.sbuf_tensor("row7", [1, NT + D], F32, side="right"))
    psT7 = sc7b.enter_context(nc.psum_tensor("psT7", [128, 1024], BF16))
    psU7 = sc7b.enter_context(nc.psum_tensor("psU7", [128, 512], F32))

    if True:
        b = blk
        dv7 = s_dv.n
        pe7 = s_pe.n
        ev7 = s_ev.n

        @b.scalar
        def _(s):
            for j in range(3):
                s.wait_ge(s_wt[j].h, s_wt[j].n)
            s.activation(identf2[:, :], ident[:, :], ACT.Copy)
            s_ac.inc(s.drain())
        ac7 = s_ac.n

        @b.tensor
        def _(t):
            t.wait_ge(s_dv.h, dv7)
            ins = None
            for bb in range(NTC):
                ins = t.transpose(psT7[:, bb * 128:(bb + 1) * 128],
                                  Mabs[:, bb * 128:(bb + 1) * 128], ident[:, :])
            s_pe.inc(ins)
            t.wait_ge(s_dv.h, dv7 + 1)
            ins = None
            for bb in range(NTC):
                ins = t.transpose(psT7[:, bb * 128:(bb + 1) * 128],
                                  Mlo[:, bb * 128:(bb + 1) * 128], ident[:, :])
            s_pe.inc(ins)

        @b.vector
        def _(v):
            v.wait_ge(s_pe.h, pe7 + 1)
            for bb in range(NTC):
                v.reduce_max(rmt[:, bb:bb + 1], psT7[:, bb * 128:(bb + 1) * 128], axis=AX.X)
            s_dv.inc(v.drain())
            v.wait_ge(s_pe.h, pe7 + 2)
            for bb in range(NTC):
                v.tensor_reduce(rmx224[:, bb:bb + 1], psT7[:, bb * 128:(bb + 1) * 128],
                                axis=AX.X, op=ALU.min, negate=True)
            v.drain()
            v.tensor_max(rmt[:, :], rmt[:, :], rmx224[:, :])
            v.drain()
            v.tensor_scalar_mul(rmt4[:, :], rmt[:, :], 4.0 / 448.0)
            v.reciprocal(rmx224[:, :], rmt[:, :])
            v.drain()
            v.tensor_scalar_mul(rmx224[:, :], rmx224[:, :], 224.0)
            v.wait_ge(s_dnv.h, NFC)
            v.tensor_scalar_mul(rwdq[:, :], rwd_keep[:, :], 1.0 / 448.0)
            s_dv.inc(v.drain())

        pe7b = s_pe.n
        dv7b = s_dv.n
        @b.tensor
        def _(t):
            t.wait_ge(s_dv.h, dv7b)
            t.wait_ge(s_ac.h, ac7)
            rr = 0
            for rnd in range(2):
                if rr >= 1:
                    t.wait_ge(s_dv.h, dv7b + rr)
                ins = None
                for bb in range(4):
                    ins = t.transpose(psU7[0:1, bb * 128:(bb + 1) * 128],
                                      rmx224[:, rnd * 4 + bb:rnd * 4 + bb + 1], identf2[:, :])
                s_pe.inc(ins)
                rr += 1
            for rnd in range(4):
                t.wait_ge(s_dv.h, dv7b + rr)
                ins = None
                for bb in range(4):
                    ins = t.transpose(psU7[0:1, bb * 128:(bb + 1) * 128],
                                      rwdq[:, rnd * 4 + bb:rnd * 4 + bb + 1], identf2[:, :])
                s_pe.inc(ins)
                rr += 1

        @b.vector
        def _(v):
            for rr in range(6):
                v.wait_ge(s_pe.h, pe7b + rr + 1)
                s_dv.inc(v.tensor_copy(row7[0:1, rr * 512:(rr + 1) * 512], psU7[0:1, 0:512]))

        @b.sync
        def _(sync):
            sync.wait_ge(s_dv.h, dv7b + 6)
            s_rb7.inc(sync.dma_start(out=rowd7.rearrange("(o d) -> o d", o=1),
                                     in_=row7[0:1, 0:NT + D]), 16)
            sync.wait_ge(s_rb7.h, 16)
            s_rb7.inc(sync.dma_start(
                out=bc_qt[:, :], in_=bass.AP(rowd7.tensor, 0, [[0, 128], [1, NT]])), 16)
            s_rb7.inc(sync.dma_start(
                out=bc_rwd[:, :], in_=bass.AP(rowd7.tensor, NT, [[0, 128], [1, D]])), 16)

        dvq = s_dv.n
        qt_val = {}
        bch = bc_qt.tensor if hasattr(bc_qt, "tensor") else bc_qt
        bc4 = bass.AP(bch, 0, [[NT, 128], [0, 4], [1, NT]])
        for sq in range(NSQ):
            @b.vector
            def _(v, sq=sq):
                if sq == 0:
                    v.wait_ge(s_rb7.h, 48)
                s_dv.inc(v.tensor_tensor(qt_fm[:, 4 * sq:4 * sq + 4, :],
                                         t_fm[:, 4 * sq:4 * sq + 4, :], bc4, ALU.mult))
            qt_val[sq] = s_dv.n

        if dbg:
            @b.sync
            def _(sync):
                sync.wait_ge(s_dv.h, qt_val[NSQ - 1])
                for mm in range(NIC):
                    s_out.inc(sync.dma_start(out=io["dbg_qtf"][mm * 128:(mm + 1) * 128, :],
                                             in_=qt_fm[:, mm, :]), 16)
                sync.wait_ge(s_out.h, s_out.n)

    sc7b.close()
    sc6t.close()

    sc8 = contextlib.ExitStack()
    qwdT = sc8.enter_context(nc.sbuf_tensor("qwdT", [128, NIC, D], FP8, side="right"))
    res2 = sc8.enter_context(nc.sbuf_tensor("res2", [128, 2, D], F32, side="right"))
    lnst = sc8.enter_context(nc.sbuf_tensor("lnst", [128, 2, D], BF16, side="right"))
    psE = [sc8.enter_context(nc.psum_tensor(f"psE{j}", [128, 512], F32)) for j in range(8)]

    if True:
        b = blk
        pe8 = s_pe.n
        dv8 = s_dv.n
        ac8 = s_ac.n
        ev8 = s_ev.n
        qw_marks = {}

        @b.gpsimd
        def _(g):
            g.wait_ge(s_pe.h, pe8)
            g.wait_ge(s_rb7.h, 16)
            for tc in range(NTC):
                if tc >= 2:
                    g.wait_ge(s_dv.h, dv8 + 2 * (tc - 2) + 1)
                s_lnr[tc % 2].inc(g.dma_start(out=lnst[:, tc % 2, :],
                                              in_=ln1_d[tc * 128:(tc + 1) * 128, :]), 16)

        @b.sync
        def _(sync):
            sync.wait_ge(s_dnimg.h, 16 * NFC)
            for sq in range(NSQ):
                jj = sq % 2
                sync.wait_ge(s_dv.h, qt_val[sq])
                if sq >= 2:
                    sync.wait_ge(s_qw[jj].h, qw_marks[sq - 2][1])
                s_qw[jj].inc(sync.dma_start(
                    out=qwdT[:, 4 * sq:4 * sq + 4, :],
                    in_=qwd_img[sq * 512:(sq + 1) * 512, :].rearrange(
                        "(bk p) d -> p bk d", p=128)), 16)
                qw_marks[sq] = (jj, s_qw[jj].n)

        @b.tensor
        def _(t):
            t.wait_ge(s_dv.h, qt_val[0])
            for tg in range(4):
                if tg >= 1:
                    t.wait_ge(s_ev.h, ev8 + 8 * tg)
                ins = None
                for mi in range(0, NIC, 2):
                    if tg == 0 and (mi % 4) == 0:
                        jj, val = qw_marks[min(mi // 4, NSQ - 1)]
                        t.wait_ge(s_qw[jj].h, val)
                    for tci in range(2):
                        tc = 2 * tg + tci
                        for n in range(4):
                            ins = t.matmul(psE[4 * tci + n][:, :],
                                           qt_fm[:, mi:mi + 2, tc * 128:(tc + 1) * 128],
                                           qwdT[:, mi:mi + 2, n * 512:(n + 1) * 512],
                                           start=(mi == 0), stop=(mi == NIC - 2),
                                           perf_mode=PM_DR)
                s_pe.inc(ins)

        out_base = s_out.n

        @b.vector
        def _(v):
            for tg in range(4):
                v.wait_ge(s_pe.h, pe8 + tg + 1)
                for tci in range(2):
                    tc = 2 * tg + tci
                    if tg >= 1:
                        v.wait_ge(s_ost[tci].h, 16 * tg)
                    for n in range(4):
                        v.scalar_tensor_tensor(
                            res2[:, tci, n * 512:(n + 1) * 512], psE[4 * tci + n][:, :],
                            rmt4[:, tc:tc + 1], bc_rwd[:, n * 512:(n + 1) * 512],
                            ALU.mult, ALU.mult)
                    v.drain()
                    v.wait_ge(s_lnr[tc % 2].h, 16 * (tc // 2 + 1))
                    v.tensor_add(res2[:, tci, :], res2[:, tci, :], lnst[:, tc % 2, :])
                    v.drain()
                    for cc in range(4):
                        v.bn_stats(stats[:, 6 * cc:6 * cc + 6],
                                   res2[:, tci, 512 * cc:512 * (cc + 1)])
                    v.drain()
                    v.bn_aggr(aggr[:, :], stats[:, :])
                    v.drain()
                    if tc >= 1:
                        v.wait_ge(s_ac.h, ac8 + 2 * tc)
                    v.tensor_scalar_add(rstd[:, :], aggr[:, 1:2], EPS)
                    v.drain()
                    v.reciprocal(rstd[:, :], rstd[:, :])
                    s_dv.inc(v.drain())
                    v.wait_ge(s_ac.h, ac8 + 2 * tc + 1)
                    v.tensor_scalar_mul(nmr[:, :], aggr[:, 0:1], -1.0)
                    v.drain()
                    v.tensor_mul(nmr[:, :], nmr[:, :], rstd[:, :])
                    s_dv.inc(v.drain())
                for _ in range(8):
                    s_ev.inc(v.engine_nop())

        @b.scalar
        def _(s):
            for tg in range(4):
                for tci in range(2):
                    tc = 2 * tg + tci
                    s.wait_ge(s_dv.h, dv8 + 2 * tc + 1)
                    s.activation(rstd[:, :], rstd[:, :], ACT.Sqrt)
                    s_ac.inc(s.drain())
                    s.wait_ge(s_dv.h, dv8 + 2 * tc + 2)
                    s.activation(res2[:, tci, :], res2[:, tci, :], ACT.Identity,
                                 bias=nmr[:, 0:1], scale=rstd[:, 0:1])
                    s_ac.inc(s.drain())

        @b.sync
        def _(sync):
            for tg in range(4):
                for tci in range(2):
                    tc = 2 * tg + tci
                    sync.wait_ge(s_ac.h, ac8 + 2 * tc + 2)
                    s_ost[tci].inc(sync.dma_start(out=out[tc * 128:(tc + 1) * 128, :],
                                                  in_=res2[:, tci, :]), 16)
            if dbg:
                sync.wait_ge(s_qw[0].h, s_qw[0].n)
                sync.wait_ge(s_qw[1].h, s_qw[1].n)
                for mi in range(NIC):
                    s_out.inc(sync.dma_start(out=io["dbg_qwd"][mi * 128:(mi + 1) * 128, :],
                                             in_=qwdT[:, mi, :]), 16)
            sync.wait_ge(s_ost[0].h, s_ost[0].n)
            sync.wait_ge(s_ost[1].h, s_ost[1].n)
            sync.wait_ge(s_out.h, s_out.n)

    sc8.close()
    sc7.close()


def _host_prep(inputs):
    hs_full = np.ascontiguousarray(np.asarray(inputs["L0_attn_hidden_states"], np.float32)[0])
    rb = np.asarray(inputs["L0_attn_rel_bias"], np.float32)
    pos = np.arange(TW)
    rel = np.clip(pos[None, :] - pos[:, None], -128, 128) + 128
    bmat = np.ascontiguousarray(rb[rel])
    di_ = np.asarray(inputs["L0_ffn_down_reorder_index"])
    wdn = np.ascontiguousarray(np.asarray(inputs["L0_ffn_w_down"], np.float32)[:, np.argsort(di_)])
    import ml_dtypes
    com = dict(
        wc=np.ascontiguousarray(np.asarray(inputs["L0_attn_Wc"], np.float32)),
        wq=np.ascontiguousarray(np.asarray(inputs["L0_attn_Wq"], np.float32)),
        wk=np.ascontiguousarray(np.asarray(inputs["L0_attn_Wk"], np.float32)),
        wv=np.ascontiguousarray(np.asarray(inputs["L0_attn_Wv"], np.float32)),
        wd=np.ascontiguousarray(np.asarray(inputs["L0_attn_Wd"], np.float32)),
        bc=np.ascontiguousarray(np.asarray(inputs["L0_attn_bc"], np.float32)),
        bq=np.ascontiguousarray(np.asarray(inputs["L0_attn_bq"], np.float32)),
        bk=np.ascontiguousarray(np.asarray(inputs["L0_attn_bk"], np.float32)),
        bv=np.ascontiguousarray(np.asarray(inputs["L0_attn_bv"], np.float32)),
        bd=np.ascontiguousarray(np.asarray(inputs["L0_attn_bd"], np.float32)),
        bmat=bmat,
        idn=np.eye(128, dtype=ml_dtypes.bfloat16),
        idn8=np.eye(128, dtype=ml_dtypes.float8_e4m3),
        wg=np.ascontiguousarray(np.asarray(inputs["L0_ffn_w_gate"], np.float32)),
        wu=np.ascontiguousarray(np.asarray(inputs["L0_ffn_w_up"], np.float32)),
        wdn=wdn,
    )
    in_maps = []
    for c in range(8):
        m = dict(com)
        m["hs"] = np.ascontiguousarray(hs_full[c * NT:(c + 1) * NT])
        in_maps.append(m)
    return in_maps


def kernel(**inputs):
    nc = build_nc(dbg=False)
    in_maps = _host_prep(inputs)
    res = run_bass_kernel_spmd(nc, in_maps, core_ids=list(range(8)))
    outp = np.concatenate([np.asarray(r["out"]) for r in res.results], axis=0)
    return outp[None].astype(np.float32)


# version 25
# speedup vs baseline: 1.0468x; 1.0189x over previous
import contextlib
import numpy as np
import concourse.bass as bass
import concourse.mybir as mybir
from concourse.bass_utils import run_bass_kernel_spmd

F32 = mybir.dt.float32
BF16 = mybir.dt.bfloat16
FP8 = mybir.dt.float8e4
ALU = mybir.AluOpType
ACT = mybir.ActivationFunctionType
AX = mybir.AxisListType

NT, D, HC, NH, DH, TW = 1024, 2048, 1024, 16, 64, 512
NW = NT // TW
II = 5632
NIC, NFC, NKC, NTC = II // 128, D // 128, HC // 128, NT // 128
EPS = 1e-5

PIECES = [(k, m, 0) for m in range(NIC) for k in ("g", "u")]
NP = len(PIECES)
NSQ = 11


class Sem:
    def __init__(self, ctx, nc, name):
        self.h = ctx.enter_context(nc.semaphore(name))
        self.n = 0
    def inc(self, ins, k=1):
        ins.then_inc(self.h, k)
        self.n += k
        return self.n
    def wait(self, eng, val=None):
        eng.wait_ge(self.h, self.n if val is None else val)


def build_nc(dbg=False):
    nc = bass.Bass("TRN2", target_bir_lowering=False)
    io = {}
    def di(name, shape, dt=F32, kind="ExternalInput"):
        io[name] = nc.dram_tensor(name, shape, dt, kind=kind).ap()
    di("hs", [NT, D]); di("wc", [D, HC]); di("wq", [HC, HC]); di("wk", [HC, HC])
    di("wv", [HC, HC]); di("wd", [HC, D]); di("bc", [HC]); di("bq", [HC])
    di("bk", [HC]); di("bv", [HC]); di("bd", [D]); di("bmat", [TW, TW])
    di("idn", [128, 128], BF16); di("idn8", [128, 128], FP8)
    di("wg", [II, D]); di("wu", [II, D]); di("wdn", [D, II])
    di("out", [NT, D], F32, "ExternalOutput")
    if dbg:
        for nm, shp, dt in [("xc", [HC, NT], BF16), ("qf", [HC, NT], BF16),
                            ("kf", [HC, NT], BF16), ("vtm", [NT, HC], BF16),
                            ("ctx", [HC, NT], BF16), ("res1", [NT, D], F32),
                            ("x1", [NT, D], BF16), ("qxf", [D, NT], FP8),
                            ("tf", [II, NT], BF16), ("qtf", [II, NT], FP8),
                            ("qwd", [NIC * 128, D], FP8)]:
            di("dbg_" + nm, shp, dt, "ExternalOutput")
    with contextlib.ExitStack() as ctx:
        _build(nc, ctx, io, dbg)
    return nc


def _build(nc, ctx, io, dbg):
    E = ctx.enter_context
    hs, wc, wqd, wkd, wvd, wdd = io["hs"], io["wc"], io["wq"], io["wk"], io["wv"], io["wd"]
    bcd, bqd, bkd, bvd, bdd = io["bc"], io["bq"], io["bk"], io["bv"], io["bd"]
    bmat, idn, wg, wu, wdn, out = io["bmat"], io["idn"], io["wg"], io["wu"], io["wdn"], io["out"]
    idn8 = io["idn8"]

    ident = E(nc.sbuf_tensor("ident", [128, 128], BF16))
    ident8 = E(nc.sbuf_tensor("ident8", [128, 128], FP8))
    ones16 = E(nc.sbuf_tensor("ones16", [16, 128], F32))
    bc_sb = E(nc.sbuf_tensor("bc_sb", [128, NKC], F32))
    bq8_sb = E(nc.sbuf_tensor("bq8_sb", [128, NKC], F32))
    bk_sb = E(nc.sbuf_tensor("bk_sb", [128, NKC], F32))
    bv_sb = E(nc.sbuf_tensor("bv_sb", [128, NKC], F32))
    stats = E(nc.sbuf_tensor("stats", [128, 24], F32))
    aggr = E(nc.sbuf_tensor("aggr", [128, 2], F32))
    rstd = E(nc.sbuf_tensor("rstd", [128, 1], F32))
    nmr = E(nc.sbuf_tensor("nmr", [128, 1], F32))
    rmx = E(nc.sbuf_tensor("rmx", [128, NTC], F32))
    rmx224 = E(nc.sbuf_tensor("rmx224", [128, NTC], F32))
    rmt = E(nc.sbuf_tensor("rmt", [128, NTC], F32))
    rmt4 = E(nc.sbuf_tensor("rmt4", [128, NTC], F32))
    rwd_keep = E(nc.sbuf_tensor("rwd_keep", [128, NFC], F32))
    rwdq = E(nc.sbuf_tensor("rwdq", [128, NFC], F32))
    rwdrec = E(nc.sbuf_tensor("rwdrec", [128, NFC], F32))

    s_ld = Sem(ctx, nc, "s_ld")
    s_aw = {k: Sem(ctx, nc, f"s_aw_{k}") for k in ("c", "q", "k", "v", "d", "B")}
    s_tr = [Sem(ctx, nc, "s_tr0"), Sem(ctx, nc, "s_tr1")]
    s_pe = Sem(ctx, nc, "s_pe")
    s_dv = Sem(ctx, nc, "s_dv")
    s_ac = Sem(ctx, nc, "s_ac")
    s_ev = Sem(ctx, nc, "s_ev")
    s_ev2 = Sem(ctx, nc, "s_ev2")
    s_out = Sem(ctx, nc, "s_out")
    NR = 2
    NQ = 4
    s_wr = [Sem(ctx, nc, f"swr{j}") for j in range(NR)]
    s_ws = [Sem(ctx, nc, f"sws{j}") for j in range(NR)]
    s_wt = [Sem(ctx, nc, f"swt{j}") for j in range(4)]
    s_w8 = [Sem(ctx, nc, f"sw8{j}") for j in range(NQ)]
    s_cons = [Sem(ctx, nc, f"scons{j}") for j in range(NQ)]
    s_qd = [Sem(ctx, nc, f"sqd{j}") for j in range(2)]
    s_qw = [Sem(ctx, nc, "s_qwA"), Sem(ctx, nc, "s_qwB")]
    s_gp = Sem(ctx, nc, "s_gp")
    s_ost = [Sem(ctx, nc, "s_ostA"), Sem(ctx, nc, "s_ostB")]
    s_hs = [Sem(ctx, nc, "s_hsA"), Sem(ctx, nc, "s_hsB")]
    s_dnl = Sem(ctx, nc, "s_dnl")
    s_dnv = Sem(ctx, nc, "s_dnv")
    s_dnsc = Sem(ctx, nc, "s_dnsc")
    s_dntr = [Sem(ctx, nc, "s_dntrA"), Sem(ctx, nc, "s_dntrB")]
    s_dnq = Sem(ctx, nc, "s_dnq")
    s_dnimg = Sem(ctx, nc, "s_dnimg")
    s_wdn2 = [Sem(ctx, nc, "s_wdnA"), Sem(ctx, nc, "s_wdnB")]
    s_ln = [Sem(ctx, nc, "s_lnA"), Sem(ctx, nc, "s_lnB")]
    s_lnr = [Sem(ctx, nc, "s_lnrA"), Sem(ctx, nc, "s_lnrB")]
    s_rb5 = Sem(ctx, nc, "s_rb5")
    s_rb7 = Sem(ctx, nc, "s_rb7")

    ln1_d = nc.dram_tensor("ln1_d", [NT, D], BF16).ap()
    rowd5 = nc.dram_tensor("rowd5", [NT], F32).ap()
    rowd7 = nc.dram_tensor("rowd7", [NT + D], F32).ap()
    qwd_img = nc.dram_tensor("qwd_img", [II, D], FP8).ap()

    blk = ctx.enter_context(nc.Block())

    sc_hs = contextlib.ExitStack()
    hs_tm = sc_hs.enter_context(nc.sbuf_tensor("hs_tm", [128, NTC, D], BF16, side="left"))
    sc_dn = contextlib.ExitStack()
    dn_raw = sc_dn.enter_context(nc.sbuf_tensor("dn_raw", [128, II], F32, side="left"))
    dn_sc = sc_dn.enter_context(nc.sbuf_tensor("dn_sc", [128, II // 2], BF16, side="left"))
    dn_tr = sc_dn.enter_context(nc.sbuf_tensor("dn_tr", [128, 2, II // 2], BF16, side="left",
                                               is_transpose_dest=True))
    dn_st = sc_dn.enter_context(nc.sbuf_tensor("dn_st", [128, II], FP8, side="left"))
    sc_xc = contextlib.ExitStack()
    xc_fm = sc_xc.enter_context(nc.sbuf_tensor("xc_fm", [128, NKC, NT], BF16, side="right"))
    sc01 = contextlib.ExitStack()
    hs_fm = sc01.enter_context(nc.sbuf_tensor("hs_fm", [128, NFC, NT], BF16, side="right"))
    wc_sb = sc01.enter_context(nc.sbuf_tensor("wc_sb", [128, NFC, HC], BF16, side="right"))
    sc00 = contextlib.ExitStack()
    hs_st = sc00.enter_context(nc.sbuf_tensor("hs_st", [128, 2, D], F32, side="right"))
    hs_bf = sc00.enter_context(nc.sbuf_tensor("hs_bf", [128, 2, D], BF16, side="right"))
    bd_bc = sc00.enter_context(nc.sbuf_tensor("bd_bc", [128, D], F32, side="right"))
    scps0 = contextlib.ExitStack()
    psA = scps0.enter_context(nc.psum_tensor("psA", [128, 512], F32))
    psB = scps0.enter_context(nc.psum_tensor("psB", [128, 512], F32))
    ps = [psA, psB]

    HF = II // 2
    dncur = {"load": 0, "vec": 0, "sc": 0, "q8": 0, "img": 0}
    dnrec = {"tr": {}}

    def dn_load(g, upto):
        while dncur["load"] < min(upto, NFC):
            c = dncur["load"]
            if c >= 1 and dncur["sc"] < 2 * c:
                return
            dncur["load"] += 1
            if c >= 1:
                g.wait_ge(s_dnsc.h, 2 * c)
            s_dnl.inc(g.dma_start(out=dn_raw[:, :], in_=wdn[c * 128:(c + 1) * 128, :]), 16)

    def dn_vec(v, upto):
        while dncur["vec"] < min(upto, NFC, dncur["load"]):
            c = dncur["vec"]; dncur["vec"] += 1
            v.wait_ge(s_dnl.h, 16 * (c + 1))
            v.reduce_max(rwd_keep[:, c:c + 1], dn_raw[:, :], axis=AX.X,
                         apply_absolute_value=True)
            v.drain()
            v.reciprocal(rwdrec[:, c:c + 1], rwd_keep[:, c:c + 1])
            s_dnv.inc(v.drain())

    def dn_sct(s, upto):
        while dncur["sc"] < min(upto, 2 * NFC, 2 * dncur["vec"]):
            ih = dncur["sc"]
            c, h = ih // 2, ih % 2
            if ih >= 2 and dncur["q8"] < ih // 2:
                return
            dncur["sc"] += 1
            s.wait_ge(s_dnv.h, c + 1)
            if ih >= 1:
                s.wait_ge(s_dntr[(ih - 1) % 2].h, dnrec["tr"][ih - 1])
            if ih >= 2:
                s.wait_ge(s_dnq.h, ih // 2)
            s.activation(dn_sc[:, :], dn_raw[:, h * HF:(h + 1) * HF], ACT.Copy,
                         scale=rwdrec[:, c:c + 1])
            s_dnsc.inc(s.drain())
            dnrec["tr"][ih] = s_dntr[ih % 2].inc(s.dma_start_transpose(
                dn_tr[:, ih % 2, :].rearrange("p (b j) -> p b j", j=128),
                dn_sc[:, :].rearrange("p (b j) -> p b j", j=128)), 16)

    def dn_q8(v, upto):
        while dncur["q8"] < min(upto, NFC) and dncur["sc"] >= 2 * (dncur["q8"] + 1):
            c = dncur["q8"]; dncur["q8"] += 1
            v.wait_ge(s_dntr[0].h, dnrec["tr"][2 * c])
            v.wait_ge(s_dntr[1].h, dnrec["tr"][2 * c + 1])
            if c >= 1:
                v.wait_ge(s_dnimg.h, 16 * c)
            s_dnq.inc(v.tensor_scalar_mul(dn_st[:, :], dn_tr[:, :, :], 224.0))

    def dn_img(g, upto):
        while dncur["img"] < min(upto, NFC, dncur["q8"]):
            c = dncur["img"]; dncur["img"] += 1
            g.wait_ge(s_dnq.h, c + 1)
            s_dnimg.inc(g.dma_start(
                out=bass.AP(qwd_img.tensor, c * 128, [[D, 128], [128 * D, NIC], [1, 128]]),
                in_=dn_st[:, :].rearrange("p (b j) -> p b j", j=128)), 16)

    def dn_round(b, n):
        for _ in range(n):
            @b.gpsimd
            def _(g):
                dn_load(g, NFC)
                dn_img(g, NFC)
            @b.vector
            def _(v):
                dn_vec(v, NFC)
                dn_q8(v, NFC)
            @b.scalar
            def _(s):
                dn_sct(s, 2 * NFC)

    if True:
        b = blk
        ac0 = s_ac.n
        dv0 = s_dv.n
        @b.sync
        def _(sync):
            s_ld.inc(sync.dma_start(out=ident[:, :], in_=idn[:, :]), 16)
            s_ld.inc(sync.dma_start(out=ident8[:, :], in_=idn8[:, :]), 16)
            for dst, src in ((bc_sb, bcd), (bq8_sb, bqd), (bk_sb, bkd), (bv_sb, bvd)):
                s_ld.inc(sync.dma_start(out=dst[:, :], in_=src.rearrange("(c p) -> p c", p=128),
                                        allow_slow_non_contiguous=True), 16)
            s_ld.inc(sync.dma_start(out=bd_bc[:, :],
                                    in_=bass.AP(bdd.tensor, 0, [[0, 128], [1, D]])), 16)
            for tc in range(NTC):
                if tc >= 2:
                    sync.wait_ge(s_ac.h, ac0 + 2 * (tc - 2) + 2)
                    sync.wait_ge(s_dv.h, dv0 + 1 + (tc - 2) + 1)
                s_hs[tc % 2].inc(sync.dma_start(out=hs_st[:, tc % 2, :],
                                                in_=hs[tc * 128:(tc + 1) * 128, :]), 16)

        @b.gpsimd
        def _(g):
            s_aw["c"].inc(g.dma_start(out=wc_sb[:, :, :],
                          in_=wc.rearrange("(c p) j -> p c j", p=128)), 16)
            s_gp.inc(g.memset(ones16[:, :], 1.0))

        @b.scalar
        def _(s):
            for tc in range(NTC):
                s.wait_ge(s_hs[tc % 2].h, 16 * (tc // 2 + 1))
                if tc >= 2:
                    s.wait_ge(s_tr[tc % 2].h, 16 * ((tc - 2) // 2 + 1))
                s_ac.inc(s.activation(hs_bf[:, tc % 2, :], hs_st[:, tc % 2, :], ACT.Copy))
                s_ac.inc(s.drain())
                s_tr[tc % 2].inc(s.dma_start_transpose(
                    hs_fm[:, :, tc * 128:(tc + 1) * 128],
                    hs_bf[:, tc % 2, :].rearrange("p (c j) -> p c j", j=128)), 16)

        @b.vector
        def _(v):
            s_ld.wait(v)
            v.tensor_scalar_mul(bq8_sb[:, :], bq8_sb[:, :], 0.125)
            s_dv.inc(v.drain())
            for tc in range(NTC):
                v.wait_ge(s_hs[tc % 2].h, 16 * (tc // 2 + 1))
                s_dv.inc(v.scalar_tensor_tensor(hs_tm[:, tc, :], hs_st[:, tc % 2, :], 1.0,
                                                bd_bc[:, :], ALU.mult, ALU.add))

    dn_round(blk, 4)
    sc00.close()

    if True:
        b = blk
        pe1 = s_pe.n
        e21 = s_ev2.n
        @b.tensor
        def _(t):
            s_aw["c"].wait(t)
            s_tr[0].wait(t)
            s_tr[1].wait(t)
            t.wait_ge(s_ev.h, s_ev.n)
            i = 0
            for m in range(NKC):
                for n in range(2):
                    if i >= 2:
                        t.wait_ge(s_ev2.h, e21 + i - 1)
                    ins = None
                    for k in range(NFC):
                        ins = t.matmul(ps[i % 2][:, :], wc_sb[:, k, m * 128:(m + 1) * 128],
                                       hs_fm[:, k, n * 512:(n + 1) * 512],
                                       start=(k == 0), stop=(k == NFC - 1))
                    s_pe.inc(ins)
                    i += 1

        @b.scalar
        def _(s):
            i = 0
            for m in range(NKC):
                for n in range(2):
                    s_pe.wait(s, pe1 + i + 1)
                    s_ev2.inc(s.activation(xc_fm[:, m, n * 512:(n + 1) * 512], ps[i % 2][:, :],
                                           ACT.Identity, bias=bc_sb[:, m:m + 1]))
                    i += 1
        if dbg:
            @b.sync
            def _(sync):
                sync.wait_ge(s_ev2.h, s_ev2.n)
                for m in range(NKC):
                    s_out.inc(sync.dma_start(out=io["dbg_xc"][m * 128:(m + 1) * 128, :],
                                             in_=xc_fm[:, m, :]), 16)
                sync.wait_ge(s_out.h, s_out.n)

    dn_round(blk, 4)
    sc01.close()

    sc_ctx = contextlib.ExitStack()
    ctx_fm = sc_ctx.enter_context(nc.sbuf_tensor("ctx_fm", [128, NKC, NT], BF16, side="left"))
    sc2 = contextlib.ExitStack()
    q_fm = sc2.enter_context(nc.sbuf_tensor("q_fm", [128, NKC, NT], BF16, side="right"))
    k_fm = sc2.enter_context(nc.sbuf_tensor("k_fm", [128, NKC, NT], BF16, side="right"))
    v_tm = sc2.enter_context(nc.sbuf_tensor("v_tm", [128, NTC, HC], BF16, side="right"))
    sc2b = contextlib.ExitStack()
    v_fm = sc2b.enter_context(nc.sbuf_tensor("v_fm", [128, NKC, NT], BF16, side="right"))
    wx_sb = sc2b.enter_context(nc.sbuf_tensor("wx_sb", [128, 2, NKC, HC], BF16, side="right"))
    psT16 = sc2b.enter_context(nc.psum_tensor("psT16", [128, 1024], BF16))

    if True:
        b = blk
        pe2 = s_pe.n
        e22 = s_ev2.n
        @b.gpsimd
        def _(g):
            g.wait_ge(s_tr[0].h, s_tr[0].n)
            g.wait_ge(s_tr[1].h, s_tr[1].n)
            g.wait_ge(s_dv.h, s_dv.n)
            g.wait_ge(s_ac.h, s_ac.n)
            s_aw["q"].inc(g.dma_start(out=wx_sb[:, 0, :, :],
                          in_=wqd.rearrange("(c p) j -> p c j", p=128)), 16)
            s_aw["k"].inc(g.dma_start(out=wx_sb[:, 1, :, :],
                          in_=wkd.rearrange("(c p) j -> p c j", p=128)), 16)
            g.wait_ge(s_pe.h, pe2 + 16)
            s_aw["v"].inc(g.dma_start(out=wx_sb[:, 0, :, :],
                          in_=wvd.rearrange("(c p) j -> p c j", p=128)), 16)

        dsts = None
        @b.tensor
        def _(t):
            t.wait_ge(s_ev2.h, e22)
            projs = [(0, "q"), (1, "k"), (0, "v")]
            i = 0
            for pi, (wslot, awk) in enumerate(projs):
                t.wait_ge(s_aw[awk].h, 16)
                for m in range(NKC):
                    for n in range(2):
                        if i >= 2:
                            t.wait_ge(s_ev2.h, e22 + i - 1)
                        ins = None
                        for k in range(NKC):
                            ins = t.matmul(ps[i % 2][:, :],
                                           wx_sb[:, wslot, k, m * 128:(m + 1) * 128],
                                           xc_fm[:, k, n * 512:(n + 1) * 512],
                                           start=(k == 0), stop=(k == NKC - 1))
                        s_pe.inc(ins)
                        i += 1

        @b.scalar
        def _(s):
            projs2 = [(q_fm, bq8_sb, 0.125), (k_fm, bk_sb, 1.0), (v_fm, bv_sb, 1.0)]
            i = 0
            for dst, bias, scale in projs2:
                for m in range(NKC):
                    for n in range(2):
                        s_pe.wait(s, pe2 + i + 1)
                        s_ev2.inc(s.activation(dst[:, m, n * 512:(n + 1) * 512],
                                               ps[i % 2][:, :], ACT.Identity,
                                               bias=bias[:, m:m + 1], scale=scale))
                        i += 1

        pe_vt = s_pe.n
        ev_vt = s_ev.n
        @b.tensor
        def _(t):
            t.wait_ge(s_ev2.h, e22 + 48)
            for tc in range(NTC):
                if tc >= 1:
                    t.wait_ge(s_ev.h, ev_vt + tc)
                ins = None
                for m in range(NKC):
                    ins = t.transpose(psT16[:, m * 128:(m + 1) * 128],
                                      v_fm[:, m, tc * 128:(tc + 1) * 128], ident[:, :])
                s_pe.inc(ins)

        @b.vector
        def _(v):
            for tc in range(NTC):
                s_pe.wait(v, pe_vt + tc + 1)
                s_ev.inc(v.tensor_copy(v_tm[:, tc, :], psT16[:, :]))

        if dbg:
            @b.sync
            def _(sync):
                sync.wait_ge(s_ev2.h, s_ev2.n)
                sync.wait_ge(s_ev.h, s_ev.n)
                for m in range(NKC):
                    s_out.inc(sync.dma_start(out=io["dbg_qf"][m * 128:(m + 1) * 128, :],
                                             in_=q_fm[:, m, :]), 16)
                    s_out.inc(sync.dma_start(out=io["dbg_kf"][m * 128:(m + 1) * 128, :],
                                             in_=k_fm[:, m, :]), 16)
                for tc in range(NTC):
                    s_out.inc(sync.dma_start(out=io["dbg_vtm"][tc * 128:(tc + 1) * 128, :],
                                             in_=v_tm[:, tc, :]), 16)
                sync.wait_ge(s_out.h, s_out.n)

    dn_round(blk, 5)
    sc2b.close()
    scps0.close()

    scps34 = contextlib.ExitStack()
    psC = scps34.enter_context(nc.psum_tensor("psC", [128, 512], F32))
    psD = scps34.enter_context(nc.psum_tensor("psD", [128, 512], F32))
    sc3 = contextlib.ExitStack()
    Bsb = sc3.enter_context(nc.sbuf_tensor("Bsb", [128, 4, TW], BF16, side="right"))
    sneg = sc3.enter_context(nc.sbuf_tensor("sneg", [128, 4, TW], F32, side="right"))
    pexp = sc3.enter_context(nc.sbuf_tensor("pexp", [128, 4, TW], BF16, side="right"))
    probsT = sc3.enter_context(nc.sbuf_tensor("probsT", [128, 4, TW], BF16, side="right"))
    m4 = sc3.enter_context(nc.sbuf_tensor("m4", [128, 4], F32, side="right"))
    den4 = sc3.enter_context(nc.sbuf_tensor("den4", [128, 4], F32, side="right"))
    rc4 = sc3.enter_context(nc.sbuf_tensor("rc4", [128, 4], F32, side="right"))
    psC2 = sc3.enter_context(nc.psum_tensor("psC2", [128, 512], F32))
    psD2 = sc3.enter_context(nc.psum_tensor("psD2", [128, 512], F32))
    psPT = sc3.enter_context(nc.psum_tensor("psPT", [128, 1024], BF16))
    psPT2 = sc3.enter_context(nc.psum_tensor("psPT2", [128, 1024], BF16))
    psX = sc3.enter_context(nc.psum_tensor("psX", [128, 512], F32))
    psc = [psC, psD, psC2, psD2]
    pn = pexp

    WH = [(w, h) for w in range(NW) for h in range(NH)]

    def q_ap(w, h, qc):
        base = (h % 2) * 64
        return q_fm[base:base + 64, h // 2, w * TW + qc * 128: w * TW + (qc + 1) * 128]

    def k_ap(w, h):
        base = (h % 2) * 64
        return k_fm[base:base + 64, h // 2, w * TW:(w + 1) * TW]

    def ptbank(kc):
        return (psPT if kc < 2 else psPT2), (kc % 2) * 512

    NWH = len(WH)

    if True:
        b = blk
        pe_s = s_pe.n
        dv_s = s_dv.n
        ac_s = s_ac.n

        def pe_sc_val(i, qc):
            return pe_s + (qc + 1 if i == 0 else 4 + 6 * (i - 1) + qc + 1)

        def pe_tr_val(i):
            return pe_s + 4 + 6 * i + (4 if i + 1 < NWH else 0) + 1

        def pe_cx_val(i):
            return pe_tr_val(i) + 1

        @b.gpsimd
        def _(g):
            g.wait_ge(s_pe.h, pe_s)
            s_aw["B"].inc(g.dma_start(out=Bsb[:, :, :],
                          in_=bmat.rearrange("(c p) k -> p c k", p=128)), 16)

        @b.tensor
        def _(t):
            s_aw["B"].wait(t)
            t.wait_ge(s_ev.h, ev_vt + 8)
            w0, h0 = WH[0]
            for qc in range(4):
                s_pe.inc(t.matmul(psc[qc][:, :], q_ap(w0, h0, qc), k_ap(w0, h0),
                                  start=True, stop=True))
            for i in range(len(WH)):
                if i + 1 < len(WH):
                    w2, h2 = WH[i + 1]
                    t.wait_ge(s_dv.h, dv_s + 2 * i + 1)
                    for qc in range(4):
                        s_pe.inc(t.matmul(psc[qc][:, :], q_ap(w2, h2, qc), k_ap(w2, h2),
                                          start=True, stop=True))
                w, h = WH[i]
                t.wait_ge(s_dv.h, dv_s + 2 * i + 2)
                if i >= 1:
                    t.wait_ge(s_ac.h, ac_s + 3 * i - 1)
                ins = None
                for kc in range(4):
                    bank, off = ptbank(kc)
                    for qc in range(4):
                        ins = t.transpose(bank[:, off + qc * 128:off + (qc + 1) * 128],
                                          pn[:, qc, kc * 128:(kc + 1) * 128], ident[:, :])
                s_pe.inc(ins)
                t.wait_ge(s_ac.h, ac_s + 3 * i + 2)
                for kc in range(4):
                    ins = t.matmul(psX[0:64, :], v_tm[:, w * 4 + kc, h * 64:(h + 1) * 64],
                                   probsT[:, kc, :], start=(kc == 0), stop=(kc == 3))
                s_pe.inc(ins)

        @b.vector
        def _(v):
            for i in range(len(WH)):
                for qc in range(4):
                    v.wait_ge(s_pe.h, pe_sc_val(i, qc))
                    v.tensor_add(sneg[:, qc, :], psc[qc][:, :], Bsb[:, qc, :])
                v.drain()
                for qc in range(4):
                    v.reduce_max(m4[:, qc:qc + 1], sneg[:, qc, :], axis=AX.X, negate=True)
                s_dv.inc(v.drain())
                v.wait_ge(s_ac.h, ac_s + 3 * i + 1)
                if i >= 1:
                    v.wait_ge(s_pe.h, pe_tr_val(i - 1))
                v.reciprocal(rc4[:, :], den4[:, :])
                v.drain()
                for qc in range(4):
                    v.tensor_scalar_mul(pn[:, qc, :], pexp[:, qc, :], rc4[:, qc:qc + 1])
                s_dv.inc(v.drain())

        @b.scalar
        def _(s):
            for i in range(len(WH)):
                s.wait_ge(s_dv.h, dv_s + 2 * i + 1)
                if i >= 1:
                    s.wait_ge(s_pe.h, pe_tr_val(i - 1))
                for qc in range(4):
                    s.activation(pexp[:, qc, :], sneg[:, qc, :], ACT.Exp,
                                 bias=m4[:, qc:qc + 1], scale=1.0,
                                 accum_out=den4[:, qc:qc + 1])
                s_ac.inc(s.drain())
                s.wait_ge(s_pe.h, pe_tr_val(i))
                for kc in range(4):
                    bank, off = ptbank(kc)
                    s.activation(probsT[:, kc, :], bank[:, off:off + 512], ACT.Copy)
                s_ac.inc(s.drain())
                s.wait_ge(s_pe.h, pe_cx_val(i))
                w, h = WH[i]
                base = (h % 2) * 64
                s_ac.inc(s.activation(ctx_fm[base:base + 64, h // 2, w * TW:(w + 1) * TW],
                                      psX[0:64, :], ACT.Copy))

        if dbg:
            @b.sync
            def _(sync):
                sync.wait_ge(s_ac.h, s_ac.n)
                for m in range(NKC):
                    s_out.inc(sync.dma_start(out=io["dbg_ctx"][m * 128:(m + 1) * 128, :],
                                             in_=ctx_fm[:, m, :]), 16)
                sync.wait_ge(s_out.h, s_out.n)

    dn_round(blk, 3)
    sc3.close()
    sc2.close()
    sc_xc.close()

    sc4 = contextlib.ExitStack()
    res1 = sc4.enter_context(nc.sbuf_tensor("res1_tm", [128, NTC, D], F32, side="right"))
    sc4w = contextlib.ExitStack()
    wd_n = sc4w.enter_context(nc.sbuf_tensor("wd_n", [128, 2, NKC, 512], BF16, side="right"))
    ps2 = [psC, psD]

    if True:
        b = blk
        pe_w = s_pe.n
        ev_w = s_ev.n
        @b.gpsimd
        def _(g):
            for n in range(4):
                if n == 0:
                    g.wait_ge(s_dv.h, s_dv.n)
                    g.wait_ge(s_ac.h, s_ac.n)
                    g.wait_ge(s_pe.h, pe_w)
                if n >= 2:
                    g.wait_ge(s_pe.h, pe_w + 8 * (n - 1))
                s_wdn2[n % 2].inc(g.dma_start(
                    out=wd_n[:, n % 2, :, :],
                    in_=wdd.rearrange("(c p) j -> p c j", p=128)[:, :, n * 512:(n + 1) * 512]), 16)

        @b.tensor
        def _(t):
            t.wait_ge(s_ac.h, s_ac.n)
            i = 0
            for n in range(4):
                t.wait_ge(s_wdn2[n % 2].h, 16 * (n // 2 + 1))
                for tc in range(NTC):
                    if i >= 2:
                        t.wait_ge(s_ev.h, ev_w + i - 1)
                    ins = None
                    for k in range(NKC):
                        ins = t.matmul(ps2[i % 2][:, :], ctx_fm[:, k, tc * 128:(tc + 1) * 128],
                                       wd_n[:, n % 2, k, :],
                                       start=(k == 0), stop=(k == NKC - 1))
                    s_pe.inc(ins)
                    i += 1

        @b.vector
        def _(v):
            i = 0
            for n in range(4):
                for tc in range(NTC):
                    v.wait_ge(s_pe.h, pe_w + i + 1)
                    s_ev.inc(v.scalar_tensor_tensor(
                        res1[:, tc, n * 512:(n + 1) * 512], ps2[i % 2][:, :], 1.0,
                        hs_tm[:, tc, n * 512:(n + 1) * 512], ALU.mult, ALU.add))
                    i += 1
        if dbg:
            @b.sync
            def _(sync):
                sync.wait_ge(s_ev.h, s_ev.n)
                for tc in range(NTC):
                    s_out.inc(sync.dma_start(out=io["dbg_res1"][tc * 128:(tc + 1) * 128, :],
                                             in_=res1[:, tc, :]), 16)
                sync.wait_ge(s_out.h, s_out.n)

    dn_round(blk, 6)
    assert dncur["img"] == NFC, f"dn prepass incomplete: {dncur}"
    sc4w.close()
    sc_ctx.close()
    sc_dn.close()
    sc_hs.close()
    scps34.close()

    sc5q = contextlib.ExitStack()
    qx_fm = sc5q.enter_context(nc.sbuf_tensor("qx_fm", [128, NFC, NT], FP8, side="left"))
    bc_rm = sc5q.enter_context(nc.sbuf_tensor("bc_rm", [128, NT], F32, side="left"))
    NRT = 4
    wraw = [sc5q.enter_context(nc.sbuf_tensor(f"wp_raw{j}", [128, 2048], F32, side="left"))
            for j in range(NR)]
    wsc = [sc5q.enter_context(nc.sbuf_tensor(f"wp_sc{j}", [128, 2048], BF16, side="left"))
           for j in range(NR)]
    wtrn = [sc5q.enter_context(nc.sbuf_tensor(f"wp_tr{j}", [128, 2048], BF16, side="left",
                                              is_transpose_dest=True)) for j in range(NRT)]
    wq8 = [sc5q.enter_context(nc.sbuf_tensor(f"wp_q8{j}", [128, 2048], FP8, side="left"))
           for j in range(NQ)]
    wp_rw = sc5q.enter_context(nc.sbuf_tensor("wp_rw", [128, NR], F32, side="left"))
    wp_rwp = sc5q.enter_context(nc.sbuf_tensor("wp_rwp", [128, 4], F32, side="left"))
    wp_rwrec = sc5q.enter_context(nc.sbuf_tensor("wp_rwrec", [128, 2 * NIC], F32, side="left"))
    wp_rwq = sc5q.enter_context(nc.sbuf_tensor("wp_rwq", [128, 2 * NIC], F32, side="left"))
    GORD = {}
    _gi = 0
    for _i, (_k, _c, _q) in enumerate(PIECES):
        if _k in ("g", "u"):
            GORD[_i] = _gi
            _gi += 1
    wp_guard = {"pe": None, "ev": None}
    sc5 = contextlib.ExitStack()
    ln1f = sc5.enter_context(nc.sbuf_tensor("ln1f", [128, 1, D], F32, side="right"))
    ln1_bf2 = sc5.enter_context(nc.sbuf_tensor("ln1_bf2", [128, 2, D], BF16, side="right"))
    qx_tm = sc5.enter_context(nc.sbuf_tensor("qx_tm", [128, 2, D], FP8, side="right"))
    identf = sc5.enter_context(nc.sbuf_tensor("identf", [128, 128], F32, side="right"))
    rtmp = sc5.enter_context(nc.sbuf_tensor("rtmp", [128, NTC], F32, side="right"))
    row5 = sc5.enter_context(nc.sbuf_tensor("row5", [1, NT], F32, side="right"))
    ps8 = sc5.enter_context(nc.psum_tensor("ps8", [128, 4096], FP8))
    psY = sc5.enter_context(nc.psum_tensor("psY", [128, 512], F32))
    psZ = sc5.enter_context(nc.psum_tensor("psZ", [128, 512], F32))

    if True:
        b = blk
        dv5 = s_dv.n
        ac5 = s_ac.n
        pe5 = s_pe.n
        ev5 = s_ev.n
        wp_guard["pe"] = pe5
        wp_guard["ev"] = ev5

        @b.vector
        def _(v):
            for tc in range(NTC):
                if tc >= 1:
                    v.wait_ge(s_pe.h, pe5 + tc)
                    s_ev.inc(v.tensor_copy(
                        qx_fm[:, :, (tc - 1) * 128:tc * 128],
                        bass.AP(ps8.tensor if hasattr(ps8, "tensor") else ps8, 0,
                                [[4096, 128], [256, 16], [2, 128]])))
                v.wait_ge(s_ev.h, ev_w + 24 + tc + 1)
                for c in range(4):
                    v.bn_stats(stats[:, 6 * c:6 * c + 6], res1[:, tc, 512 * c:512 * (c + 1)])
                v.drain()
                v.bn_aggr(aggr[:, :], stats[:, :])
                v.drain()
                v.tensor_scalar_add(rstd[:, :], aggr[:, 1:2], EPS)
                v.drain()
                v.reciprocal(rstd[:, :], rstd[:, :])
                s_dv.inc(v.drain())
                v.wait_ge(s_ac.h, ac5 + 4 * tc + 1)
                v.tensor_scalar_mul(nmr[:, :], aggr[:, 0:1], -1.0)
                v.drain()
                v.tensor_mul(nmr[:, :], nmr[:, :], rstd[:, :])
                s_dv.inc(v.drain())
                v.wait_ge(s_ac.h, ac5 + 4 * tc + 2)
                v.reduce_max(rmx[:, tc:tc + 1], ln1f[:, 0, :], axis=AX.X,
                             apply_absolute_value=True)
                v.drain()
                v.reciprocal(rmx224[:, tc:tc + 1], rmx[:, tc:tc + 1])
                v.drain()
                v.tensor_scalar_mul(rmx224[:, tc:tc + 1], rmx224[:, tc:tc + 1], 224.0)
                s_dv.inc(v.drain())

        @b.scalar
        def _(s):
            for tc in range(NTC):
                s.wait_ge(s_dv.h, dv5 + 3 * tc + 1)
                s.activation(rstd[:, :], rstd[:, :], ACT.Sqrt)
                s_ac.inc(s.drain())
                s.wait_ge(s_dv.h, dv5 + 3 * tc + 2)
                if tc >= 2:
                    s.wait_ge(s_pe.h, pe5 + tc - 1)
                s.activation(ln1f[:, 0, :], res1[:, tc, :], ACT.Identity,
                             bias=nmr[:, 0:1], scale=rstd[:, 0:1])
                s_ac.inc(s.drain())
                if tc >= 2:
                    s.wait_ge(s_ln[tc % 2].h, 16 * ((tc - 2) // 2 + 1))
                s.activation(ln1_bf2[:, tc % 2, :], ln1f[:, 0, :], ACT.Copy)
                s_ac.inc(s.drain())
                s.wait_ge(s_dv.h, dv5 + 3 * tc + 3)
                s.activation(qx_tm[:, tc % 2, :], ln1f[:, 0, :], ACT.Copy,
                             scale=rmx224[:, tc:tc + 1])
                s_ac.inc(s.drain())

        @b.sync
        def _(sync):
            for tc in range(NTC):
                sync.wait_ge(s_ac.h, ac5 + 4 * tc + 3)
                s_ln[tc % 2].inc(sync.dma_start(out=ln1_d[tc * 128:(tc + 1) * 128, :],
                                                in_=ln1_bf2[:, tc % 2, :]), 16)

        @b.tensor
        def _(t):
            t.wait_ge(s_ev.h, ev5)
            for tc in range(NTC):
                t.wait_ge(s_ac.h, ac5 + 4 * tc + 4)
                if tc >= 1:
                    t.wait_ge(s_ev.h, ev5 + tc)
                ins = None
                for fc in range(NFC):
                    dst = bass.AP(ps8.tensor if hasattr(ps8, "tensor") else ps8,
                                  fc * 256, [[4096, 128], [2, 128]])
                    ins = t.transpose(dst, qx_tm[:, tc % 2, fc * 128:(fc + 1) * 128],
                                      ident8[:, :])
                s_pe.inc(ins)

        @b.vector
        def _(v):
            v.wait_ge(s_pe.h, pe5 + NTC)
            s_ev.inc(v.tensor_copy(
                qx_fm[:, :, (NTC - 1) * 128:NTC * 128],
                bass.AP(ps8.tensor if hasattr(ps8, "tensor") else ps8, 0,
                        [[4096, 128], [256, 16], [2, 128]])))

        @b.scalar
        def _(s):
            s.activation(identf[:, :], ident[:, :], ACT.Copy)
            s_ac.inc(s.drain())

        @b.vector
        def _(v):
            v.wait_ge(s_dv.h, dv5 + 3 * NTC)
            v.tensor_scalar_mul(rtmp[:, 0:NTC], rmx[:, 0:NTC], 4.0 / (448.0 * 448.0))
            s_dv.inc(v.drain())

        pe5b = s_pe.n
        dv5b = s_dv.n
        @b.tensor
        def _(t):
            t.wait_ge(s_dv.h, dv5b)
            t.wait_ge(s_ac.h, s_ac.n)
            for rnd in range(2):
                if rnd >= 1:
                    t.wait_ge(s_dv.h, dv5b + rnd)
                ins = None
                for bb in range(4):
                    ins = t.transpose(psY[0:1, bb * 128:(bb + 1) * 128],
                                      rtmp[:, rnd * 4 + bb:rnd * 4 + bb + 1], identf[:, :])
                s_pe.inc(ins)

        @b.vector
        def _(v):
            for rnd in range(2):
                v.wait_ge(s_pe.h, pe5b + rnd + 1)
                s_dv.inc(v.tensor_copy(row5[0:1, rnd * 512:(rnd + 1) * 512], psY[0:1, 0:512]))

        @b.sync
        def _(sync):
            sync.wait_ge(s_dv.h, dv5b + 2)
            s_rb5.inc(sync.dma_start(out=rowd5.rearrange("(o d) -> o d", o=1),
                                     in_=row5[0:1, 0:NT]), 16)
            sync.wait_ge(s_rb5.h, 16)
            s_rb5.inc(sync.dma_start(
                out=bc_rm[:, :], in_=bass.AP(rowd5.tensor, 0, [[0, 128], [1, NT]])), 16)

        if dbg:
            @b.sync
            def _(sync):
                sync.wait_ge(s_ac.h, s_ac.n)
                sync.wait_ge(s_ev.h, s_ev.n)
                for fc in range(NFC):
                    s_out.inc(sync.dma_start(out=io["dbg_qxf"][fc * 128:(fc + 1) * 128, :],
                                             in_=qx_fm[:, fc, :]), 16)
                sync.wait_ge(s_out.h, s_out.n)

    sc5.close()
    sc4.close()

    sc6t = contextlib.ExitStack()
    t_fm = sc6t.enter_context(nc.sbuf_tensor("t_fm", [128, NIC, NT], BF16, side="right"))
    Mabs = sc6t.enter_context(nc.sbuf_tensor("Mabs", [128, NT], BF16, side="right"))
    Mlo = sc6t.enter_context(nc.sbuf_tensor("Mlo", [128, NT], BF16, side="right"))
    sc6 = contextlib.ExitStack()
    gbuf = sc6.enter_context(nc.sbuf_tensor("gbuf", [128, NT], F32, side="right"))
    ubuf = sc6.enter_context(nc.sbuf_tensor("ubuf", [128, NT], F32, side="right"))
    psG = [sc6.enter_context(nc.psum_tensor(f"psG{j}", [128, 512], F32)) for j in range(8)]


    wrec = {"load": {}, "vec": {}, "sc": {}, "tr": {}, "act": {}, "cons": {}}
    wcur = {"load": 0, "loadu": 0, "vec": 0, "sc": 0, "act": 0}
    wsrc = {"g": wg, "u": wu}
    IGORD = {d: i for i, d in GORD.items()}
    PM_DR = mybir.MatmulPerfMode.DoubleRow

    def pfree(i):
        k, c, q = PIECES[i]
        return PF[q] if k == "dr" else 2048

    def wp_load(sync, upto):
        upto = min(upto, NP)
        while wcur["load"] < upto:
            i = wcur["load"]
            if i % 2 == 1:
                wcur["load"] += 1
                continue
            if i >= NR and (i - NR) not in wrec["sc"]:
                return
            if i == 0 and wp_guard["pe"] is not None:
                sync.wait_ge(s_pe.h, wp_guard["pe"])
                sync.wait_ge(s_ev.h, wp_guard["ev"])
                sync.wait_ge(s_dnimg.h, 16 * NFC)
                sync.wait_ge(s_dnl.h, 16 * NFC)
                sync.wait_ge(s_dnq.h, NFC)
                sync.wait_ge(s_dntr[0].h, s_dntr[0].n)
                sync.wait_ge(s_dntr[1].h, s_dntr[1].n)
            wcur["load"] += 1
            j = i % NR
            k, c, q = PIECES[i]
            if i >= NR:
                sync.wait_ge(s_cons[j].h, wrec["sc"][i - NR])
            wrec["load"][i] = s_wr[j].inc(sync.dma_start(
                out=wraw[j][:, :], in_=wsrc[k][c * 128:(c + 1) * 128, :]), 16)

    def wp_load_u(g, upto):
        upto = min(upto, NP, wcur["load"] + 1)
        while wcur["loadu"] < upto:
            i = wcur["loadu"]
            if i % 2 == 0:
                wcur["loadu"] += 1
                continue
            if i >= NR and (i - NR) not in wrec["sc"]:
                return
            if i == 1:
                g.wait_ge(s_dnimg.h, 16 * NFC)
                g.wait_ge(s_dnq.h, NFC)
                g.wait_ge(s_dnv.h, NFC)
            wcur["loadu"] += 1
            j = i % NR
            k, c, q = PIECES[i]
            if i >= NR:
                g.wait_ge(s_cons[j].h, wrec["sc"][i - NR])
            wrec["load"][i] = s_wr[j].inc(g.dma_start(
                out=wraw[j][:, :], in_=wsrc[k][c * 128:(c + 1) * 128, :]), 16)

    def wp_vec(v, upto):
        upto = min(upto, NP)
        while wcur["vec"] < upto:
            i = wcur["vec"]
            if i not in wrec["load"]:
                return
            wcur["vec"] += 1
            j = i % NR
            k, c, q = PIECES[i]
            f = pfree(i)
            v.wait_ge(s_wr[j].h, wrec["load"][i])
            if k == "dr":
                v.reduce_max(wp_rwp[:, q:q + 1], wraw[j][:, 0:f], axis=AX.X,
                             apply_absolute_value=True)
                if q == 2:
                    v.drain()
                    v.tensor_max(wp_rwp[:, 3:4], wp_rwp[:, 0:1], wp_rwp[:, 1:2])
                    v.drain()
                    v.tensor_max(rwd_keep[:, c:c + 1], wp_rwp[:, 3:4], wp_rwp[:, 2:3])
                wrec["vec"][i] = s_ws[j].inc(v.drain())
                continue
            d = GORD[i]
            v.reduce_max(wp_rwq[:, d:d + 1], wraw[j][:, 0:f], axis=AX.X,
                         apply_absolute_value=True)
            v.drain()
            v.reciprocal(wp_rwrec[:, d:d + 1], wp_rwq[:, d:d + 1])
            wrec["vec"][i] = s_ws[j].inc(v.drain())

    def wp_sct(s, upto):
        upto = min(upto, NP)
        while wcur["sc"] < upto:
            i = wcur["sc"]
            if PIECES[i][0] == "dr":
                wrec["tr"][i] = None
                wcur["sc"] += 1
                continue
            if i not in wrec["vec"]:
                return
            d = GORD[i]
            p3 = IGORD.get(d - NRT)
            if p3 is not None and p3 not in wrec["act"]:
                return
            wcur["sc"] += 1
            if i == 0:
                s.wait_ge(s_dnimg.h, 16 * NFC)
            j = i % NR
            js = d % NR
            jt = d % NRT
            p2 = IGORD.get(d - NR)
            if p2 is not None:
                s.wait_ge(s_wt[GORD[p2] % NRT].h, wrec["tr"][p2])
            s.wait_ge(s_ws[j].h, wrec["vec"][i])
            if p3 is not None:
                sem, val = wrec["act"][p3]
                s.wait_ge(sem.h, val)
            s.activation(wsc[js][:, :], wraw[j][:, :], ACT.Copy,
                         scale=wp_rwrec[:, GORD[i]:GORD[i] + 1])
            wrec["sc"][i] = s_cons[j].inc(s.drain())
            wrec["tr"][i] = s_wt[jt].inc(s.dma_start_transpose(
                wtrn[jt][:, :].rearrange("p (c j2) -> p c j2", j2=128),
                wsc[js][:, :].rearrange("p (c j2) -> p c j2", j2=128)), 16)

    def wp_act(v, upto):
        upto = min(upto, NP)
        while wcur["act"] < upto:
            i = wcur["act"]
            if PIECES[i][0] == "dr":
                wcur["act"] += 1
                continue
            if wrec["tr"].get(i) is None:
                return
            d = GORD[i]
            pq = IGORD.get(d - NQ)
            if pq is not None and pq not in wrec["cons"]:
                return
            wcur["act"] += 1
            if i == 0:
                v.wait_ge(s_dnimg.h, 16 * NFC)
            jt = d % NRT
            jq = d % NQ
            v.wait_ge(s_wt[jt].h, wrec["tr"][i])
            if pq is not None:
                sem, val = wrec["cons"][pq]
                v.wait_ge(sem.h, val)
            ins = v.tensor_scalar_mul(wq8[jq][:, :], wtrn[jt][:, :], 224.0)
            wrec["act"][i] = (s_w8[jq], s_w8[jq].inc(ins))

    GIDX = {}
    for i, (k, c, q) in enumerate(PIECES):
        if k in ("g", "u"):
            GIDX[(k, c)] = i

    if True:
        b = blk
        dv6 = s_dv.n
        ac6 = s_ac.n
        pe6 = s_pe.n

        @b.vector
        def _(v):
            v.wait_ge(s_ln[0].h, s_ln[0].n)
            v.wait_ge(s_ln[1].h, s_ln[1].n)
            v.wait_ge(s_rb5.h, 16)
            v.memset(Mabs[:, :], -1e30)
            v.memset(Mlo[:, :], 1e30)
            s_dv.inc(v.drain())
        dv6 += 1

        for jj in range(8):
            @b.sync
            def _(sync, jj=jj):
                wp_load(sync, 2 + 2 * jj)
            @b.gpsimd
            def _(g, jj=jj):
                wp_load_u(g, 2 + 2 * jj)
            @b.vector
            def _(v, jj=jj):
                for _ in range(3):
                    wp_vec(v, 1 + 2 * jj)
                    wp_act(v, 2 * jj)
            @b.scalar
            def _(s, jj=jj):
                wp_sct(s, 2 * jj)

        for m in range(NIC):
            ig = GIDX[("g", m)]
            iu = GIDX[("u", m)]

            @b.tensor
            def _(t, m=m, ig=ig, iu=iu):
                if m == 0:
                    t.wait_ge(s_ev.h, s_ev.n)
                    t.wait_ge(s_rb5.h, 32)
                for pidx, pi in ((0, ig), (1, iu)):
                    jq = GORD[pi] % NQ
                    sem, val = wrec["act"][pi]
                    t.wait_ge(sem.h, val)
                    if m >= 2 and pidx == 0:
                        t.wait_ge(s_dv.h, dv6 + 2 * (m - 2) + 1)
                    ins = None
                    for n in range(2):
                        for k in range(8):
                            ins = t.matmul(psG[4 * (m % 2) + 2 * pidx + n][:, :],
                                           wq8[jq][:, k * 256:(k + 1) * 256].rearrange(
                                               "p (r mm) -> p r mm", r=2),
                                           qx_fm[:, 2 * k:2 * k + 2, n * 512:(n + 1) * 512],
                                           start=(k == 0), stop=(k == 7),
                                           perf_mode=PM_DR)
                    s_pe.inc(ins)
                    wrec["cons"][pi] = (s_pe, s_pe.n)

            for rr in range(3):
                @b.sync
                def _(sync, m=m):
                    tgt = (GIDX[("u", m + 1)] + 1 + 4) if m + 1 < NIC else NP
                    wp_load(sync, tgt)

                @b.gpsimd
                def _(g, m=m):
                    tgt = (GIDX[("u", m + 1)] + 1 + 4) if m + 1 < NIC else NP
                    wp_load_u(g, tgt)

                @b.vector
                def _(v, m=m):
                    tgt = (GIDX[("u", m + 1)] + 1 + 2) if m + 1 < NIC else NP
                    wp_vec(v, tgt)
                    wp_act(v, tgt - 2)

                @b.scalar
                def _(s, m=m):
                    tgt = (GIDX[("u", m + 1)] + 1 + 1) if m + 1 < NIC else NP
                    wp_sct(s, tgt)

            @b.vector
            def _(v, m=m, ig=ig, iu=iu):
                v.wait_ge(s_pe.h, pe6 + 2 * m + 2)
                for pidx, pi, buf in ((0, ig, gbuf), (1, iu, ubuf)):
                    for n in range(2):
                        v.scalar_tensor_tensor(
                            buf[:, n * 512:(n + 1) * 512],
                            psG[4 * (m % 2) + 2 * pidx + n][:, :],
                            wp_rwq[:, GORD[pi]:GORD[pi] + 1], bc_rm[:, n * 512:(n + 1) * 512],
                            ALU.mult, ALU.mult)
                s_dv.inc(v.drain())
                v.wait_ge(s_ac.h, ac6 + m + 1)
                v.tensor_mul(t_fm[:, m, :], gbuf[:, :], ubuf[:, :])
                v.drain()
                v.tensor_max(Mabs[:, :], Mabs[:, :], t_fm[:, m, :])
                v.tensor_tensor(Mlo[:, :], Mlo[:, :], t_fm[:, m, :], ALU.min)
                s_dv.inc(v.drain())

            @b.scalar
            def _(s, m=m, ig=ig, iu=iu):
                s.wait_ge(s_dv.h, dv6 + 2 * m + 1)
                s_ac.inc(s.activation(gbuf[:, :], gbuf[:, :], ACT.Silu))

        for jj in range(6):
            @b.sync
            def _(sync, jj=jj):
                wp_load(sync, NP)
            @b.gpsimd
            def _(g, jj=jj):
                wp_load_u(g, NP)
            @b.vector
            def _(v, jj=jj):
                for _ in range(3):
                    wp_vec(v, NP)
                    wp_act(v, NP)
            @b.scalar
            def _(s, jj=jj):
                wp_sct(s, NP)

        if dbg:
            @b.sync
            def _(sync):
                sync.wait_ge(s_dv.h, s_dv.n)
                for mm in range(NIC):
                    s_out.inc(sync.dma_start(out=io["dbg_tf"][mm * 128:(mm + 1) * 128, :],
                                             in_=t_fm[:, mm, :]), 16)
                sync.wait_ge(s_out.h, s_out.n)

    sc6.close()
    sc5q.close()

    sc7 = contextlib.ExitStack()
    qt_fm = sc7.enter_context(nc.sbuf_tensor("qt_fm", [128, NIC, NT], FP8, side="left"))
    bc_rwd = sc7.enter_context(nc.sbuf_tensor("bc_rwd", [128, D], F32, side="left"))
    identf2 = sc7.enter_context(nc.sbuf_tensor("identf2", [128, 128], F32, side="left"))
    sc7b = contextlib.ExitStack()
    bc_qt = sc7b.enter_context(nc.sbuf_tensor("bc_qt", [128, NT], F32, side="right"))
    row7 = sc7b.enter_context(nc.sbuf_tensor("row7", [1, NT + D], F32, side="right"))
    psT7 = sc7b.enter_context(nc.psum_tensor("psT7", [128, 1024], BF16))
    psU7 = sc7b.enter_context(nc.psum_tensor("psU7", [128, 512], F32))

    if True:
        b = blk
        dv7 = s_dv.n
        pe7 = s_pe.n
        ev7 = s_ev.n

        @b.scalar
        def _(s):
            for j in range(4):
                s.wait_ge(s_wt[j].h, s_wt[j].n)
            s.activation(identf2[:, :], ident[:, :], ACT.Copy)
            s_ac.inc(s.drain())
        ac7 = s_ac.n

        @b.tensor
        def _(t):
            t.wait_ge(s_dv.h, dv7)
            ins = None
            for bb in range(NTC):
                ins = t.transpose(psT7[:, bb * 128:(bb + 1) * 128],
                                  Mabs[:, bb * 128:(bb + 1) * 128], ident[:, :])
            s_pe.inc(ins)
            t.wait_ge(s_dv.h, dv7 + 1)
            ins = None
            for bb in range(NTC):
                ins = t.transpose(psT7[:, bb * 128:(bb + 1) * 128],
                                  Mlo[:, bb * 128:(bb + 1) * 128], ident[:, :])
            s_pe.inc(ins)

        @b.vector
        def _(v):
            v.wait_ge(s_pe.h, pe7 + 1)
            for bb in range(NTC):
                v.reduce_max(rmt[:, bb:bb + 1], psT7[:, bb * 128:(bb + 1) * 128], axis=AX.X)
            s_dv.inc(v.drain())
            v.wait_ge(s_pe.h, pe7 + 2)
            for bb in range(NTC):
                v.tensor_reduce(rmx224[:, bb:bb + 1], psT7[:, bb * 128:(bb + 1) * 128],
                                axis=AX.X, op=ALU.min, negate=True)
            v.drain()
            v.tensor_max(rmt[:, :], rmt[:, :], rmx224[:, :])
            v.drain()
            v.tensor_scalar_mul(rmt4[:, :], rmt[:, :], 4.0 / 448.0)
            v.reciprocal(rmx224[:, :], rmt[:, :])
            v.drain()
            v.tensor_scalar_mul(rmx224[:, :], rmx224[:, :], 224.0)
            v.wait_ge(s_dnv.h, NFC)
            v.tensor_scalar_mul(rwdq[:, :], rwd_keep[:, :], 1.0 / 448.0)
            s_dv.inc(v.drain())

        pe7b = s_pe.n
        dv7b = s_dv.n
        @b.tensor
        def _(t):
            t.wait_ge(s_dv.h, dv7b)
            t.wait_ge(s_ac.h, ac7)
            rr = 0
            for rnd in range(2):
                if rr >= 1:
                    t.wait_ge(s_dv.h, dv7b + rr)
                ins = None
                for bb in range(4):
                    ins = t.transpose(psU7[0:1, bb * 128:(bb + 1) * 128],
                                      rmx224[:, rnd * 4 + bb:rnd * 4 + bb + 1], identf2[:, :])
                s_pe.inc(ins)
                rr += 1
            for rnd in range(4):
                t.wait_ge(s_dv.h, dv7b + rr)
                ins = None
                for bb in range(4):
                    ins = t.transpose(psU7[0:1, bb * 128:(bb + 1) * 128],
                                      rwdq[:, rnd * 4 + bb:rnd * 4 + bb + 1], identf2[:, :])
                s_pe.inc(ins)
                rr += 1

        @b.vector
        def _(v):
            for rr in range(6):
                v.wait_ge(s_pe.h, pe7b + rr + 1)
                s_dv.inc(v.tensor_copy(row7[0:1, rr * 512:(rr + 1) * 512], psU7[0:1, 0:512]))

        @b.sync
        def _(sync):
            sync.wait_ge(s_dv.h, dv7b + 6)
            s_rb7.inc(sync.dma_start(out=rowd7.rearrange("(o d) -> o d", o=1),
                                     in_=row7[0:1, 0:NT + D]), 16)
            sync.wait_ge(s_rb7.h, 16)
            s_rb7.inc(sync.dma_start(
                out=bc_qt[:, :], in_=bass.AP(rowd7.tensor, 0, [[0, 128], [1, NT]])), 16)
            s_rb7.inc(sync.dma_start(
                out=bc_rwd[:, :], in_=bass.AP(rowd7.tensor, NT, [[0, 128], [1, D]])), 16)

        dvq = s_dv.n
        qt_val = {}
        bch = bc_qt.tensor if hasattr(bc_qt, "tensor") else bc_qt
        bc4 = bass.AP(bch, 0, [[NT, 128], [0, 4], [1, NT]])
        for sq in range(NSQ):
            @b.vector
            def _(v, sq=sq):
                if sq == 0:
                    v.wait_ge(s_rb7.h, 48)
                s_dv.inc(v.tensor_tensor(qt_fm[:, 4 * sq:4 * sq + 4, :],
                                         t_fm[:, 4 * sq:4 * sq + 4, :], bc4, ALU.mult))
            qt_val[sq] = s_dv.n

        if dbg:
            @b.sync
            def _(sync):
                sync.wait_ge(s_dv.h, qt_val[NSQ - 1])
                for mm in range(NIC):
                    s_out.inc(sync.dma_start(out=io["dbg_qtf"][mm * 128:(mm + 1) * 128, :],
                                             in_=qt_fm[:, mm, :]), 16)
                sync.wait_ge(s_out.h, s_out.n)

    sc7b.close()
    sc6t.close()

    sc8 = contextlib.ExitStack()
    qwdT = sc8.enter_context(nc.sbuf_tensor("qwdT", [128, NIC, D], FP8, side="right"))
    res2 = sc8.enter_context(nc.sbuf_tensor("res2", [128, 2, D], F32, side="right"))
    lnst = sc8.enter_context(nc.sbuf_tensor("lnst", [128, 2, D], BF16, side="right"))
    psE = [sc8.enter_context(nc.psum_tensor(f"psE{j}", [128, 512], F32)) for j in range(8)]

    if True:
        b = blk
        pe8 = s_pe.n
        dv8 = s_dv.n
        ac8 = s_ac.n
        ev8 = s_ev.n
        qw_marks = {}

        @b.gpsimd
        def _(g):
            g.wait_ge(s_pe.h, pe8)
            g.wait_ge(s_rb7.h, 16)
            for tc in range(NTC):
                if tc >= 2:
                    g.wait_ge(s_dv.h, dv8 + 2 * (tc - 2) + 1)
                s_lnr[tc % 2].inc(g.dma_start(out=lnst[:, tc % 2, :],
                                              in_=ln1_d[tc * 128:(tc + 1) * 128, :]), 16)

        @b.sync
        def _(sync):
            sync.wait_ge(s_dnimg.h, 16 * NFC)
            for sq in range(NSQ):
                jj = sq % 2
                sync.wait_ge(s_dv.h, qt_val[sq])
                if sq >= 2:
                    sync.wait_ge(s_qw[jj].h, qw_marks[sq - 2][1])
                s_qw[jj].inc(sync.dma_start(
                    out=qwdT[:, 4 * sq:4 * sq + 4, :],
                    in_=qwd_img[sq * 512:(sq + 1) * 512, :].rearrange(
                        "(bk p) d -> p bk d", p=128)), 16)
                qw_marks[sq] = (jj, s_qw[jj].n)

        @b.tensor
        def _(t):
            t.wait_ge(s_dv.h, qt_val[0])
            for tg in range(4):
                if tg >= 1:
                    t.wait_ge(s_ev.h, ev8 + 8 * tg)
                ins = None
                for mi in range(0, NIC, 2):
                    if tg == 0 and (mi % 4) == 0:
                        jj, val = qw_marks[min(mi // 4, NSQ - 1)]
                        t.wait_ge(s_qw[jj].h, val)
                    for tci in range(2):
                        tc = 2 * tg + tci
                        for n in range(4):
                            ins = t.matmul(psE[4 * tci + n][:, :],
                                           qt_fm[:, mi:mi + 2, tc * 128:(tc + 1) * 128],
                                           qwdT[:, mi:mi + 2, n * 512:(n + 1) * 512],
                                           start=(mi == 0), stop=(mi == NIC - 2),
                                           perf_mode=PM_DR)
                s_pe.inc(ins)

        out_base = s_out.n

        @b.vector
        def _(v):
            for tg in range(4):
                v.wait_ge(s_pe.h, pe8 + tg + 1)
                for tci in range(2):
                    tc = 2 * tg + tci
                    if tg >= 1:
                        v.wait_ge(s_ost[tci].h, 16 * tg)
                    for n in range(4):
                        v.scalar_tensor_tensor(
                            res2[:, tci, n * 512:(n + 1) * 512], psE[4 * tci + n][:, :],
                            rmt4[:, tc:tc + 1], bc_rwd[:, n * 512:(n + 1) * 512],
                            ALU.mult, ALU.mult)
                    v.drain()
                    v.wait_ge(s_lnr[tc % 2].h, 16 * (tc // 2 + 1))
                    v.tensor_add(res2[:, tci, :], res2[:, tci, :], lnst[:, tc % 2, :])
                    v.drain()
                    for cc in range(4):
                        v.bn_stats(stats[:, 6 * cc:6 * cc + 6],
                                   res2[:, tci, 512 * cc:512 * (cc + 1)])
                    v.drain()
                    v.bn_aggr(aggr[:, :], stats[:, :])
                    v.drain()
                    if tc >= 1:
                        v.wait_ge(s_ac.h, ac8 + 2 * tc)
                    v.tensor_scalar_add(rstd[:, :], aggr[:, 1:2], EPS)
                    v.drain()
                    v.reciprocal(rstd[:, :], rstd[:, :])
                    s_dv.inc(v.drain())
                    v.wait_ge(s_ac.h, ac8 + 2 * tc + 1)
                    v.tensor_scalar_mul(nmr[:, :], aggr[:, 0:1], -1.0)
                    v.drain()
                    v.tensor_mul(nmr[:, :], nmr[:, :], rstd[:, :])
                    s_dv.inc(v.drain())
                for _ in range(8):
                    s_ev.inc(v.engine_nop())

        @b.scalar
        def _(s):
            for tg in range(4):
                for tci in range(2):
                    tc = 2 * tg + tci
                    s.wait_ge(s_dv.h, dv8 + 2 * tc + 1)
                    s.activation(rstd[:, :], rstd[:, :], ACT.Sqrt)
                    s_ac.inc(s.drain())
                    s.wait_ge(s_dv.h, dv8 + 2 * tc + 2)
                    s.activation(res2[:, tci, :], res2[:, tci, :], ACT.Identity,
                                 bias=nmr[:, 0:1], scale=rstd[:, 0:1])
                    s_ac.inc(s.drain())

        @b.sync
        def _(sync):
            for tg in range(4):
                for tci in range(2):
                    tc = 2 * tg + tci
                    sync.wait_ge(s_ac.h, ac8 + 2 * tc + 2)
                    s_ost[tci].inc(sync.dma_start(out=out[tc * 128:(tc + 1) * 128, :],
                                                  in_=res2[:, tci, :]), 16)
            if dbg:
                sync.wait_ge(s_qw[0].h, s_qw[0].n)
                sync.wait_ge(s_qw[1].h, s_qw[1].n)
                for mi in range(NIC):
                    s_out.inc(sync.dma_start(out=io["dbg_qwd"][mi * 128:(mi + 1) * 128, :],
                                             in_=qwdT[:, mi, :]), 16)
            sync.wait_ge(s_ost[0].h, s_ost[0].n)
            sync.wait_ge(s_ost[1].h, s_ost[1].n)
            sync.wait_ge(s_out.h, s_out.n)

    sc8.close()
    sc7.close()


def _host_prep(inputs):
    hs_full = np.ascontiguousarray(np.asarray(inputs["L0_attn_hidden_states"], np.float32)[0])
    rb = np.asarray(inputs["L0_attn_rel_bias"], np.float32)
    pos = np.arange(TW)
    rel = np.clip(pos[None, :] - pos[:, None], -128, 128) + 128
    bmat = np.ascontiguousarray(rb[rel])
    di_ = np.asarray(inputs["L0_ffn_down_reorder_index"])
    wdn = np.ascontiguousarray(np.asarray(inputs["L0_ffn_w_down"], np.float32)[:, np.argsort(di_)])
    import ml_dtypes
    com = dict(
        wc=np.ascontiguousarray(np.asarray(inputs["L0_attn_Wc"], np.float32)),
        wq=np.ascontiguousarray(np.asarray(inputs["L0_attn_Wq"], np.float32)),
        wk=np.ascontiguousarray(np.asarray(inputs["L0_attn_Wk"], np.float32)),
        wv=np.ascontiguousarray(np.asarray(inputs["L0_attn_Wv"], np.float32)),
        wd=np.ascontiguousarray(np.asarray(inputs["L0_attn_Wd"], np.float32)),
        bc=np.ascontiguousarray(np.asarray(inputs["L0_attn_bc"], np.float32)),
        bq=np.ascontiguousarray(np.asarray(inputs["L0_attn_bq"], np.float32)),
        bk=np.ascontiguousarray(np.asarray(inputs["L0_attn_bk"], np.float32)),
        bv=np.ascontiguousarray(np.asarray(inputs["L0_attn_bv"], np.float32)),
        bd=np.ascontiguousarray(np.asarray(inputs["L0_attn_bd"], np.float32)),
        bmat=bmat,
        idn=np.eye(128, dtype=ml_dtypes.bfloat16),
        idn8=np.eye(128, dtype=ml_dtypes.float8_e4m3),
        wg=np.ascontiguousarray(np.asarray(inputs["L0_ffn_w_gate"], np.float32)),
        wu=np.ascontiguousarray(np.asarray(inputs["L0_ffn_w_up"], np.float32)),
        wdn=wdn,
    )
    in_maps = []
    for c in range(8):
        m = dict(com)
        m["hs"] = np.ascontiguousarray(hs_full[c * NT:(c + 1) * NT])
        in_maps.append(m)
    return in_maps


def kernel(**inputs):
    nc = build_nc(dbg=False)
    in_maps = _host_prep(inputs)
    res = run_bass_kernel_spmd(nc, in_maps, core_ids=list(range(8)))
    outp = np.concatenate([np.asarray(r["out"]) for r in res.results], axis=0)
    return outp[None].astype(np.float32)
